# version 03
# baseline (speedup 1.0000x reference)
import numpy as np
import ml_dtypes
from contextlib import ExitStack

import concourse.bass as bass
import concourse.mybir as mybir
from concourse.bass_utils import run_bass_kernel_spmd

F32 = mybir.dt.float32
F32R = mybir.dt.float32r
BF16 = mybir.dt.bfloat16
I32 = mybir.dt.int32
AF = mybir.ActivationFunctionType
ALU = mybir.AluOpType
AXX = mybir.AxisListType.X

NCORE = 8
T, C = 8192, 2048
NH, DH, DHR, LL = 16, 128, 64, 512
TOK = T // NCORE
TB = 512
NTB = TOK // TB
SI, MI, NE = 5632, 1408, 8
EPS = 1e-5
SCALE = float((DH + DHR) ** -0.5)
CMARGIN = -1.02
NQB = T // TB
HPC = 2
BF = ml_dtypes.bfloat16


def dma_tiles(E, dst2d, src3d, a):
    return E.dma_start(out=dst2d.rearrange("p (a f) -> p a f", a=a),
                       in_=src3d.rearrange("a p f -> p a f"))


def dma_cols(E, dst2d, src2d, a=16):
    return E.dma_start(out=dst2d.rearrange("p (a t) -> p a t", a=a),
                       in_=src2d.rearrange("(a p) t -> p a t", p=128))


class Sem:
    def __init__(self, handle):
        self.h = handle
        self.n = 0


class Rec:
    ENG = ("sync", "tensor", "scalar", "vector", "gpsimd")

    def __init__(self, block):
        self.block = block
        self.q = {k: [] for k in self.ENG}

    def wait(self, eng, sem, n=None):
        tgt = sem.n if n is None else n
        if tgt > 0:
            self.q[eng].append(lambda E, s=sem.h, t=tgt: E.wait_ge(s, t))

    def do(self, eng, build, sem=None, k=1):
        if sem is None:
            self.q[eng].append(build)
            return None
        sem.n += k
        tgt = sem.n

        def f(E, b=build, s=sem.h, kk=k):
            b(E).then_inc(s, kk)
        self.q[eng].append(f)
        return tgt

    def dos(self, eng, build, sem, k=1):
        t = self.do(eng, build, sem, k)
        self.wait(eng, sem, t)
        return t

    def play(self):
        for name in self.ENG:
            ops = self.q[name]
            if ops:
                getattr(self.block, name)(
                    lambda E, ops=ops: [op(E) for op in ops])
        self.q = {k: [] for k in self.ENG}


class RingSem:

    def __init__(self, mksem, k):
        self.sems = [mksem() for _ in range(k)]
        self.k = k
        self.idx = 0

    def next(self):
        s = self.sems[self.idx % self.k]
        self.idx += 1
        return s


def emit_ln(rec, S, B, src_tile, sq_tile, apply_emit, n_tiles=16, denom=C):
    row_m, row_v, row_sd, row_r, row_mr = B["rows"]
    sq_hist = []
    st_t = []
    rec.wait("tensor", S["dv"])
    for a in range(n_tiles):
        if a >= 4:
            rec.wait("scalar", S["st"], st_t[a - 4])
        rec.do("scalar", lambda E, d=sq_tile(a), s=src_tile(a): E.activation(
            out=d, in_=s, func=AF.Square), S["sq"])
        sq_hist.append(S["sq"].n)
        rec.wait("tensor", S["sq"], sq_hist[a])
        rec.do("tensor", lambda E, s=src_tile(a), a=a, n=n_tiles: E.matmul(
            B["pSX"], B["ones_x"], s, start=(a == 0), stop=(a == n - 1)))
        rec.do("tensor", lambda E, d=sq_tile(a), a=a, n=n_tiles: E.matmul(
            B["pSQ"], B["ones_sq"], d, start=(a == 0), stop=(a == n - 1)),
            S["st"])
        st_t.append(S["st"].n)
    rec.wait("vector", S["st"])
    rec.dos("vector", lambda E: E.tensor_scalar(
        row_m[:], B["pSX"], 1.0 / denom, None, ALU.mult), S["ch"])
    rec.dos("vector", lambda E: E.tensor_scalar(
        row_v[:], B["pSQ"], 1.0 / denom, None, ALU.mult), S["ch"])
    rec.dos("vector", lambda E: E.tensor_scalar(
        row_v[:], row_v[:], EPS, None, ALU.add), S["ch"])
    rec.dos("vector", lambda E: E.tensor_tensor(
        row_mr[:], row_m[:], row_m[:], ALU.mult), S["ch"])
    rec.do("vector", lambda E: E.tensor_tensor(
        row_v[:], row_v[:], row_mr[:], ALU.subtract), S["dv"])
    rec.wait("scalar", S["dv"])
    rec.do("scalar", lambda E: E.sqrt(out=row_sd[:], in_=row_v[:]), S["sd"])
    rec.wait("vector", S["sd"])
    rec.dos("vector", lambda E: E.reciprocal(row_r[:], row_sd[:]), S["ch"])
    rec.do("vector", lambda E: E.tensor_tensor(
        row_mr[:], row_m[:], row_r[:], ALU.mult), S["dv"])
    rec.wait("tensor", S["dv"])
    rec.do("tensor", lambda E: E.matmul(
        B["pRb"], B["ones_row"], row_r[:], start=True, stop=True))
    rec.do("tensor", lambda E: E.matmul(
        B["pMrb"], B["ones_row"], row_mr[:], start=True, stop=True), S["bc"])
    rec.wait("vector", S["bc"])
    for a in range(n_tiles):
        apply_emit(rec, a, B["pRb"], B["pMrb"])


def build(dbg=()):
    dbg = set(dbg)
    nc = bass.Bass("TRN2", target_bir_lowering=False)
    d = {}

    def din(name, shape, dt):
        d[name] = nc.dram_tensor(name, list(shape), dt, kind="ExternalInput")

    din("xT", (C, TOK), F32R)
    din("cid", (1, 1), I32)
    din("waT", (8, 16, 128, 128), F32R)
    din("bA_r", (128, 8), F32)
    din("wbT", (8, 128, DHR), F32)
    din("bB_c", (DHR, 1), F32)
    din("cosk", (DHR, TOK), BF16)
    din("sink", (DHR, TOK), BF16)
    din("wuq", (4, 128, 256), F32)
    din("wuk", (4, 128, 256), F32)
    din("wuv", (4, 128, 256), F32)
    din("wcc", (8, 128, 128), F32)
    din("bC_c", (128, 1), F32)
    din("cosq", (16, 128, TB), BF16)
    din("sinq", (16, 128, TB), BF16)
    din("RT", (128, 128), BF16)
    din("eye", (128, 128), BF16)
    din("maskd", (128, 4 * TB), BF16)
    din("ones_bf", (128, 128), BF16)
    din("ones_r", (128, 128), F32R)
    din("wzT", (16, 16, 128, 128), F32R)
    din("bZ_r", (128, 16), F32)
    din("gwT", (128, 16 * NE), F32R)
    din("sgwT", (128, 16), F32R)
    din("sguT", (88, 16, 128, 128), F32R)
    din("sdwT", (4, 44, 4, 128, 128), F32R)
    din("w13T", (NE, 22, 16, 128, 128), F32R)
    din("w2T", (NE, 4, 11, 4, 128, 128), F32R)

    yT = nc.dram_tensor("yT", [C, TOK], F32, kind="ExternalOutput")

    hG_in = nc.dram_tensor("hG_in", [1088, TOK], BF16)
    hG_out = nc.dram_tensor("hG_out", [NCORE * 1088, TOK], BF16)
    oG_in = nc.dram_tensor("oG_in", [HPC * DH, T], BF16)
    oG_out = nc.dram_tensor("oG_out", [NH * DH, T], BF16)
    shs = nc.dram_tensor("shs", [44, 128, TB], BF16)
    sguB = nc.dram_tensor("sguB", [88, 16, 128, 128], BF16)
    sdwB = nc.dram_tensor("sdwB", [4, 44, 4, 128, 128], BF16)
    w13B = nc.dram_tensor("w13B", [NE, 22, 16, 128, 128], BF16)
    w2B = nc.dram_tensor("w2B", [NE, 4, 11, 4, 128, 128], BF16)
    RG = [list(range(NCORE))]

    dbg_t = {}

    def dbg_out(name, shape, dt):
        if name in dbg:
            dbg_t[name] = nc.dram_tensor("dbg_" + name, list(shape), dt,
                                         kind="ExternalOutput")
            return dbg_t[name]
        return None

    with ExitStack() as TOP:
        e = TOP.enter_context
        e(nc.allow_low_precision(
            reason="deliberate f32r/bf16 intermediates; fp32 PSUM accumulation"))
        _semctr = [0]
        _sem_free = []

        def mksem(nm=None):
            if nm is None and _sem_free:
                return _sem_free.pop()
            _semctr[0] += 1
            return Sem(e(nc.semaphore(nm or f"s{_semctr[0]}")))

        def release(*items):
            for it in items:
                if isinstance(it, RingSem):
                    _sem_free.extend(it.sems)
                elif isinstance(it, dict):
                    _sem_free.extend(it.values())
                elif isinstance(it, Sem):
                    _sem_free.append(it)

        bar = mksem("bar")
        bar_k = [0]

        def barrier(block):
            bar_k[0] += 1
            tgt = 5 * bar_k[0]
            for reg in (block.sync, block.tensor, block.scalar, block.vector,
                        block.gpsimd):
                def f(eng, tgt=tgt):
                    eng.sem_inc(bar.h, 1)
                    eng.wait_ge(bar.h, tgt)
                reg(f)
            bar.n = tgt

        ones_bf = e(nc.sbuf_tensor("onesbf", [128, 128], BF16))
        ones_r = e(nc.sbuf_tensor("onesr", [128, 128], F32R))
        rt_sb = e(nc.sbuf_tensor("rtsb", [128, 128], BF16))
        eye_sb = e(nc.sbuf_tensor("eyesb", [128, 128], BF16))
        bA_sb = e(nc.sbuf_tensor("bAsb", [128, 8], F32))
        bB_sb = e(nc.sbuf_tensor("bBsb", [DHR, 1], F32))
        bC_sb = e(nc.sbuf_tensor("bCsb", [128, 1], F32))
        bZ_sb = e(nc.sbuf_tensor("bZsb", [128, 16], F32))
        gw_sb = e(nc.sbuf_tensor("gwsb", [128, 16 * NE], BF16))
        sgw_sb = e(nc.sbuf_tensor("sgwsb", [128, 16], BF16))
        cid_sb = e(nc.sbuf_tensor("cidsb", [1, 1], I32))

        row_m = e(nc.sbuf_tensor("rowm", [1, TB], F32))
        row_v = e(nc.sbuf_tensor("rowv", [1, TB], F32))
        row_sd = e(nc.sbuf_tensor("rowsd", [1, TB], F32))
        row_r = e(nc.sbuf_tensor("rowr", [1, TB], F32R))
        row_mr = e(nc.sbuf_tensor("rowmr", [1, TB], F32R))
        ROWS = (row_m, row_v, row_sd, row_r, row_mr)

        block = e(nc.Block())

        s_ld = mksem("ld")
        s_cv = mksem("cv")
        with ExitStack() as st0:
            e0 = st0.enter_context
            gw_f = e0(nc.sbuf_tensor("gwf", [128, 16 * NE], F32R))
            sgw_f = e0(nc.sbuf_tensor("sgwf", [128, 16], F32R))
            rec = Rec(block)
            for nm, t in [("ones_bf", ones_bf), ("ones_r", ones_r),
                          ("RT", rt_sb), ("eye", eye_sb),
                          ("bA_r", bA_sb), ("bB_c", bB_sb), ("bC_c", bC_sb),
                          ("bZ_r", bZ_sb), ("cid", cid_sb)]:
                rec.do("sync", lambda E, t=t, nm=nm: E.dma_start(
                    out=t[:], in_=d[nm].ap()), s_ld, 16)
            for nm, t in [("gwT", gw_f), ("sgwT", sgw_f)]:
                rec.do("sync", lambda E, t=t, nm=nm: E.dma_start(
                    out=t[:], in_=d[nm].ap()), s_ld, 16)
            rec.wait("vector", s_ld)
            rec.do("vector", lambda E: E.tensor_copy(gw_sb[:], gw_f[:]))
            rec.do("vector", lambda E: E.tensor_copy(sgw_sb[:], sgw_f[:]), s_cv)
            rec.play()
            barrier(block)
        LDC = s_ld.n

        s_cc1 = mksem("cc1")
        with ExitStack() as st1:
            e1 = st1.enter_context
            xsb = e1(nc.sbuf_tensor("p1x", [128, 16 * TB], F32R))
            sqb = e1(nc.sbuf_tensor("p1sqb", [128, 4 * TB], BF16))
            tmpb = e1(nc.sbuf_tensor("p1tmp", [128, 2 * TB], F32R))
            xn1 = e1(nc.sbuf_tensor("p1xn1", [128, 16 * TB], F32R))
            hsb = e1(nc.sbuf_tensor("p1h", [128, 8 * TOK], BF16))
            krraw = e1(nc.sbuf_tensor("p1kraw", [DHR, TB], BF16))
            krt1 = e1(nc.sbuf_tensor("p1krt1", [DHR, TB], BF16))
            krt2 = e1(nc.sbuf_tensor("p1krt2", [DHR, TB], BF16))
            hkr = e1(nc.sbuf_tensor("p1hkr", [DHR, TOK], BF16))
            wr1 = e1(nc.sbuf_tensor("p1wr", [128, 48 * 128], F32R))
            cosk_sb = e1(nc.sbuf_tensor("cosksb", [DHR, TOK], BF16))
            sink_sb = e1(nc.sbuf_tensor("sinksb", [DHR, TOK], BF16))
            wb_f = e1(nc.sbuf_tensor("wbf", [128, 8 * DHR], F32))
            wb_bf = e1(nc.sbuf_tensor("wbbf", [128, 8 * DHR], BF16))
            pH = [e1(nc.psum_tensor(f"p1ph{i}", [128, TB], F32)) for i in range(2)]
            pSX = e1(nc.psum_tensor("p1psx", [1, TB], F32))
            pSQ = e1(nc.psum_tensor("p1psq", [1, TB], F32))
            pRbA = e1(nc.psum_tensor("p1prb", [128, TB], F32))
            pMrbA = e1(nc.psum_tensor("p1pmrb", [128, TB], F32))
            pKR = e1(nc.psum_tensor("p1pkr", [DHR, TB], F32))
            pRot = e1(nc.psum_tensor("p1prot", [DHR, TB], F32))

            sS = {k: mksem() for k in ("sq", "st", "dv", "sd", "bc", "ch")}
            s_x = mksem()
            rs_wld = RingSem(mksem, 3)
            s_xn = mksem()
            s_hmm = mksem()
            s_hev = mksem()
            s_kr = mksem()
            s_hgw = mksem()

            rec = Rec(block)
            s_ld1 = mksem()
            for eng in ("tensor", "scalar", "vector"):
                rec.wait(eng, s_ld, LDC)
            for nm, t in [("cosk", cosk_sb), ("sink", sink_sb)]:
                rec.do("sync", lambda E, t=t, nm=nm: E.dma_start(
                    out=t[:], in_=d[nm].ap()), s_ld1, 16)
            rec.do("sync", lambda E: dma_tiles(E, wb_f[:], d["wbT"].ap(), 8),
                   s_ld1, 16)
            rec.wait("vector", s_ld1)
            rec.do("vector", lambda E: E.tensor_copy(wb_bf[:], wb_f[:]), s_cv)
            rec.wait("tensor", s_cv)
            ev_hist = []
            hmm_hist = []
            ch_all = {}

            def load_wa(cidx, war_tgt=None):
                slot0 = (cidx % 3) * 16
                j = cidx % 8
                if war_tgt is not None:
                    rec.wait("sync", s_hmm, war_tgt)
                sem = rs_wld.next()
                tgt = rec.do("sync", lambda E, j=j, s0=slot0: dma_tiles(
                    E, wr1[:, s0 * 128:(s0 + 16) * 128],
                    d["waT"].ap()[j], 16), sem, 16)
                ch_all[cidx] = (slot0, sem, tgt)

            for cidx in range(3):
                load_wa(cidx)
            for tb in range(NTB):
                if tb > 0:
                    rec.wait("sync", s_xn)
                    rec.wait("sync", sS["st"])
                rec.do("sync", lambda E, tb=tb: dma_cols(
                    E, xsb[:], d["xT"].ap()[:, tb * TB:(tb + 1) * TB]),
                    s_x, 16)
                rec.wait("scalar", s_x)
                rec.wait("tensor", s_x)
                if tb > 0:
                    rec.wait("tensor", s_xn)
                    rec.wait("vector", s_hmm)

                def apply1(rec, a, pRb, pMrb):
                    rec.dos("vector", lambda E, a=a: E.tensor_tensor(
                        tmpb[:, (a % 2) * TB:(a % 2 + 1) * TB],
                        xsb[:, a * TB:(a + 1) * TB], pRb, ALU.mult), sS["ch"])
                    rec.do("vector", lambda E, a=a: E.tensor_tensor(
                        xn1[:, a * TB:(a + 1) * TB],
                        tmpb[:, (a % 2) * TB:(a % 2 + 1) * TB], pMrb,
                        ALU.subtract), s_xn)

                emit_ln(rec, sS,
                        dict(ones_x=ones_r[:, 0:1], ones_sq=ones_bf[:, 0:1],
                             ones_row=ones_r[0:1, :],
                             pSX=pSX[0:1, :], pSQ=pSQ[0:1, :],
                             pRb=pRbA[:], pMrb=pMrbA[:], rows=ROWS),
                        lambda a: xsb[:, a * TB:(a + 1) * TB],
                        lambda a: sqb[:, (a % 4) * TB:(a % 4 + 1) * TB],
                        apply1)
                rec.wait("tensor", s_xn)
                for j in range(8):
                    cidx = tb * 8 + j
                    slot0, wsem, tgt = ch_all[cidx]
                    rec.wait("tensor", wsem, tgt)
                    if cidx >= 2:
                        rec.wait("tensor", s_hev, ev_hist[cidx - 2])
                    for ki in range(16):
                        rec.do("tensor", lambda E, s0=slot0, ki=ki, j=j:
                               E.matmul(
                            pH[j % 2][:],
                            wr1[:, (s0 + ki) * 128:(s0 + ki + 1) * 128],
                            xn1[:, ki * TB:(ki + 1) * TB],
                            start=(ki == 0), stop=(ki == 15)),
                            s_hmm if ki == 15 else None)
                    hmm_hist.append(s_hmm.n)
                    if cidx + 3 < 16:
                        load_wa(cidx + 3, war_tgt=s_hmm.n)
                    rec.wait("scalar", s_hmm)
                    rec.do("scalar", lambda E, j=j, tb=tb: E.activation(
                        out=hsb[:, j * TOK + tb * TB:j * TOK + (tb + 1) * TB],
                        in_=pH[j % 2][:], func=AF.Identity,
                        bias=bA_sb[:, j:j + 1]), s_hev)
                    ev_hist.append(s_hev.n)
                rec.wait("tensor", s_hev)
                if tb > 0:
                    rec.wait("tensor", s_kr)
                for ki in range(8):
                    rec.do("tensor", lambda E, ki=ki, tb=tb: E.matmul(
                        pKR[:], wb_bf[:, ki * DHR:(ki + 1) * DHR],
                        hsb[:, ki * TOK + tb * TB:ki * TOK + (tb + 1) * TB],
                        start=(ki == 0), stop=(ki == 7)),
                        s_kr if ki == 7 else None)
                rec.wait("scalar", s_kr)
                rec.do("scalar", lambda E: E.activation(
                    out=krraw[:], in_=pKR[:], func=AF.Identity, bias=bB_sb[:]),
                    s_kr)
                rec.wait("tensor", s_kr)
                rec.do("tensor", lambda E: E.matmul(
                    pRot[:], rt_sb[0:DHR, 0:DHR], krraw[:],
                    start=True, stop=True), s_kr)
                rec.wait("vector", s_kr)
                rec.dos("vector", lambda E, tb=tb: E.tensor_tensor(
                    krt1[:], krraw[:], cosk_sb[:, tb * TB:(tb + 1) * TB],
                    ALU.mult), sS["ch"])
                rec.dos("vector", lambda E, tb=tb: E.tensor_tensor(
                    krt2[:], pRot[:], sink_sb[:, tb * TB:(tb + 1) * TB],
                    ALU.mult), sS["ch"])
                rec.do("vector", lambda E, tb=tb: E.tensor_tensor(
                    hkr[:, tb * TB:(tb + 1) * TB], krt1[:], krt2[:], ALU.add),
                    s_kr)
            for j in range(8):
                rec.wait("gpsimd", s_hev, ev_hist[8 + j])
                rec.do("gpsimd", lambda E, j=j: E.dma_start(
                    out=hG_in.ap()[j * 128:(j + 1) * 128, :],
                    in_=hsb[:, j * TOK:(j + 1) * TOK]), s_hgw, 16)
            rec.wait("gpsimd", s_kr)
            rec.do("gpsimd", lambda E: E.dma_start(
                out=hG_in.ap()[1024:1088, :], in_=hkr[:]), s_hgw, 16)
            rec.wait("gpsimd", s_hgw)
            rec.do("gpsimd", lambda E: E.collective_compute(
                "AllGather", ALU.bypass, replica_groups=RG,
                ins=[hG_in.ap().opt()], outs=[hG_out.ap().opt()]), s_cc1, 1)
            t = dbg_out("hg", (NCORE * 1088, TOK), BF16)
            if t is not None:
                rec.wait("gpsimd", s_cc1, 1)
                rec.do("gpsimd", lambda E: E.dma_start(
                    out=t.ap(), in_=hG_out.ap()), mksem(), 16)
            rec.play()
            barrier(block)
            release(sS, s_x, rs_wld, s_xn, s_hmm, s_hev, s_kr, s_hgw, s_ld1)

        s_cc2 = mksem("cc2")
        with ExitStack() as stA:
            eA = stA.enter_context
            kf0 = eA(nc.sbuf_tensor("kf0", [128, HPC * T], BF16))
            kf1 = eA(nc.sbuf_tensor("kf1", [65, T], BF16))
            qf0 = eA(nc.sbuf_tensor("qf0", [128, HPC * T], BF16))
            qf1 = [eA(nc.sbuf_tensor(f"qf1_{h}", [65, T], BF16))
                   for h in range(HPC)]
            vf = eA(nc.sbuf_tensor("vf", [128, (T // 128) * 256], BF16))
            kmax_sb = eA(nc.sbuf_tensor("kmax", [1, 4], F32))
            qn_row = eA(nc.sbuf_tensor("qnrow", [1, TB], F32))

            wuq_bf = eA(nc.sbuf_tensor("wuqbf", [128, 4 * 256], BF16))
            wuk_bf = eA(nc.sbuf_tensor("wukbf", [128, 4 * 256], BF16))
            wuv_bf = eA(nc.sbuf_tensor("wuvbf", [128, 4 * 256], BF16))
            wcc_bf = eA(nc.sbuf_tensor("wccbf", [128, 8 * 128], BF16))
            with ExitStack() as st2:
                e2 = st2.enter_context
                stage = e2(nc.sbuf_tensor("p2stage", [128, 1024], F32))
                hgr = e2(nc.sbuf_tensor("hgr", [128, 16 * TB], BF16))
                trr = e2(nc.sbuf_tensor("trr", [128, 4 * TB], BF16))
                qrw = e2(nc.sbuf_tensor("qrw", [128, 2 * TB], BF16))
                rr1 = e2(nc.sbuf_tensor("rr1", [128, 2 * TB], BF16))
                rr2 = e2(nc.sbuf_tensor("rr2", [128, 2 * TB], BF16))
                nsq = e2(nc.sbuf_tensor("nsq", [128, 2 * TB], BF16))
                nsq2 = e2(nc.sbuf_tensor("nsq2", [64, 2 * TB], BF16))
                pP = [e2(nc.psum_tensor(f"p2p{i}", [128, TB], F32))
                      for i in range(4)]
                pR2 = e2(nc.psum_tensor("p2r", [128, TB], F32))
                pRot2 = e2(nc.psum_tensor("p2rot", [128, TB], F32))
                pV = [e2(nc.psum_tensor(f"p2v{i}", [128, 256], F32))
                      for i in range(2)]

                rs_hr = RingSem(mksem, 2)
                s_hfr = mksem()
                rs_tr = RingSem(mksem, 2)
                s_k1 = mksem()
                s_mm = mksem()
                s_ev = mksem()
                s_rp = mksem()
                s_nm = mksem()
                s_ns = mksem()
                s_nx = mksem()

                rec = Rec(block)
                s_st2 = mksem()
                s_ld2 = mksem()
                for nm, t, aa in [("wuq", wuq_bf, 4), ("wuk", wuk_bf, 4),
                                  ("wuv", wuv_bf, 4), ("wcc", wcc_bf, 8)]:
                    rec.wait("sync", s_st2)
                    rec.do("sync", lambda E, nm=nm, aa=aa: dma_tiles(
                        E, stage[:], d[nm].ap(), aa), s_ld2, 16)
                    rec.wait("vector", s_ld2)
                    rec.do("vector", lambda E, t=t: E.tensor_copy(
                        t[:], stage[:]), s_st2)
                rec.wait("tensor", s_st2)
                rec.wait("sync", s_cc1, 1)
                rec.wait("gpsimd", s_cc1, 1)
                for r in range(8):
                    rec.do("gpsimd", lambda E, r=r: E.dma_start(
                        out=kf1[0:64, r * TOK:(r + 1) * TOK],
                        in_=hG_out.ap()[1088 * r + 1024:1088 * r + 1088, :]),
                        s_k1, 16)
                rec.do("vector", lambda E: E.memset(kf1[64:65, :], 1.0),
                       s_k1, 1)

                evq = [0, 0]
                evk = [0, 0]
                evr = [0]
                evv = [0, 0]
                rpd = [0, 0]
                hfr_hist = [0]
                for tbi in range(16):
                    r, co = tbi // 2, (tbi % 2) * TB
                    hs = (tbi % 2) * 8
                    par = tbi % 2
                    if tbi >= 2:
                        rec.wait("sync", s_hfr, hfr_hist[tbi - 1])
                    hrsem = rs_hr.next()
                    for l in range(8):
                        rec.do("sync", lambda E, r=r, co=co, l=l, hs=hs:
                               E.dma_start(
                            out=hgr[:, (hs + l) * TB:(hs + l + 1) * TB],
                            in_=hG_out.ap()[1088 * r + 128 * l:
                                            1088 * r + 128 * (l + 1),
                                            co:co + TB]), hrsem, 16)
                    trsem = rs_tr.next()
                    rec.do("sync", lambda E, tbi=tbi, par=par: E.dma_start(
                        out=trr[:, par * 2 * TB:(par * 2 + 1) * TB],
                        in_=d["cosq"].ap()[tbi]), trsem, 16)
                    rec.do("sync", lambda E, tbi=tbi, par=par: E.dma_start(
                        out=trr[:, (par * 2 + 1) * TB:(par * 2 + 2) * TB],
                        in_=d["sinq"].ap()[tbi]), trsem, 16)
                    rec.wait("tensor", hrsem)
                    for h in range(HPC):
                        rec.wait("tensor", s_ev, evq[h])
                        for li in range(4):
                            rec.do("tensor", lambda E, h=h, li=li, hs=hs:
                                   E.matmul(
                                pP[h][:],
                                wuq_bf[:, li * 256 + h * 128:
                                       li * 256 + (h + 1) * 128],
                                hgr[:, (hs + 4 + li) * TB:(hs + 5 + li) * TB],
                                start=(li == 0), stop=(li == 3)),
                                s_mm if li == 3 else None)
                    for h in range(HPC):
                        rec.wait("tensor", s_ev, evk[h])
                        for li in range(4):
                            rec.do("tensor", lambda E, h=h, li=li, hs=hs:
                                   E.matmul(
                                pP[2 + h][:],
                                wuk_bf[:, li * 256 + h * 128:
                                       li * 256 + (h + 1) * 128],
                                hgr[:, (hs + li) * TB:(hs + li + 1) * TB],
                                start=(li == 0), stop=(li == 3)),
                                s_mm if li == 3 else None)
                    mm_q = s_mm.n
                    rec.wait("tensor", s_ev, evr[0])
                    order = [4, 5, 6, 7, 0, 1, 2, 3]
                    for idx, ki in enumerate(order):
                        free = ki >= 4
                        last = idx == 7
                        rec.do("tensor", lambda E, ki=ki, hs=hs, idx=idx:
                               E.matmul(
                            pR2[:], wcc_bf[:, ki * 128:(ki + 1) * 128],
                            hgr[:, (hs + ki) * TB:(hs + ki + 1) * TB],
                            start=(idx == 0), stop=(idx == 7)),
                            s_hfr if free else (s_mm if last else None))
                    mm_qr = s_mm.n
                    rec.wait("scalar", s_mm, mm_q)
                    for h in range(HPC):
                        rec.do("scalar", lambda E, h=h, tbi=tbi: E.activation(
                            out=qf0[:, h * T + tbi * TB:h * T + (tbi + 1) * TB],
                            in_=pP[h][:], func=AF.Copy, scale=SCALE), s_ev)
                        evq[h] = s_ev.n
                        rec.do("scalar", lambda E, h=h, tbi=tbi: E.activation(
                            out=kf0[:, h * T + tbi * TB:h * T + (tbi + 1) * TB],
                            in_=pP[2 + h][:], func=AF.Copy), s_ev)
                        evk[h] = s_ev.n
                    rec.wait("scalar", s_mm, mm_qr)
                    rec.wait("scalar", s_rp, rpd[par])
                    rec.do("scalar", lambda E, par=par: E.activation(
                        out=qrw[:, par * TB:(par + 1) * TB],
                        in_=pR2[:], func=AF.Identity, bias=bC_sb[:]), s_ev)
                    evr[0] = s_ev.n
                    ev_qr = s_ev.n
                    rec.wait("tensor", s_ev, ev_qr)
                    rec.wait("tensor", s_rp, rpd[par])
                    rec.do("tensor", lambda E, par=par: E.matmul(
                        pRot2[:], rt_sb[:], qrw[:, par * TB:(par + 1) * TB],
                        start=True, stop=True), s_mm)
                    mm_rot = s_mm.n
                    for tt in range(4):
                        rec.wait("tensor", s_ev, evv[tt % 2])
                        for li in range(4):
                            sem = None
                            if tt == 3:
                                sem = s_hfr
                            elif li == 3:
                                sem = s_mm
                            rec.do("tensor", lambda E, tt=tt, li=li, hs=hs:
                                   E.matmul(
                                pV[tt % 2][:],
                                hgr[:, (hs + li) * TB + tt * 128:
                                       (hs + li) * TB + (tt + 1) * 128],
                                wuv_bf[:, li * 256:(li + 1) * 256],
                                start=(li == 0), stop=(li == 3)), sem)
                        vt = s_mm.n if tt < 3 else s_hfr.n
                        vs = s_mm if tt < 3 else s_hfr
                        rec.wait("scalar", vs, vt)
                        rec.do("scalar", lambda E, tbi=tbi, tt=tt: E.activation(
                            out=vf[:, (tbi * 4 + tt) * 256:
                                   (tbi * 4 + tt + 1) * 256],
                            in_=pV[tt % 2][:], func=AF.Copy), s_ev)
                        evv[tt % 2] = s_ev.n
                    hfr_hist.append(s_hfr.n)
                    rec.wait("vector", trsem)
                    rec.wait("vector", s_mm, mm_rot)
                    rec.dos("vector", lambda E, par=par: E.tensor_tensor(
                        rr1[:, par * TB:(par + 1) * TB],
                        qrw[:, par * TB:(par + 1) * TB],
                        trr[:, par * 2 * TB:(par * 2 + 1) * TB], ALU.mult),
                        s_rp)
                    rec.dos("vector", lambda E, par=par: E.tensor_tensor(
                        rr2[:, par * TB:(par + 1) * TB], pRot2[:],
                        trr[:, (par * 2 + 1) * TB:(par * 2 + 2) * TB],
                        ALU.mult), s_rp)
                    rec.do("vector", lambda E, par=par, tbi=tbi:
                           E.tensor_tensor(
                        qf1[0][0:64, tbi * TB:(tbi + 1) * TB],
                        rr1[0:64, par * TB:(par + 1) * TB],
                        rr2[0:64, par * TB:(par + 1) * TB], ALU.add))
                    rec.do("vector", lambda E, par=par, tbi=tbi:
                           E.tensor_tensor(
                        qf1[1][0:64, tbi * TB:(tbi + 1) * TB],
                        rr1[64:128, par * TB:(par + 1) * TB],
                        rr2[64:128, par * TB:(par + 1) * TB], ALU.add), s_rp)
                    rpd[par] = s_rp.n

                rec.wait("scalar", s_ev)
                rec.wait("scalar", s_rp)
                rec.wait("scalar", s_k1)
                nm_hist = []
                for h in range(HPC):
                    for tbi in range(16):
                        seq = h * 16 + tbi
                        par = seq % 2
                        if seq >= 2:
                            rec.wait("scalar", s_nm, nm_hist[seq - 2])
                        rec.do("scalar", lambda E, h=h, tbi=tbi, par=par:
                               E.activation(
                            out=nsq[:, par * TB:(par + 1) * TB],
                            in_=kf0[:, h * T + tbi * TB:h * T + (tbi + 1) * TB],
                            func=AF.Square))
                        rec.do("scalar", lambda E, tbi=tbi, par=par:
                               E.activation(
                            out=nsq2[:, par * TB:(par + 1) * TB],
                            in_=kf1[0:64, tbi * TB:(tbi + 1) * TB],
                            func=AF.Square), s_ns)
                        rec.wait("tensor", s_ns)
                        if seq >= 2:
                            rec.wait("tensor", s_nx, seq - 1)
                        rec.do("tensor", lambda E, par=par: E.matmul(
                            pP[par][0:1, :], ones_bf[:, 0:1],
                            nsq[:, par * TB:(par + 1) * TB],
                            start=True, stop=False))
                        rec.do("tensor", lambda E, par=par: E.matmul(
                            pP[par][0:1, :], ones_bf[0:64, 0:1],
                            nsq2[:, par * TB:(par + 1) * TB],
                            start=False, stop=True), s_nm)
                        nm_hist.append(s_nm.n)
                        rec.wait("vector", s_nm)
                        rec.dos("vector", lambda E, par=par: E.reduce_max(
                            row_m[0:1, 0:1], pP[par][0:1, :], axis=AXX), s_nx)
                        if tbi == 0:
                            rec.dos("vector", lambda E, h=h: E.tensor_copy(
                                kmax_sb[0:1, 2 + h:3 + h], row_m[0:1, 0:1]),
                                s_nx)
                        else:
                            rec.dos("vector", lambda E, h=h: E.tensor_tensor(
                                kmax_sb[0:1, 2 + h:3 + h],
                                kmax_sb[0:1, 2 + h:3 + h],
                                row_m[0:1, 0:1], ALU.max), s_nx)
                rec.wait("scalar", s_nx)
                rec.do("scalar", lambda E: E.sqrt(
                    out=kmax_sb[0:1, 0:2], in_=kmax_sb[0:1, 2:4]), s_ns)
                km_t = s_ns.n
                for h in range(HPC):
                    for tbi in range(16):
                        seq = h * 16 + tbi
                        par = seq % 2
                        if seq >= 2:
                            rec.wait("scalar", s_nm)
                        rec.do("scalar", lambda E, h=h, tbi=tbi, par=par:
                               E.activation(
                            out=nsq[:, par * TB:(par + 1) * TB],
                            in_=qf0[:, h * T + tbi * TB:h * T + (tbi + 1) * TB],
                            func=AF.Square))
                        rec.do("scalar", lambda E, h=h, tbi=tbi, par=par:
                               E.activation(
                            out=nsq2[:, par * TB:(par + 1) * TB],
                            in_=qf1[h][0:64, tbi * TB:(tbi + 1) * TB],
                            func=AF.Square), s_ns)
                        rec.wait("tensor", s_ns)
                        if seq >= 1:
                            rec.wait("tensor", s_ns, km_t + 2 * seq)
                        rec.do("tensor", lambda E: E.matmul(
                            pP[2][0:1, :], ones_bf[:, 0:1],
                            nsq[:, (seq % 2) * TB:(seq % 2 + 1) * TB],
                            start=True, stop=False))
                        rec.do("tensor", lambda E: E.matmul(
                            pP[2][0:1, :], ones_bf[0:64, 0:1],
                            nsq2[:, (seq % 2) * TB:(seq % 2 + 1) * TB],
                            start=False, stop=True), s_nm)
                        rec.wait("scalar", s_nm)
                        rec.wait("scalar", s_nx)
                        rec.do("scalar", lambda E: E.sqrt(
                            out=qn_row[:], in_=pP[2][0:1, :]), s_ns)
                        rec.wait("vector", s_ns)
                        rec.do("vector", lambda E, h=h, tbi=tbi:
                               E.tensor_scalar(
                            qf1[h][64:65, tbi * TB:(tbi + 1) * TB],
                            qn_row[:], kmax_sb[0:1, h:h + 1], CMARGIN,
                            ALU.mult, ALU.mult), s_nx)
                rec.play()

            barrier(block)
            release(rs_hr, s_hfr, rs_tr, s_k1, s_mm, s_ev, s_rp, s_nm, s_ns,
                    s_nx, s_st2, s_ld2)

            with ExitStack() as st3:
                e3 = st3.enter_context
                cvout = e3(nc.sbuf_tensor("cvout", [128, 2 * 2048], BF16))
                maskd_sb = e3(nc.sbuf_tensor("masksb", [128, 4 * TB], BF16))
                ebuf = e3(nc.sbuf_tensor("ebuf", [128, 2 * TB], BF16))
                osb = e3(nc.sbuf_tensor("osb", [128, 2 * TB], BF16))
                otile = e3(nc.sbuf_tensor("otile", [128, 2 * TB], BF16))
                dsb = e3(nc.sbuf_tensor("dsb", [1, 2 * TB], F32))
                rsb = e3(nc.sbuf_tensor("rsb", [1, 2 * TB], F32R))
                pS = [e3(nc.psum_tensor(f"ps{i}", [128, TB], F32))
                      for i in range(2)]
                pO = [e3(nc.psum_tensor(f"po{i}", [128, TB], F32))
                      for i in range(2)]
                pDen = [e3(nc.psum_tensor(f"pdn{i}", [1, TB], F32))
                        for i in range(2)]
                pRb2 = [e3(nc.psum_tensor(f"prb2{i}", [128, TB], F32))
                        for i in range(2)]

                s_s = mksem()
                s_e = mksem()
                s_pv = mksem()
                s_dn = mksem()
                s_rc = mksem()
                s_rb = mksem()
                s_oe = mksem()
                s_oo = mksem()
                s_ow = mksem()
                rs_ow = RingSem(mksem, 2)
                ow_t = {}

                rec = Rec(block)
                s_mk = mksem()
                rec.do("sync", lambda E: E.dma_start(
                    out=maskd_sb[:], in_=d["maskd"].ap()), s_mk, 16)
                rec.wait("vector", s_mk)
                cv_jobs = []
                for j in range(88):
                    cv_jobs.append((d["sguT"].ap()[j], sguB.ap()[j], 16))
                for fq in range(4):
                    for i in range(44):
                        cv_jobs.append((d["sdwT"].ap()[fq, i],
                                        sdwB.ap()[fq, i], 4))
                for en in range(NE):
                    for j in range(22):
                        cv_jobs.append((d["w13T"].ap()[en, j],
                                        w13B.ap()[en, j], 16))
                    for fq in range(4):
                        for i in range(11):
                            cv_jobs.append((d["w2T"].ap()[en, fq, i],
                                            w2B.ap()[en, fq, i], 4))
                rs_cvi = RingSem(mksem, 2)
                s_cvc = mksem()
                rs_cvo = RingSem(mksem, 2)
                cv_state = {"i": 0, "conv": [None, None], "out": [None, None]}

                def emit_cv(njobs):
                    for _ in range(njobs):
                        ci = cv_state["i"]
                        if ci >= len(cv_jobs):
                            return
                        cv_state["i"] += 1
                        srcap, dstap, nt = cv_jobs[ci]
                        par = ci % 2
                        w = nt * 128
                        isem = rs_cvi.next()
                        if cv_state["out"][par] is not None:
                            osem, otgt = cv_state["out"][par]
                            rec.wait("gpsimd", osem, otgt)
                        itgt = rec.do("gpsimd", lambda E, s=srcap, par=par,
                                      w=w, nt=nt: dma_tiles(
                            E, cvout[:, par * 2048:par * 2048 + w], s, nt),
                            isem, 16)
                        rec.wait("gpsimd", isem, itgt)
                        osem = rs_cvo.next()
                        otgt = rec.do("gpsimd", lambda E, dd=dstap, par=par,
                                      w=w, nt=nt: E.dma_start(
                            out=dd.rearrange("a p f -> p a f"),
                            in_=cvout[:, par * 2048:par * 2048 + w]
                            .rearrange("p (a f) -> p a f", a=nt)), osem, 16)
                        cv_state["out"][par] = (osem, otgt)

                pairs = [(h, qb) for h in range(HPC) for qb in range(NQB)]
                e_t = {}
                pv_t = {}
                oe_t = {}
                oo_t = {}
                rc_t = {}
                rb_t = {}
                dn_t = {}
                g = 0
                for p, (h, qb) in enumerate(pairs):
                    emit_cv(25)
                    nk = 4 * qb + 4
                    par = p % 2
                    for kt in range(nk):
                        sl = g % 2
                        if g >= 2:
                            rec.wait("tensor", s_e, e_t[g - 2])
                        rec.do("tensor", lambda E, h=h, kt=kt, qb=qb, sl=sl:
                               E.matmul(
                            pS[sl][:],
                            kf0[:, h * T + kt * 128:h * T + (kt + 1) * 128],
                            qf0[:, h * T + qb * TB:h * T + (qb + 1) * TB],
                            start=True, stop=False))
                        rec.do("tensor", lambda E, h=h, kt=kt, sl=sl, qb=qb:
                               E.matmul(
                            pS[sl][:], kf1[:, kt * 128:(kt + 1) * 128],
                            qf1[h][:, qb * TB:(qb + 1) * TB],
                            start=False, stop=True), s_s)
                        rec.wait("scalar", s_s)
                        if g >= 2:
                            rec.wait("scalar", s_pv, pv_t[g - 2])
                        rec.do("scalar", lambda E, sl=sl: E.activation(
                            out=ebuf[:, sl * TB:(sl + 1) * TB],
                            in_=pS[sl][:], func=AF.Exp), s_e)
                        e_t[g] = s_e.n
                        if kt >= nk - 4:
                            rel = kt - (nk - 4)
                            rec.wait("vector", s_e, e_t[g])
                            rec.do("vector", lambda E, sl=sl, rel=rel:
                                   E.tensor_tensor(
                                ebuf[:, sl * TB:(sl + 1) * TB],
                                ebuf[:, sl * TB:(sl + 1) * TB],
                                maskd_sb[:, rel * TB:(rel + 1) * TB],
                                ALU.mult), s_e)
                            e_t[g] = s_e.n
                        rec.wait("tensor", s_e, e_t[g])
                        if kt == 0 and p >= 2:
                            rec.wait("tensor", s_oe, oe_t[p - 2])
                            rec.wait("tensor", s_dn, dn_t[p - 2])
                        rec.do("tensor", lambda E, kt=kt, h=h, sl=sl, nk=nk,
                               par=par: E.matmul(
                            pO[par][:],
                            vf[:, kt * 256 + h * 128:kt * 256 + (h + 1) * 128],
                            ebuf[:, sl * TB:(sl + 1) * TB],
                            start=(kt == 0), stop=(kt == nk - 1)))
                        rec.do("tensor", lambda E, kt=kt, sl=sl, nk=nk,
                               par=par: E.matmul(
                            pDen[par][0:1, :], ones_bf[:, 0:1],
                            ebuf[:, sl * TB:(sl + 1) * TB],
                            start=(kt == 0), stop=(kt == nk - 1)), s_pv)
                        pv_t[g] = s_pv.n
                        g += 1
                    rec.wait("scalar", s_pv, pv_t[g - 1])
                    if p >= 2:
                        rec.wait("scalar", s_rc, rc_t[p - 2])
                    rec.do("scalar", lambda E, par=par: E.activation(
                        out=dsb[0:1, par * TB:(par + 1) * TB],
                        in_=pDen[par][0:1, :], func=AF.Copy), s_dn)
                    dn_t[p] = s_dn.n
                    if p >= 2:
                        rec.wait("scalar", s_oo, oo_t[p - 2])
                    rec.do("scalar", lambda E, par=par: E.activation(
                        out=osb[:, par * TB:(par + 1) * TB],
                        in_=pO[par][:], func=AF.Copy), s_oe)
                    oe_t[p] = s_oe.n
                    rec.wait("vector", s_dn, dn_t[p])
                    if p >= 2:
                        rec.wait("vector", s_rb, rb_t[p - 2])
                    rec.do("vector", lambda E, par=par: E.reciprocal(
                        rsb[0:1, par * TB:(par + 1) * TB],
                        dsb[0:1, par * TB:(par + 1) * TB]), s_rc)
                    rc_t[p] = s_rc.n
                    rec.wait("tensor", s_rc, rc_t[p])
                    if p >= 2:
                        rec.wait("tensor", s_oo, oo_t[p - 2])
                    rec.do("tensor", lambda E, par=par: E.matmul(
                        pRb2[par][:], ones_r[0:1, :],
                        rsb[0:1, par * TB:(par + 1) * TB],
                        start=True, stop=True), s_rb)
                    rb_t[p] = s_rb.n
                    rec.wait("vector", s_oe, oe_t[p])
                    rec.wait("vector", s_rb, rb_t[p])
                    if p >= 2:
                        sem_prev, tgt_prev = ow_t[p - 2]
                        rec.wait("vector", sem_prev, tgt_prev)
                    rec.do("vector", lambda E, par=par: E.tensor_tensor(
                        otile[:, par * TB:(par + 1) * TB],
                        osb[:, par * TB:(par + 1) * TB],
                        pRb2[par][:], ALU.mult), s_oo)
                    oo_t[p] = s_oo.n
                    rec.wait("gpsimd", s_oo, oo_t[p])
                    owsem = rs_ow.next()
                    tgt = rec.do("gpsimd", lambda E, h=h, qb=qb, par=par:
                           E.dma_start(
                        out=oG_in.ap()[h * 128:(h + 1) * 128,
                                       qb * TB:(qb + 1) * TB],
                        in_=otile[:, par * TB:(par + 1) * TB]), owsem, 16)
                    ow_t[p] = (owsem, tgt)
                for sem in rs_ow.sems:
                    rec.wait("gpsimd", sem)
                rec.do("gpsimd", lambda E: E.collective_compute(
                    "AllGather", ALU.bypass, replica_groups=RG,
                    ins=[oG_in.ap().opt()], outs=[oG_out.ap().opt()]),
                    s_cc2, 1)
                t = dbg_out("og", (NH * DH, T), BF16)
                if t is not None:
                    rec.wait("gpsimd", s_cc2, 1)
                    rec.do("gpsimd", lambda E: E.dma_start(
                        out=t.ap(), in_=oG_out.ap()), mksem(), 16)
                rec.play()
                p2b_rel = [s_s, s_e, s_pv, s_dn, s_rc, s_rb, s_oe, s_oo,
                           s_ow, rs_ow, s_mk, rs_cvi, s_cvc, rs_cvo]

            barrier(block)
            release(*p2b_rel)

        with ExitStack() as stB:
            eB = stB.enter_context
            x2_sb = eB(nc.sbuf_tensor("x2sb", [128, 16 * TOK], BF16))
            s_x2 = mksem()

            with ExitStack() as st4:
                e4 = st4.enter_context
                ogb = e4(nc.sbuf_tensor("ogb", [128, 16 * TB], BF16))
                ogr = e4(nc.sbuf_tensor("ogr", [128, 16 * TB], F32R))
                xsb2 = e4(nc.sbuf_tensor("xsb2", [128, 16 * TB], F32R))
                zsb = e4(nc.sbuf_tensor("zsb", [128, 16 * TB], F32R))
                zsq = e4(nc.sbuf_tensor("zsq", [128, 4 * TB], BF16))
                ztmp = e4(nc.sbuf_tensor("ztmp", [128, 2 * TB], F32R))
                ztmp2 = e4(nc.sbuf_tensor("ztmp2", [128, 2 * TB], F32R))
                wr2 = e4(nc.sbuf_tensor("wr2", [128, 32 * 128], F32R))
                pZ = [e4(nc.psum_tensor(f"pz{i}", [128, TB], F32))
                      for i in range(2)]
                pSX2 = e4(nc.psum_tensor("psx2", [1, TB], F32))
                pSQ2 = e4(nc.psum_tensor("psq2", [1, TB], F32))
                pRb3 = e4(nc.psum_tensor("prb3", [128, TB], F32))
                pMrb3 = e4(nc.psum_tensor("pmrb3", [128, TB], F32))

                sZ = {k: mksem() for k in ("sq", "st", "dv", "sd", "bc", "ch")}
                s_zl = mksem()
                s_zc = mksem()
                rs_zw = RingSem(mksem, 2)
                s_zm = mksem()
                s_ze = mksem()

                rec = Rec(block)
                rec.wait("sync", s_cc2, 1)
                cidval = [None]

                def ld_cid(E):
                    cidval[0] = E.value_load(cid_sb[:])
                rec.do("sync", ld_cid)

                zm_hist = []
                ze_hist = []
                chz = {}

                def load_wz(cidx, war_tgt=None):
                    slot0 = (cidx % 2) * 16
                    f = cidx % 16
                    if war_tgt is not None:
                        rec.wait("sync", s_zm, war_tgt)
                    sem = rs_zw.next()
                    tgt = rec.do("sync", lambda E, f=f, s0=slot0: dma_tiles(
                        E, wr2[:, s0 * 128:(s0 + 16) * 128],
                        d["wzT"].ap()[f], 16), sem, 16)
                    chz[cidx] = (slot0, sem, tgt)

                for cidx in range(2):
                    load_wz(cidx)
                for tb in range(NTB):
                    if tb > 0:
                        rec.wait("sync", s_zm)
                        rec.wait("sync", s_x2)
                        rec.wait("sync", sZ["st"])
                    rec.do("sync", lambda E, tb=tb: dma_cols(
                        E, ogb[:],
                        oG_out.ap()[:, bass.ts(cidval[0] * 2 + tb, TB)]),
                        s_zl, 16)
                    rec.do("sync", lambda E, tb=tb: dma_cols(
                        E, xsb2[:], d["xT"].ap()[:, tb * TB:(tb + 1) * TB]),
                        s_zl, 16)
                    rec.wait("vector", s_zl)
                    if tb > 0:
                        rec.wait("vector", s_zm)
                    for a in range(16):
                        rec.do("vector", lambda E, a=a: E.tensor_copy(
                            ogr[:, a * TB:(a + 1) * TB],
                            ogb[:, a * TB:(a + 1) * TB]), s_zc)
                    rec.wait("tensor", s_zc)
                    if tb > 0:
                        rec.wait("scalar", s_x2)
                    for f in range(16):
                        cidx = tb * 16 + f
                        slot0, zwsem, tgt = chz[cidx]
                        rec.wait("tensor", zwsem, tgt)
                        if cidx >= 2:
                            rec.wait("tensor", s_ze, ze_hist[cidx - 2])
                        for kin in range(16):
                            rec.do("tensor", lambda E, s0=slot0, kin=kin, f=f:
                                   E.matmul(
                                pZ[f % 2][:],
                                wr2[:, (s0 + kin) * 128:(s0 + kin + 1) * 128],
                                ogr[:, kin * TB:(kin + 1) * TB],
                                start=(kin == 0), stop=(kin == 15)),
                                s_zm if kin == 15 else None)
                        zm_hist.append(s_zm.n)
                        if cidx + 2 < 32:
                            load_wz(cidx + 2, war_tgt=s_zm.n)
                        rec.wait("scalar", s_zm)
                        rec.do("scalar", lambda E, f=f: E.activation(
                            out=zsb[:, f * TB:(f + 1) * TB],
                            in_=pZ[f % 2][:], func=AF.Identity,
                            bias=bZ_sb[:, f:f + 1]), s_ze)
                        ze_hist.append(s_ze.n)
                    rec.wait("tensor", s_ze)
                    if tb > 0:
                        rec.wait("tensor", s_x2)

                    def apply2(rec, a, pRb, pMrb, tb=tb):
                        rec.dos("vector", lambda E, a=a: E.tensor_tensor(
                            ztmp[:, (a % 2) * TB:(a % 2 + 1) * TB],
                            zsb[:, a * TB:(a + 1) * TB], pRb, ALU.mult),
                            sZ["ch"])
                        rec.dos("vector", lambda E, a=a: E.tensor_tensor(
                            ztmp2[:, (a % 2) * TB:(a % 2 + 1) * TB],
                            ztmp[:, (a % 2) * TB:(a % 2 + 1) * TB], pMrb,
                            ALU.subtract), sZ["ch"])
                        rec.do("vector", lambda E, a=a, tb=tb: E.tensor_tensor(
                            x2_sb[:, a * TOK + tb * TB:a * TOK + (tb + 1) * TB],
                            xsb2[:, a * TB:(a + 1) * TB],
                            ztmp2[:, (a % 2) * TB:(a % 2 + 1) * TB], ALU.add),
                            s_x2)

                    emit_ln(rec, sZ,
                            dict(ones_x=ones_r[:, 0:1], ones_sq=ones_bf[:, 0:1],
                                 ones_row=ones_r[0:1, :],
                                 pSX=pSX2[0:1, :], pSQ=pSQ2[0:1, :],
                                 pRb=pRb3[:], pMrb=pMrb3[:], rows=ROWS),
                            lambda a: zsb[:, a * TB:(a + 1) * TB],
                            lambda a: zsq[:, (a % 4) * TB:(a % 4 + 1) * TB],
                            apply2)
                t = dbg_out("x2", (128, 16 * TOK), BF16)
                if t is not None:
                    rec.wait("gpsimd", s_x2)
                    rec.do("gpsimd", lambda E: E.dma_start(
                        out=t.ap(), in_=x2_sb[:]), mksem(), 16)
                rec.play()
                p2c_rel = [sZ, s_zl, s_zc, rs_zw, s_zm, s_ze]

            barrier(block)
            release(*p2c_rel)

            with ExitStack() as st5:
                e5 = st5.enter_context
                xn3 = e5(nc.sbuf_tensor("xn3", [128, 16 * TB], BF16))
                x2sq = e5(nc.sbuf_tensor("x2sq", [128, 4 * TB], BF16))
                fsq = e5(nc.sbuf_tensor("fsq", [128, 4 * TB], BF16))
                ftmp = e5(nc.sbuf_tensor("ftmp", [128, 2 * TB], F32R))
                ftmp2 = e5(nc.sbuf_tensor("ftmp2", [128, 2 * TB], F32R))
                facc = e5(nc.sbuf_tensor("facc", [128, 16 * TB], F32R))
                sh_e = e5(nc.sbuf_tensor("sh_e", [128, 11 * TB], BF16))
                asb = e5(nc.sbuf_tensor("asb", [128, 2 * TB], BF16))
                bsb = e5(nc.sbuf_tensor("bsb", [128, 2 * TB], BF16))
                shtile = e5(nc.sbuf_tensor("shtile", [128, 2 * TB], BF16))
                shr = e5(nc.sbuf_tensor("shr", [128, 4 * TB], BF16))
                wr3 = e5(nc.sbuf_tensor("wr3", [128, 32 * 128], BF16))
                sigb = e5(nc.sbuf_tensor("sigb", [128, TB], F32))
                wb_sb = e5(nc.sbuf_tensor("wb_sb", [128, TB], F32))
                lgsb = e5(nc.sbuf_tensor("lgsb", [NE, 4 * 128], BF16))
                ltok = e5(nc.sbuf_tensor("ltok", [128, 4 * NE], BF16))
                wtok = e5(nc.sbuf_tensor("wtok", [128, 4 * NE], BF16))
                wrow = e5(nc.sbuf_tensor("wrow", [NE, 4 * 128], BF16))
                wexp = e5(nc.sbuf_tensor("wexp", [1, TB], BF16))
                rsc = e5(nc.sbuf_tensor("rsc", [128, 40], F32))
                outsb = e5(nc.sbuf_tensor("outsb", [128, 2 * TB], F32))
                sig_row = e5(nc.sbuf_tensor("sigrow", [1, TB], F32R))
                pb = [e5(nc.psum_tensor(f"pb{i}", [128, TB], F32))
                      for i in range(6)]
                pTrB = e5(nc.psum_tensor("ptrb", [128, TB], BF16))

                sL3 = {k: mksem() for k in ("sq", "st", "dv", "sd", "bc", "ch")}
                sL4 = {k: mksem() for k in ("sq", "st", "dv", "sd", "bc", "ch")}
                rs_w3 = RingSem(mksem, 9)
                s_w3f = mksem()
                rs_shr = RingSem(mksem, 4)
                rs_spl = RingSem(mksem, 2)
                s_xn3 = mksem()
                s_rt = mksem()
                s_wex = mksem()
                s_sg = mksem()
                s_sl = mksem()
                s_sh = mksem()
                s_spl = mksem()
                s_shr = mksem()
                s_fa = mksem()
                s_wb = mksem()
                s_out = mksem()

                rec = Rec(block)
                t_xn3 = dbg_out("xn3", (NTB, 128, 16 * TB), F32R)
                t_wts = dbg_out("wts", (NTB, NE, 4 * 128), BF16)
                t_ltok = dbg_out("ltok", (NTB, 128, 4 * NE), BF16)
                t_wtok = dbg_out("wtok", (NTB, 128, 4 * NE), BF16)
                t_rsc = dbg_out("rsc", (NTB, 128, 40), F32)
                t_shs = dbg_out("shs", (44, 128, TB), F32R)
                t_shrd = dbg_out("shrd", (128, 4 * TB), F32R)
                t_fq0 = dbg_out("fq0", (128, 4 * TB), F32R)
                t_sgb = dbg_out("sgb", (128, TB), F32)
                t_sig = dbg_out("sig", (NTB, 128, TB), F32)
                t_fsh = dbg_out("fsh", (NTB, 128, 16 * TB), F32R)
                t_fac = dbg_out("fac", (NTB, 128, 16 * TB), F32R)
                s_dbg = mksem()
                ring_pos = [0]
                ring_owner = [0] * 32
                ring_seq = [0]
                free_tgt = {}

                def r3_load(src_fn, n):
                    if ring_pos[0] + n > 32:
                        ring_pos[0] = 0
                    s0 = ring_pos[0]
                    ring_pos[0] += n
                    prev = max(ring_owner[s0:s0 + n])
                    ring_seq[0] += 1
                    seq = ring_seq[0]
                    for s in range(s0, s0 + n):
                        ring_owner[s] = seq
                    if prev > 0:
                        rec.wait("sync", s_w3f, free_tgt[prev])
                    sem = rs_w3.next()
                    tgt = rec.do("sync", lambda E, s0=s0, n=n, src_fn=src_fn:
                                 dma_tiles(
                        E, wr3[:, s0 * 128:(s0 + n) * 128], src_fn(), n),
                        sem, 16)
                    return seq, s0, (sem, tgt)

                sl_par = [0, 0]
                sh_par = [0, 0]
                spl_hist = [None] * 44
                out_hist = []

                for tb in range(NTB):
                    rec.wait("scalar", s_x2)
                    rec.wait("tensor", s_x2)
                    if tb > 0:
                        rec.wait("tensor", s_xn3)
                        rec.wait("vector", s_w3f)
                        rec.wait("scalar", sL3["st"])

                    def apply3(rec, a, pRb, pMrb, tb=tb):
                        rec.dos("vector", lambda E, a=a, tb=tb: E.tensor_tensor(
                            ftmp[:, (a % 2) * TB:(a % 2 + 1) * TB],
                            x2_sb[:, a * TOK + tb * TB:a * TOK + (tb + 1) * TB],
                            pRb, ALU.mult), sL3["ch"])
                        rec.do("vector", lambda E, a=a: E.tensor_tensor(
                            xn3[:, a * TB:(a + 1) * TB],
                            ftmp[:, (a % 2) * TB:(a % 2 + 1) * TB], pMrb,
                            ALU.subtract), s_xn3)

                    emit_ln(rec, sL3,
                            dict(ones_x=ones_bf[:, 0:1], ones_sq=ones_bf[:, 0:1],
                                 ones_row=ones_r[0:1, :],
                                 pSX=pb[0][0:1, :], pSQ=pb[1][0:1, :],
                                 pRb=pb[2][:], pMrb=pb[3][:], rows=ROWS),
                            lambda a, tb=tb: x2_sb[:, a * TOK + tb * TB:
                                                   a * TOK + (tb + 1) * TB],
                            lambda a: x2sq[:, (a % 4) * TB:(a % 4 + 1) * TB],
                            apply3)
                    if t_xn3 is not None:
                        rec.wait("gpsimd", s_xn3)
                        rec.do("gpsimd", lambda E, tb=tb: E.dma_start(
                            out=t_xn3.ap()[tb], in_=xn3[:]), s_dbg, 16)
                    rec.wait("tensor", s_xn3)
                    for a in range(16):
                        rec.do("tensor", lambda E, a=a: E.matmul(
                            pb[4][0:NE, :], gw_sb[:, a * NE:(a + 1) * NE],
                            xn3[:, a * TB:(a + 1) * TB],
                            start=(a == 0), stop=(a == 15)))
                        rec.do("tensor", lambda E, a=a: E.matmul(
                            pb[5][0:1, :], sgw_sb[:, a:a + 1],
                            xn3[:, a * TB:(a + 1) * TB],
                            start=(a == 0), stop=(a == 15)),
                            s_rt if a == 15 else None)
                    rec.wait("scalar", s_rt)
                    rec.do("scalar", lambda E: E.activation(
                        out=lgsb[0:NE, :], in_=pb[4][0:NE, 0:4 * 128],
                        func=AF.Copy), s_rt)
                    rec.do("scalar", lambda E: E.activation(
                        out=sig_row[:], in_=pb[5][0:1, :], func=AF.Sigmoid),
                        s_sg)
                    rec.wait("tensor", s_rt)
                    for tt in range(4):
                        rec.do("tensor", lambda E, tt=tt: E.transpose(
                            pTrB[:, tt * NE:(tt + 1) * NE],
                            lgsb[0:NE, tt * 128:(tt + 1) * 128],
                            eye_sb[0:NE, 0:NE]),
                            s_rt if tt == 3 else None)
                    rec.wait("scalar", s_rt)
                    rec.dos("scalar", lambda E: E.activation(
                        out=ltok[:], in_=pTrB[:, 0:4 * NE], func=AF.Copy),
                        s_rt)
                    if t_ltok is not None:
                        rec.wait("gpsimd", s_rt)
                        rec.do("gpsimd", lambda E, tb=tb: E.dma_start(
                            out=t_ltok.ap()[tb], in_=ltok[:]), s_dbg, 16)
                    for tt in range(4):
                        sl = slice(tt * NE, (tt + 1) * NE)
                        rec.wait("vector", s_rt)
                        rec.dos("vector", lambda E, sl=sl: E.reduce_max(
                            rsc[:, 0:1], ltok[:, sl], axis=AXX), s_rt)
                        rec.do("vector", lambda E: E.tensor_scalar(
                            rsc[:, 1:2], rsc[:, 0:1], -1.0, None, ALU.mult),
                            s_rt)
                        rec.wait("scalar", s_rt)
                        rec.do("scalar", lambda E, sl=sl: E.activation(
                            out=rsc[:, 8:16], in_=ltok[:, sl], func=AF.Exp,
                            bias=rsc[:, 1:2]), s_rt)
                        rec.wait("vector", s_rt)
                        rec.dos("vector", lambda E: E.reduce_sum(
                            rsc[:, 2:3], rsc[:, 8:16], axis=AXX), s_rt)
                        rec.dos("vector", lambda E: E.reciprocal(
                            rsc[:, 3:4], rsc[:, 2:3]), s_rt)
                        rec.dos("vector", lambda E: E.tensor_scalar(
                            rsc[:, 16:24], rsc[:, 8:16], rsc[:, 3:4], None,
                            ALU.mult), s_rt)
                        rec.dos("vector", lambda E: E.reduce_max(
                            rsc[:, 4:5], rsc[:, 16:24], axis=AXX), s_rt)
                        rec.dos("vector", lambda E: E.tensor_scalar(
                            rsc[:, 24:32], rsc[:, 16:24], rsc[:, 4:5], None,
                            ALU.is_equal), s_rt)
                        rec.dos("vector", lambda E: E.tensor_scalar(
                            rsc[:, 24:32], rsc[:, 24:32], -1.0, 1.0,
                            ALU.mult, ALU.add), s_rt)
                        rec.dos("vector", lambda E: E.tensor_tensor(
                            rsc[:, 32:40], rsc[:, 24:32], rsc[:, 16:24],
                            ALU.mult), s_rt)
                        rec.dos("vector", lambda E: E.reduce_max(
                            rsc[:, 5:6], rsc[:, 32:40], axis=AXX), s_rt)
                        rec.dos("vector", lambda E: E.tensor_scalar(
                            rsc[:, 24:32], rsc[:, 16:24], rsc[:, 5:6], None,
                            ALU.is_ge), s_rt)
                        rec.do("vector", lambda E, tt=tt: E.tensor_tensor(
                            wtok[:, tt * NE:(tt + 1) * NE], rsc[:, 16:24],
                            rsc[:, 24:32], ALU.mult), s_rt)
                    if t_wtok is not None:
                        rec.wait("gpsimd", s_rt)
                        rec.do("gpsimd", lambda E, tb=tb: E.dma_start(
                            out=t_wtok.ap()[tb], in_=wtok[:]), s_dbg, 16)
                    if t_rsc is not None:
                        rec.wait("gpsimd", s_rt)
                        rec.do("gpsimd", lambda E, tb=tb: E.dma_start(
                            out=t_rsc.ap()[tb], in_=rsc[:]), s_dbg, 16)
                    rec.wait("tensor", s_rt)
                    for tt in range(4):
                        rec.do("tensor", lambda E, tt=tt: E.transpose(
                            pTrB[0:NE, tt * 128:(tt + 1) * 128],
                            wtok[:, tt * NE:(tt + 1) * NE], eye_sb[:]),
                            s_rt if tt == 3 else None)
                    rec.wait("scalar", s_rt)
                    rec.do("scalar", lambda E: E.activation(
                        out=wrow[:], in_=pTrB[0:NE, 0:4 * 128], func=AF.Copy),
                        s_rt)
                    rec.wait("gpsimd", s_rt)
                    if t_wts is not None:
                        rec.do("gpsimd", lambda E, tb=tb: E.dma_start(
                            out=t_wts.ap()[tb], in_=wrow[:]), s_dbg, 16)
                    rec.wait("tensor", s_sg)
                    rec.wait("tensor", s_wb)
                    rec.do("tensor", lambda E: E.matmul(
                        pb[4][:], ones_r[0:1, :], sig_row[:],
                        start=True, stop=True), s_sg)
                    rec.wait("scalar", s_sg)
                    rec.do("scalar", lambda E: E.activation(
                        out=sigb[:], in_=pb[4][:], func=AF.Copy), s_sg)
                    sig_done = s_sg.n
                    if t_sig is not None:
                        rec.wait("gpsimd", s_sg, sig_done)
                        rec.do("gpsimd", lambda E, tb=tb: E.dma_start(
                            out=t_sig.ap()[tb], in_=sigb[:]), s_dbg, 16)

                    for j in range(44):
                        _, s0g, tg = r3_load(
                            lambda j=j: sguB.ap()[j], 16)
                        seqg = ring_seq[0]
                        _, s0u, tu = r3_load(
                            lambda j=j: sguB.ap()[44 + j], 16)
                        sequ = ring_seq[0]
                        par = j % 2
                        rec.wait("tensor", tg[0], tg[1])
                        rec.wait("tensor", tu[0], tu[1])
                        if j >= 2:
                            rec.wait("tensor", s_sl, sl_par[par])
                            rec.wait("tensor", s_sh, sh_par[par])
                        if j == 0:
                            rec.wait("tensor", s_sg, sig_done)
                        for a in range(16):
                            rec.do("tensor", lambda E, s0=s0g, a=a, par=par:
                                   E.matmul(
                                pb[0 + par][:],
                                wr3[:, (s0 + a) * 128:(s0 + a + 1) * 128],
                                xn3[:, a * TB:(a + 1) * TB],
                                start=(a == 0), stop=(a == 15)),
                                s_w3f if a == 15 else None)
                        free_tgt[seqg] = s_w3f.n
                        for a in range(16):
                            rec.do("tensor", lambda E, s0=s0u, a=a, par=par:
                                   E.matmul(
                                pb[2 + par][:],
                                wr3[:, (s0 + a) * 128:(s0 + a + 1) * 128],
                                xn3[:, a * TB:(a + 1) * TB],
                                start=(a == 0), stop=(a == 15)),
                                s_w3f if a == 15 else None)
                        free_tgt[sequ] = s_w3f.n
                        rec.wait("scalar", s_w3f)
                        if j >= 2:
                            rec.wait("scalar", s_sh, sh_par[par])
                        rec.do("scalar", lambda E, par=par: E.activation(
                            out=asb[:, par * TB:(par + 1) * TB],
                            in_=pb[0 + par][:], func=AF.Silu), s_sl)
                        sl_par[par] = s_sl.n
                        rec.wait("vector", s_sl, sl_par[par])
                        if j >= 2:
                            psem, ptgt = spl_hist[j - 2]
                            rec.wait("vector", psem, ptgt)
                        rec.do("vector", lambda E, par=par: E.tensor_tensor(
                            shtile[:, par * TB:(par + 1) * TB],
                            asb[:, par * TB:(par + 1) * TB],
                            pb[2 + par][:], ALU.mult), s_sh)
                        sh_par[par] = s_sh.n
                        rec.wait("gpsimd", s_sh, sh_par[par])
                        spsem = rs_spl.next()
                        sptgt = rec.do("gpsimd", lambda E, j=j, par=par:
                                       E.dma_start(
                            out=shs.ap()[j], in_=shtile[:, par * TB:
                                                        (par + 1) * TB]),
                            spsem, 16)
                        spl_hist[j] = (spsem, sptgt)
                    if t_shs is not None and tb == 0:
                        rec.wait("gpsimd", s_spl)
                        rec.do("gpsimd", lambda E: E.dma_start(
                            out=t_shs.ap(), in_=shs.ap()), s_dbg, 16)
                    if t_sgb is not None and tb == 0:
                        rec.wait("gpsimd", s_sg, sig_done)
                        rec.do("gpsimd", lambda E: E.dma_start(
                            out=t_sgb.ap(), in_=sigb[:]), s_dbg, 16)
                    shr_t = [0, 0, 0, 0]
                    dn_mm = [0]
                    for fq in range(4):
                        for i in range(44):
                            seq, s0, tw = r3_load(
                                lambda fq=fq, i=i: sdwB.ap()[fq, i], 4)
                            if fq == 0 and i == 0:
                                for ssem, stgt in spl_hist[-2:]:
                                    rec.wait("gpsimd", ssem, stgt)
                            if fq * 44 + i >= 4:
                                rec.wait("gpsimd", s_w3f, shr_t[i % 4])
                            shsem = rs_shr.next()
                            shtgt = rec.do("gpsimd", lambda E, i=i:
                                           E.dma_start(
                                out=shr[:, (i % 4) * TB:(i % 4 + 1) * TB],
                                in_=shs.ap()[i]), shsem, 16)
                            rec.wait("tensor", shsem, shtgt)
                            rec.wait("tensor", tw[0], tw[1])
                            if i == 0:
                                rec.wait("tensor", s_fa)
                                rec.wait("tensor", s_sl)
                                rec.wait("tensor", s_sh)
                            for f4 in range(4):
                                rec.do("tensor", lambda E, s0=s0, f4=f4, i=i:
                                       E.matmul(
                                    pb[f4][:],
                                    wr3[:, (s0 + f4) * 128:(s0 + f4 + 1) * 128],
                                    shr[:, (i % 4) * TB:(i % 4 + 1) * TB],
                                    start=(i == 0), stop=(i == 43)),
                                    s_w3f if f4 == 3 else None)
                            free_tgt[seq] = s_w3f.n
                            shr_t[i % 4] = s_w3f.n
                        if t_shrd is not None and tb == 0 and fq == 0:
                            rec.wait("gpsimd", s_shr)
                            rec.do("gpsimd", lambda E: E.dma_start(
                                out=t_shrd.ap(), in_=shr[:]), s_dbg, 16)
                            rec.wait("sync", s_dbg)
                            rec.wait("tensor", s_dbg)
                        rec.wait("vector", s_w3f)
                        rec.wait("vector", s_sg, sig_done)
                        if tb > 0 and fq == 0:
                            rec.wait("vector", sL4["sq"])
                        for f4 in range(4):
                            rec.do("vector", lambda E, fq=fq, f4=f4, tb=tb:
                                   E.tensor_tensor(
                                facc[:, (fq * 4 + f4) * TB:
                                     (fq * 4 + f4 + 1) * TB],
                                pb[f4][:], sigb[:], ALU.mult), s_fa)
                        if t_fq0 is not None and tb == 0 and fq == 0:
                            rec.wait("gpsimd", s_fa)
                            rec.do("gpsimd", lambda E: E.dma_start(
                                out=t_fq0.ap(), in_=facc[:, 0:4 * TB]),
                                s_dbg, 16)
                            rec.wait("tensor", s_dbg)
                    if t_fsh is not None:
                        rec.wait("gpsimd", s_fa)
                        rec.do("gpsimd", lambda E, tb=tb: E.dma_start(
                            out=t_fsh.ap()[tb], in_=facc[:]), s_dbg, 16)
                        rec.wait("vector", s_dbg)
                    for en in range(NE):
                        if s_wb.n > 0:
                            rec.wait("gpsimd", s_wb)
                        rec.do("gpsimd", lambda E, en=en: E.dma_start(
                            out=wexp[0:1, :], in_=wrow[en:en + 1, :]),
                            s_wex, 16)
                        rec.wait("tensor", s_wex)
                        rec.wait("tensor", s_wb)
                        rec.do("tensor", lambda E: E.matmul(
                            pb[4][:], ones_bf[0:1, :], wexp[0:1, :],
                            start=True, stop=True), s_wb)
                        rec.wait("scalar", s_wb)
                        rec.do("scalar", lambda E: E.activation(
                            out=wb_sb[:], in_=pb[4][:], func=AF.Copy), s_wb)
                        wb_done = s_wb.n
                        for j in range(11):
                            _, s0g, tg = r3_load(
                                lambda en=en, j=j: w13B.ap()[en, j], 16)
                            seqg = ring_seq[0]
                            _, s0u, tu = r3_load(
                                lambda en=en, j=j: w13B.ap()[en, 11 + j], 16)
                            sequ = ring_seq[0]
                            par = j % 2
                            rec.wait("tensor", tg[0], tg[1])
                            rec.wait("tensor", tu[0], tu[1])
                            if j >= 2:
                                rec.wait("tensor", s_sl, sl_par[par])
                                rec.wait("tensor", s_sh, sh_par[par])
                            if j == 0:
                                rec.wait("tensor", s_fa)
                            for a in range(16):
                                rec.do("tensor", lambda E, s0=s0g, a=a,
                                       par=par: E.matmul(
                                    pb[0 + par][:],
                                    wr3[:, (s0 + a) * 128:(s0 + a + 1) * 128],
                                    xn3[:, a * TB:(a + 1) * TB],
                                    start=(a == 0), stop=(a == 15)),
                                    s_w3f if a == 15 else None)
                            free_tgt[seqg] = s_w3f.n
                            for a in range(16):
                                rec.do("tensor", lambda E, s0=s0u, a=a,
                                       par=par: E.matmul(
                                    pb[2 + par][:],
                                    wr3[:, (s0 + a) * 128:(s0 + a + 1) * 128],
                                    xn3[:, a * TB:(a + 1) * TB],
                                    start=(a == 0), stop=(a == 15)),
                                    s_w3f if a == 15 else None)
                            free_tgt[sequ] = s_w3f.n
                            rec.wait("scalar", s_w3f)
                            rec.wait("scalar", s_wb, wb_done)
                            if True:
                                rec.wait("scalar", s_sh, sh_par[par])
                            rec.do("scalar", lambda E, par=par: E.activation(
                                out=asb[:, par * TB:(par + 1) * TB],
                                in_=pb[0 + par][:], func=AF.Silu), s_sl)
                            sl_par[par] = s_sl.n
                            rec.wait("vector", s_sl, sl_par[par])
                            rec.dos("vector", lambda E, par=par:
                                   E.tensor_tensor(
                                bsb[:, par * TB:(par + 1) * TB],
                                pb[2 + par][:], wb_sb[:], ALU.mult), s_sh)
                            rec.do("vector", lambda E, par=par, j=j:
                                   E.tensor_tensor(
                                sh_e[:, j * TB:(j + 1) * TB],
                                asb[:, par * TB:(par + 1) * TB],
                                bsb[:, par * TB:(par + 1) * TB], ALU.mult),
                                s_sh)
                            sh_par[par] = s_sh.n
                        for fq in range(4):
                            for i in range(11):
                                seq, s0, tw = r3_load(
                                    lambda en=en, fq=fq, i=i:
                                    w2B.ap()[en, fq, i], 4)
                                rec.wait("tensor", tw[0], tw[1])
                                if i == 0:
                                    rec.wait("tensor", s_fa)
                                    rec.wait("tensor", s_sl)
                                    rec.wait("tensor", s_sh)
                                for f4 in range(4):
                                    rec.do("tensor", lambda E, s0=s0, f4=f4,
                                           i=i: E.matmul(
                                        pb[f4][:],
                                        wr3[:, (s0 + f4) * 128:
                                            (s0 + f4 + 1) * 128],
                                        sh_e[:, i * TB:(i + 1) * TB],
                                        start=(i == 0), stop=(i == 10)),
                                        s_w3f if f4 == 3 else None)
                                free_tgt[seq] = s_w3f.n
                            rec.wait("vector", s_w3f)
                            for f4 in range(4):
                                rec.do("vector", lambda E, fq=fq, f4=f4:
                                       E.tensor_tensor(
                                    facc[:, (fq * 4 + f4) * TB:
                                         (fq * 4 + f4 + 1) * TB],
                                    facc[:, (fq * 4 + f4) * TB:
                                         (fq * 4 + f4 + 1) * TB],
                                    pb[f4][:], ALU.add), s_fa)
                    if t_fac is not None:
                        rec.wait("gpsimd", s_fa)
                        rec.do("gpsimd", lambda E, tb=tb: E.dma_start(
                            out=t_fac.ap()[tb], in_=facc[:]), s_dbg, 16)
                        rec.wait("scalar", s_dbg)
                    rec.wait("scalar", s_fa)
                    rec.wait("tensor", s_fa)

                    def apply4(rec, a, pRb, pMrb, tb=tb):
                        rec.dos("vector", lambda E, a=a: E.tensor_tensor(
                            ftmp[:, (a % 2) * TB:(a % 2 + 1) * TB],
                            facc[:, a * TB:(a + 1) * TB], pRb, ALU.mult),
                            sL4["ch"])
                        rec.dos("vector", lambda E, a=a: E.tensor_tensor(
                            ftmp2[:, (a % 2) * TB:(a % 2 + 1) * TB],
                            ftmp[:, (a % 2) * TB:(a % 2 + 1) * TB], pMrb,
                            ALU.subtract), sL4["ch"])
                        if len(out_hist) >= 2:
                            rec.wait("vector", s_out, out_hist[-2])
                        rec.do("vector", lambda E, a=a, tb=tb: E.tensor_tensor(
                            outsb[:, (a % 2) * TB:(a % 2 + 1) * TB],
                            x2_sb[:, a * TOK + tb * TB:a * TOK + (tb + 1) * TB],
                            ftmp2[:, (a % 2) * TB:(a % 2 + 1) * TB], ALU.add),
                            s_xn3)
                        rec.wait("gpsimd", s_xn3)
                        rec.do("gpsimd", lambda E, a=a, tb=tb: E.dma_start(
                            out=yT.ap()[a * 128:(a + 1) * 128,
                                        tb * TB:(tb + 1) * TB],
                            in_=outsb[:, (a % 2) * TB:(a % 2 + 1) * TB]),
                            s_out, 16)
                        out_hist.append(s_out.n)

                    emit_ln(rec, sL4,
                            dict(ones_x=ones_r[:, 0:1], ones_sq=ones_bf[:, 0:1],
                                 ones_row=ones_r[0:1, :],
                                 pSX=pb[0][0:1, :], pSQ=pb[1][0:1, :],
                                 pRb=pb[2][:], pMrb=pb[3][:], rows=ROWS),
                            lambda a: facc[:, a * TB:(a + 1) * TB],
                            lambda a: fsq[:, (a % 4) * TB:(a % 4 + 1) * TB],
                            apply4)
                rec.wait("gpsimd", s_out)
                rec.play()

    return nc, dbg_t



def _rope_tables(t_len, dd):
    pos = np.repeat(np.arange(dd // 2, dtype=np.float64), 2)
    theta = np.exp(-pos / dd * np.log(10000.0))
    ang = np.arange(1, t_len + 1, dtype=np.float64)[:, None] * theta[None, :]
    return np.cos(ang), np.sin(ang)


def prep_in_maps(inputs):
    f32 = np.float32
    x = np.asarray(inputs["L0_attn_x"], f32)[0]
    WA = np.asarray(inputs["L0_attn_WA"], f32)
    bA = np.asarray(inputs["L0_attn_bA"], f32)
    WB = np.asarray(inputs["L0_attn_WB"], f32)
    bB = np.asarray(inputs["L0_attn_bB"], f32)
    WC = np.asarray(inputs["L0_attn_WC"], f32)
    bC = np.asarray(inputs["L0_attn_bC"], f32)
    WUQ = np.asarray(inputs["L0_attn_WUQ"], f32)
    WUK = np.asarray(inputs["L0_attn_WUK"], f32)
    WUV = np.asarray(inputs["L0_attn_WUV"], f32)
    WZ = np.asarray(inputs["L0_attn_WZ"], f32)
    bZ = np.asarray(inputs["L0_attn_bZ"], f32)
    gw = np.asarray(inputs["L0_ffn_gate_w"], f32)
    sgw = np.asarray(inputs["L0_ffn_shared_gate_w"], f32)
    sguw = np.asarray(inputs["L0_ffn_shared_gate_up_w"], f32)
    sdw = np.asarray(inputs["L0_ffn_shared_down_w"], f32)
    w13 = np.asarray(inputs["L0_ffn_w13"], f32)
    w2 = np.asarray(inputs["L0_ffn_w2"], f32)

    xT = np.ascontiguousarray(x.T)
    waT = np.ascontiguousarray(
        WA.reshape(16, 128, 8, 128).transpose(2, 0, 1, 3))
    bA_r = np.ascontiguousarray(bA.reshape(8, 128).T)
    wbT = np.ascontiguousarray(WB.reshape(8, 128, DHR))
    bB_c = bB.reshape(DHR, 1)
    wzT = np.ascontiguousarray(
        WZ.reshape(16, 128, 16, 128).transpose(2, 0, 1, 3))
    bZ_r = np.ascontiguousarray(bZ.reshape(16, 128).T)
    gwT = np.ascontiguousarray(
        gw.T.reshape(16, 128, NE).transpose(1, 0, 2).reshape(128, 16 * NE))
    sgwT = np.ascontiguousarray(sgw.T.reshape(16, 128).T)
    sguT = np.ascontiguousarray(
        sguw.T.reshape(16, 128, 88, 128).transpose(2, 0, 1, 3))
    sdwT = np.ascontiguousarray(
        sdw.T.reshape(44, 128, 4, 4, 128).transpose(2, 0, 3, 1, 4))
    w13T = np.ascontiguousarray(
        w13.transpose(0, 2, 1).reshape(NE, 16, 128, 22, 128)
        .transpose(0, 3, 1, 2, 4))
    w2T = np.ascontiguousarray(
        w2.transpose(0, 2, 1).reshape(NE, 11, 128, 4, 4, 128)
        .transpose(0, 3, 1, 4, 2, 5))

    cq_full, sq_full = _rope_tables(T, NH * DHR)
    ck_full, sk_full = _rope_tables(T, DHR)
    ones_bf = np.ones((128, 128), BF)
    ones_r = np.ones((128, 128), f32)
    eye = np.eye(128, dtype=f32).astype(BF)
    RTm = np.zeros((128, 128), f32)
    for i in range(64):
        RTm[2 * i + 1, 2 * i] = -1.0
        RTm[2 * i, 2 * i + 1] = 1.0
    RTm = RTm.astype(BF)
    maskd = np.zeros((128, 4 * TB), f32)
    for rel in range(4):
        kk = np.arange(128)[:, None] + 128 * rel
        qq = np.arange(TB)[None, :]
        maskd[:, rel * TB:(rel + 1) * TB] = (kk <= qq).astype(f32)
    maskd = maskd.astype(BF)

    in_maps = []
    for c in range(NCORE):
        tok = slice(c * TOK, (c + 1) * TOK)
        hd = slice(c * 2 * DH, (c + 1) * 2 * DH)
        rd = slice(c * 128, (c + 1) * 128)
        cosq = np.ascontiguousarray(
            (cq_full[:, rd].T * SCALE)).astype(BF)
        sinq = np.ascontiguousarray(
            (sq_full[:, rd].T * SCALE)).astype(BF)
        m = {
            "xT": np.ascontiguousarray(xT[:, tok]),
            "cid": np.array([[c]], np.int32),
            "waT": waT, "bA_r": bA_r, "wbT": wbT, "bB_c": bB_c,
            "cosk": np.ascontiguousarray(ck_full[tok, :].T).astype(BF),
            "sink": np.ascontiguousarray(sk_full[tok, :].T).astype(BF),
            "wuq": np.ascontiguousarray(WUQ[:, hd].reshape(4, 128, 256)),
            "wuk": np.ascontiguousarray(WUK[:, hd].reshape(4, 128, 256)),
            "wuv": np.ascontiguousarray(WUV[:, hd].reshape(4, 128, 256)),
            "wcc": np.ascontiguousarray(WC[:, rd].reshape(8, 128, 128)),
            "bC_c": np.ascontiguousarray(bC[rd].reshape(128, 1)),
            "cosq": np.ascontiguousarray(
                cosq.reshape(128, 16, TB).transpose(1, 0, 2)),
            "sinq": np.ascontiguousarray(
                sinq.reshape(128, 16, TB).transpose(1, 0, 2)),
            "RT": RTm, "eye": eye, "maskd": maskd,
            "ones_bf": ones_bf, "ones_r": ones_r,
            "wzT": wzT, "bZ_r": bZ_r, "gwT": gwT, "sgwT": sgwT,
            "sguT": sguT, "sdwT": sdwT, "w13T": w13T, "w2T": w2T,
        }
        in_maps.append(m)
    return in_maps


_CACHE = {}


def kernel(**inputs):
    if "nc" not in _CACHE:
        _CACHE["nc"], _ = build()
    nc = _CACHE["nc"]
    in_maps = prep_in_maps(inputs)
    res = run_bass_kernel_spmd(nc, in_maps, core_ids=list(range(NCORE)))
    y = np.concatenate([res.results[c]["yT"] for c in range(NCORE)], axis=1)
    return np.ascontiguousarray(y.T).reshape(1, T, C).astype(np.float32)


# version 04
# speedup vs baseline: 1.1663x; 1.1663x over previous
import numpy as np
import ml_dtypes
from contextlib import ExitStack

import concourse.bass as bass
import concourse.mybir as mybir
from concourse.bass_utils import run_bass_kernel_spmd

F32 = mybir.dt.float32
F32R = mybir.dt.float32r
BF16 = mybir.dt.bfloat16
I32 = mybir.dt.int32
AF = mybir.ActivationFunctionType
ALU = mybir.AluOpType
AXX = mybir.AxisListType.X

NCORE = 8
T, C = 8192, 2048
NH, DH, DHR, LL = 16, 128, 64, 512
TOK = T // NCORE
TB = 512
NTB = TOK // TB
SI, MI, NE = 5632, 1408, 8
EPS = 1e-5
SCALE = float((DH + DHR) ** -0.5)
CMARGIN = -1.02
NQB = T // TB
HPC = 2
BF = ml_dtypes.bfloat16


def dma_tiles(E, dst2d, src3d, a):
    return E.dma_start(out=dst2d.rearrange("p (a f) -> p a f", a=a),
                       in_=src3d.rearrange("a p f -> p a f"))


def dma_cols(E, dst2d, src2d, a=16):
    return E.dma_start(out=dst2d.rearrange("p (a t) -> p a t", a=a),
                       in_=src2d.rearrange("(a p) t -> p a t", p=128))


class Sem:
    def __init__(self, handle):
        self.h = handle
        self.n = 0


class Rec:
    ENG = ("sync", "tensor", "scalar", "vector", "gpsimd")

    def __init__(self, block):
        self.block = block
        self.q = {k: [] for k in self.ENG}

    def wait(self, eng, sem, n=None):
        tgt = sem.n if n is None else n
        if tgt > 0:
            self.q[eng].append(lambda E, s=sem.h, t=tgt: E.wait_ge(s, t))

    def do(self, eng, build, sem=None, k=1):
        if sem is None:
            self.q[eng].append(build)
            return None
        sem.n += k
        tgt = sem.n

        def f(E, b=build, s=sem.h, kk=k):
            b(E).then_inc(s, kk)
        self.q[eng].append(f)
        return tgt

    def dos(self, eng, build, sem, k=1):
        t = self.do(eng, build, sem, k)
        self.wait(eng, sem, t)
        return t

    def play(self):
        for name in self.ENG:
            ops = self.q[name]
            if ops:
                getattr(self.block, name)(
                    lambda E, ops=ops: [op(E) for op in ops])
        self.q = {k: [] for k in self.ENG}


class RingSem:

    def __init__(self, mksem, k):
        self.sems = [mksem() for _ in range(k)]
        self.k = k
        self.idx = 0

    def next(self):
        s = self.sems[self.idx % self.k]
        self.idx += 1
        return s


def emit_ln(rec, S, B, src_tile, sq_tile, apply_emit, n_tiles=16, denom=C):
    row_m, row_v, row_sd, row_r, row_mr = B["rows"]
    sq_hist = []
    st_t = []
    rec.wait("tensor", S["dv"])
    for a in range(n_tiles):
        if a >= 4:
            rec.wait("scalar", S["st"], st_t[a - 4])
        rec.do("scalar", lambda E, d=sq_tile(a), s=src_tile(a): E.activation(
            out=d, in_=s, func=AF.Square), S["sq"])
        sq_hist.append(S["sq"].n)
        rec.wait("tensor", S["sq"], sq_hist[a])
        rec.do("tensor", lambda E, s=src_tile(a), a=a, n=n_tiles: E.matmul(
            B["pSX"], B["ones_x"], s, start=(a == 0), stop=(a == n - 1)))
        rec.do("tensor", lambda E, d=sq_tile(a), a=a, n=n_tiles: E.matmul(
            B["pSQ"], B["ones_sq"], d, start=(a == 0), stop=(a == n - 1)),
            S["st"])
        st_t.append(S["st"].n)
    rec.wait("vector", S["st"])
    rec.dos("vector", lambda E: E.tensor_scalar(
        row_m[:], B["pSX"], 1.0 / denom, None, ALU.mult), S["ch"])
    rec.dos("vector", lambda E: E.tensor_scalar(
        row_v[:], B["pSQ"], 1.0 / denom, None, ALU.mult), S["ch"])
    rec.dos("vector", lambda E: E.tensor_scalar(
        row_v[:], row_v[:], EPS, None, ALU.add), S["ch"])
    rec.dos("vector", lambda E: E.tensor_tensor(
        row_mr[:], row_m[:], row_m[:], ALU.mult), S["ch"])
    rec.do("vector", lambda E: E.tensor_tensor(
        row_v[:], row_v[:], row_mr[:], ALU.subtract), S["dv"])
    rec.wait("scalar", S["dv"])
    rec.do("scalar", lambda E: E.sqrt(out=row_sd[:], in_=row_v[:]), S["sd"])
    rec.wait("vector", S["sd"])
    rec.dos("vector", lambda E: E.reciprocal(row_r[:], row_sd[:]), S["ch"])
    rec.do("vector", lambda E: E.tensor_tensor(
        row_mr[:], row_m[:], row_r[:], ALU.mult), S["dv"])
    rec.wait("tensor", S["dv"])
    rec.do("tensor", lambda E: E.matmul(
        B["pRb"], B["ones_row"], row_r[:], start=True, stop=True))
    rec.do("tensor", lambda E: E.matmul(
        B["pMrb"], B["ones_row"], row_mr[:], start=True, stop=True), S["bc"])
    rec.wait("vector", S["bc"])
    for a in range(n_tiles):
        apply_emit(rec, a, B["pRb"], B["pMrb"])


def build(dbg=()):
    dbg = set(dbg)
    nc = bass.Bass("TRN2", target_bir_lowering=False)
    d = {}

    def din(name, shape, dt):
        d[name] = nc.dram_tensor(name, list(shape), dt, kind="ExternalInput")

    din("xT", (C, TOK), F32R)
    din("cid", (1, 1), I32)
    din("waT", (8, 16, 128, 128), F32R)
    din("bA_r", (128, 8), F32)
    din("wbT", (8, 128, DHR), F32)
    din("bB_c", (DHR, 1), F32)
    din("cosk", (DHR, TOK), BF16)
    din("sink", (DHR, TOK), BF16)
    din("wuq", (4, 128, 256), F32)
    din("wuk", (4, 128, 256), F32)
    din("wuv", (4, 128, 256), F32)
    din("wcc", (8, 128, 128), F32)
    din("bC_c", (128, 1), F32)
    din("cosq", (16, 128, TB), BF16)
    din("sinq", (16, 128, TB), BF16)
    din("RT", (128, 128), BF16)
    din("eye", (128, 128), BF16)
    din("maskd", (128, 4 * TB), BF16)
    din("ones_bf", (128, 128), BF16)
    din("ones_r", (128, 128), F32R)
    din("wzT", (16, 16, 128, 128), F32R)
    din("bZ_r", (128, 16), F32)
    din("gwT", (128, 16 * NE), F32R)
    din("sgwT", (128, 16), F32R)
    din("sguT", (88, 16, 128, 128), F32R)
    din("sdwT", (4, 44, 4, 128, 128), F32R)
    din("w13T", (NE, 22, 16, 128, 128), F32R)
    din("w2T", (NE, 4, 11, 4, 128, 128), F32R)

    yT = nc.dram_tensor("yT", [C, TOK], F32, kind="ExternalOutput")

    hG_in = nc.dram_tensor("hG_in", [1088, TOK], BF16)
    hG_out = nc.dram_tensor("hG_out", [NCORE * 1088, TOK], BF16)
    oG_in = nc.dram_tensor("oG_in", [HPC * DH, T], BF16)
    oG_out = nc.dram_tensor("oG_out", [NH * DH, T], BF16)
    shs = nc.dram_tensor("shs", [44, 128, TB], BF16)
    sguB = nc.dram_tensor("sguB", [88, 16, 128, 128], BF16)
    sdwB = nc.dram_tensor("sdwB", [4, 44, 4, 128, 128], BF16)
    w13B = nc.dram_tensor("w13B", [NE, 22, 16, 128, 128], BF16)
    w2B = nc.dram_tensor("w2B", [NE, 4, 11, 4, 128, 128], BF16)
    RG = [list(range(NCORE))]

    dbg_t = {}

    def dbg_out(name, shape, dt):
        if name in dbg:
            dbg_t[name] = nc.dram_tensor("dbg_" + name, list(shape), dt,
                                         kind="ExternalOutput")
            return dbg_t[name]
        return None

    with ExitStack() as TOP:
        e = TOP.enter_context
        e(nc.allow_low_precision(
            reason="deliberate f32r/bf16 intermediates; fp32 PSUM accumulation"))
        _semctr = [0]
        _sem_free = []

        def mksem(nm=None):
            if nm is None and _sem_free:
                return _sem_free.pop()
            _semctr[0] += 1
            return Sem(e(nc.semaphore(nm or f"s{_semctr[0]}")))

        def release(*items):
            for it in items:
                if isinstance(it, RingSem):
                    _sem_free.extend(it.sems)
                elif isinstance(it, dict):
                    _sem_free.extend(it.values())
                elif isinstance(it, Sem):
                    _sem_free.append(it)

        bar = mksem("bar")
        bar_k = [0]

        def barrier(block):
            bar_k[0] += 1
            tgt = 5 * bar_k[0]
            for reg in (block.sync, block.tensor, block.scalar, block.vector,
                        block.gpsimd):
                def f(eng, tgt=tgt):
                    eng.sem_inc(bar.h, 1)
                    eng.wait_ge(bar.h, tgt)
                reg(f)
            bar.n = tgt

        ones_bf = e(nc.sbuf_tensor("onesbf", [128, 128], BF16))
        ones_r = e(nc.sbuf_tensor("onesr", [128, 128], F32R))
        rt_sb = e(nc.sbuf_tensor("rtsb", [128, 128], BF16))
        eye_sb = e(nc.sbuf_tensor("eyesb", [128, 128], BF16))
        bA_sb = e(nc.sbuf_tensor("bAsb", [128, 8], F32))
        bB_sb = e(nc.sbuf_tensor("bBsb", [DHR, 1], F32))
        bC_sb = e(nc.sbuf_tensor("bCsb", [128, 1], F32))
        bZ_sb = e(nc.sbuf_tensor("bZsb", [128, 16], F32))
        gw_sb = e(nc.sbuf_tensor("gwsb", [128, 16 * NE], BF16))
        sgw_sb = e(nc.sbuf_tensor("sgwsb", [128, 16], BF16))
        cid_sb = e(nc.sbuf_tensor("cidsb", [1, 1], I32))

        row_m = e(nc.sbuf_tensor("rowm", [1, TB], F32))
        row_v = e(nc.sbuf_tensor("rowv", [1, TB], F32))
        row_sd = e(nc.sbuf_tensor("rowsd", [1, TB], F32))
        row_r = e(nc.sbuf_tensor("rowr", [1, TB], F32R))
        row_mr = e(nc.sbuf_tensor("rowmr", [1, TB], F32R))
        ROWS = (row_m, row_v, row_sd, row_r, row_mr)

        block = e(nc.Block())

        s_ld = mksem("ld")
        s_cv = mksem("cv")
        with ExitStack() as st0:
            e0 = st0.enter_context
            gw_f = e0(nc.sbuf_tensor("gwf", [128, 16 * NE], F32R))
            sgw_f = e0(nc.sbuf_tensor("sgwf", [128, 16], F32R))
            rec = Rec(block)
            for nm, t in [("ones_bf", ones_bf), ("ones_r", ones_r),
                          ("RT", rt_sb), ("eye", eye_sb),
                          ("bA_r", bA_sb), ("bB_c", bB_sb), ("bC_c", bC_sb),
                          ("bZ_r", bZ_sb), ("cid", cid_sb)]:
                rec.do("sync", lambda E, t=t, nm=nm: E.dma_start(
                    out=t[:], in_=d[nm].ap()), s_ld, 16)
            for nm, t in [("gwT", gw_f), ("sgwT", sgw_f)]:
                rec.do("sync", lambda E, t=t, nm=nm: E.dma_start(
                    out=t[:], in_=d[nm].ap()), s_ld, 16)
            rec.wait("vector", s_ld)
            rec.do("vector", lambda E: E.tensor_copy(gw_sb[:], gw_f[:]))
            rec.do("vector", lambda E: E.tensor_copy(sgw_sb[:], sgw_f[:]), s_cv)
            rec.play()
            barrier(block)
        LDC = s_ld.n

        s_cc1 = mksem("cc1")
        with ExitStack() as st1:
            e1 = st1.enter_context
            xsb = e1(nc.sbuf_tensor("p1x", [128, 16 * TB], F32R))
            sqb = e1(nc.sbuf_tensor("p1sqb", [128, 4 * TB], BF16))
            tmpb = e1(nc.sbuf_tensor("p1tmp", [128, 2 * TB], F32R))
            xn1 = e1(nc.sbuf_tensor("p1xn1", [128, 16 * TB], F32R))
            hsb = e1(nc.sbuf_tensor("p1h", [128, 8 * TOK], BF16))
            krraw = e1(nc.sbuf_tensor("p1kraw", [DHR, TB], BF16))
            krt1 = e1(nc.sbuf_tensor("p1krt1", [DHR, TB], BF16))
            krt2 = e1(nc.sbuf_tensor("p1krt2", [DHR, TB], BF16))
            hkr = e1(nc.sbuf_tensor("p1hkr", [DHR, TOK], BF16))
            wr1 = e1(nc.sbuf_tensor("p1wr", [128, 48 * 128], F32R))
            cosk_sb = e1(nc.sbuf_tensor("cosksb", [DHR, TOK], BF16))
            sink_sb = e1(nc.sbuf_tensor("sinksb", [DHR, TOK], BF16))
            wb_f = e1(nc.sbuf_tensor("wbf", [128, 8 * DHR], F32))
            wb_bf = e1(nc.sbuf_tensor("wbbf", [128, 8 * DHR], BF16))
            pH = [e1(nc.psum_tensor(f"p1ph{i}", [128, TB], F32)) for i in range(2)]
            pSX = e1(nc.psum_tensor("p1psx", [1, TB], F32))
            pSQ = e1(nc.psum_tensor("p1psq", [1, TB], F32))
            pRbA = e1(nc.psum_tensor("p1prb", [128, TB], F32))
            pMrbA = e1(nc.psum_tensor("p1pmrb", [128, TB], F32))
            pKR = e1(nc.psum_tensor("p1pkr", [DHR, TB], F32))
            pRot = e1(nc.psum_tensor("p1prot", [DHR, TB], F32))

            sS = {k: mksem() for k in ("sq", "st", "dv", "sd", "bc", "ch")}
            s_x = mksem()
            rs_wld = RingSem(mksem, 3)
            s_xn = mksem()
            s_hmm = mksem()
            s_hev = mksem()
            s_kr = mksem()
            s_hgw = mksem()

            rec = Rec(block)
            s_ld1 = mksem()
            for eng in ("tensor", "scalar", "vector"):
                rec.wait(eng, s_ld, LDC)
            for nm, t in [("cosk", cosk_sb), ("sink", sink_sb)]:
                rec.do("sync", lambda E, t=t, nm=nm: E.dma_start(
                    out=t[:], in_=d[nm].ap()), s_ld1, 16)
            rec.do("sync", lambda E: dma_tiles(E, wb_f[:], d["wbT"].ap(), 8),
                   s_ld1, 16)
            rec.wait("vector", s_ld1)
            rec.do("vector", lambda E: E.tensor_copy(wb_bf[:], wb_f[:]), s_cv)
            rec.wait("tensor", s_cv)
            ev_hist = []
            hmm_hist = []
            ch_all = {}

            def load_wa(cidx, war_tgt=None):
                slot0 = (cidx % 3) * 16
                j = cidx % 8
                if war_tgt is not None:
                    rec.wait("sync", s_hmm, war_tgt)
                sem = rs_wld.next()
                tgt = rec.do("sync", lambda E, j=j, s0=slot0: dma_tiles(
                    E, wr1[:, s0 * 128:(s0 + 16) * 128],
                    d["waT"].ap()[j], 16), sem, 16)
                ch_all[cidx] = (slot0, sem, tgt)

            for cidx in range(3):
                load_wa(cidx)
            for tb in range(NTB):
                if tb > 0:
                    rec.wait("sync", s_xn)
                    rec.wait("sync", sS["st"])
                rec.do("sync", lambda E, tb=tb: dma_cols(
                    E, xsb[:], d["xT"].ap()[:, tb * TB:(tb + 1) * TB]),
                    s_x, 16)
                rec.wait("scalar", s_x)
                rec.wait("tensor", s_x)
                if tb > 0:
                    rec.wait("tensor", s_xn)
                    rec.wait("vector", s_hmm)

                def apply1(rec, a, pRb, pMrb):
                    rec.dos("vector", lambda E, a=a: E.tensor_tensor(
                        tmpb[:, (a % 2) * TB:(a % 2 + 1) * TB],
                        xsb[:, a * TB:(a + 1) * TB], pRb, ALU.mult), sS["ch"])
                    rec.do("vector", lambda E, a=a: E.tensor_tensor(
                        xn1[:, a * TB:(a + 1) * TB],
                        tmpb[:, (a % 2) * TB:(a % 2 + 1) * TB], pMrb,
                        ALU.subtract), s_xn)

                emit_ln(rec, sS,
                        dict(ones_x=ones_r[:, 0:1], ones_sq=ones_bf[:, 0:1],
                             ones_row=ones_r[0:1, :],
                             pSX=pSX[0:1, :], pSQ=pSQ[0:1, :],
                             pRb=pRbA[:], pMrb=pMrbA[:], rows=ROWS),
                        lambda a: xsb[:, a * TB:(a + 1) * TB],
                        lambda a: sqb[:, (a % 4) * TB:(a % 4 + 1) * TB],
                        apply1)
                rec.wait("tensor", s_xn)
                for j in range(8):
                    cidx = tb * 8 + j
                    slot0, wsem, tgt = ch_all[cidx]
                    rec.wait("tensor", wsem, tgt)
                    if cidx >= 2:
                        rec.wait("tensor", s_hev, ev_hist[cidx - 2])
                    for ki in range(16):
                        rec.do("tensor", lambda E, s0=slot0, ki=ki, j=j:
                               E.matmul(
                            pH[j % 2][:],
                            wr1[:, (s0 + ki) * 128:(s0 + ki + 1) * 128],
                            xn1[:, ki * TB:(ki + 1) * TB],
                            start=(ki == 0), stop=(ki == 15)),
                            s_hmm if ki == 15 else None)
                    hmm_hist.append(s_hmm.n)
                    if cidx + 3 < 16:
                        load_wa(cidx + 3, war_tgt=s_hmm.n)
                    rec.wait("scalar", s_hmm)
                    rec.do("scalar", lambda E, j=j, tb=tb: E.activation(
                        out=hsb[:, j * TOK + tb * TB:j * TOK + (tb + 1) * TB],
                        in_=pH[j % 2][:], func=AF.Identity,
                        bias=bA_sb[:, j:j + 1]), s_hev)
                    ev_hist.append(s_hev.n)
                rec.wait("tensor", s_hev)
                if tb > 0:
                    rec.wait("tensor", s_kr)
                for ki in range(8):
                    rec.do("tensor", lambda E, ki=ki, tb=tb: E.matmul(
                        pKR[:], wb_bf[:, ki * DHR:(ki + 1) * DHR],
                        hsb[:, ki * TOK + tb * TB:ki * TOK + (tb + 1) * TB],
                        start=(ki == 0), stop=(ki == 7)),
                        s_kr if ki == 7 else None)
                rec.wait("scalar", s_kr)
                rec.do("scalar", lambda E: E.activation(
                    out=krraw[:], in_=pKR[:], func=AF.Identity, bias=bB_sb[:]),
                    s_kr)
                rec.wait("tensor", s_kr)
                rec.do("tensor", lambda E: E.matmul(
                    pRot[:], rt_sb[0:DHR, 0:DHR], krraw[:],
                    start=True, stop=True), s_kr)
                rec.wait("vector", s_kr)
                rec.dos("vector", lambda E, tb=tb: E.tensor_tensor(
                    krt1[:], krraw[:], cosk_sb[:, tb * TB:(tb + 1) * TB],
                    ALU.mult), sS["ch"])
                rec.dos("vector", lambda E, tb=tb: E.tensor_tensor(
                    krt2[:], pRot[:], sink_sb[:, tb * TB:(tb + 1) * TB],
                    ALU.mult), sS["ch"])
                rec.do("vector", lambda E, tb=tb: E.tensor_tensor(
                    hkr[:, tb * TB:(tb + 1) * TB], krt1[:], krt2[:], ALU.add),
                    s_kr)
            for j in range(8):
                rec.wait("gpsimd", s_hev, ev_hist[8 + j])
                rec.do("gpsimd", lambda E, j=j: E.dma_start(
                    out=hG_in.ap()[j * 128:(j + 1) * 128, :],
                    in_=hsb[:, j * TOK:(j + 1) * TOK]), s_hgw, 16)
            rec.wait("gpsimd", s_kr)
            rec.do("gpsimd", lambda E: E.dma_start(
                out=hG_in.ap()[1024:1088, :], in_=hkr[:]), s_hgw, 16)
            rec.wait("gpsimd", s_hgw)
            rec.do("gpsimd", lambda E: E.collective_compute(
                "AllGather", ALU.bypass, replica_groups=RG,
                ins=[hG_in.ap().opt()], outs=[hG_out.ap().opt()]), s_cc1, 1)
            t = dbg_out("hg", (NCORE * 1088, TOK), BF16)
            if t is not None:
                rec.wait("gpsimd", s_cc1, 1)
                rec.do("gpsimd", lambda E: E.dma_start(
                    out=t.ap(), in_=hG_out.ap()), mksem(), 16)
            rec.play()
            barrier(block)
            release(sS, s_x, rs_wld, s_xn, s_hmm, s_hev, s_kr, s_hgw, s_ld1)

        s_cc2 = mksem("cc2")
        with ExitStack() as stA:
            eA = stA.enter_context
            kf0 = eA(nc.sbuf_tensor("kf0", [128, HPC * T], BF16))
            kf1 = eA(nc.sbuf_tensor("kf1", [65, T], BF16))
            qf0 = eA(nc.sbuf_tensor("qf0", [128, HPC * T], BF16))
            qf1 = [eA(nc.sbuf_tensor(f"qf1_{h}", [65, T], BF16))
                   for h in range(HPC)]
            vf = eA(nc.sbuf_tensor("vf", [128, (T // 128) * 256], BF16))
            kmax_sb = eA(nc.sbuf_tensor("kmax", [1, 4], F32))
            qn_row = eA(nc.sbuf_tensor("qnrow", [1, TB], F32))

            wuq_bf = eA(nc.sbuf_tensor("wuqbf", [128, 4 * 256], BF16))
            wuk_bf = eA(nc.sbuf_tensor("wukbf", [128, 4 * 256], BF16))
            wuv_bf = eA(nc.sbuf_tensor("wuvbf", [128, 4 * 256], BF16))
            wcc_bf = eA(nc.sbuf_tensor("wccbf", [128, 8 * 128], BF16))
            with ExitStack() as st2:
                e2 = st2.enter_context
                stage = e2(nc.sbuf_tensor("p2stage", [128, 1024], F32))
                hgr = e2(nc.sbuf_tensor("hgr", [128, 16 * TB], BF16))
                trr = e2(nc.sbuf_tensor("trr", [128, 4 * TB], BF16))
                qrw = e2(nc.sbuf_tensor("qrw", [128, 2 * TB], BF16))
                rr1 = e2(nc.sbuf_tensor("rr1", [128, 2 * TB], BF16))
                rr2 = e2(nc.sbuf_tensor("rr2", [128, 2 * TB], BF16))
                nsq = e2(nc.sbuf_tensor("nsq", [128, 2 * TB], BF16))
                nsq2 = e2(nc.sbuf_tensor("nsq2", [64, 2 * TB], BF16))
                pP = [e2(nc.psum_tensor(f"p2p{i}", [128, TB], F32))
                      for i in range(4)]
                pR2 = e2(nc.psum_tensor("p2r", [128, TB], F32))
                pRot2 = e2(nc.psum_tensor("p2rot", [128, TB], F32))
                pV = [e2(nc.psum_tensor(f"p2v{i}", [128, 256], F32))
                      for i in range(2)]

                rs_hr = RingSem(mksem, 2)
                s_hfr = mksem()
                rs_tr = RingSem(mksem, 2)
                s_k1 = mksem()
                s_mm = mksem()
                s_ev = mksem()
                s_rp = mksem()
                s_nm = mksem()
                s_ns = mksem()
                s_nx = mksem()

                rec = Rec(block)
                s_st2 = mksem()
                s_ld2 = mksem()
                for nm, t, aa in [("wuq", wuq_bf, 4), ("wuk", wuk_bf, 4),
                                  ("wuv", wuv_bf, 4), ("wcc", wcc_bf, 8)]:
                    rec.wait("sync", s_st2)
                    rec.do("sync", lambda E, nm=nm, aa=aa: dma_tiles(
                        E, stage[:], d[nm].ap(), aa), s_ld2, 16)
                    rec.wait("vector", s_ld2)
                    rec.do("vector", lambda E, t=t: E.tensor_copy(
                        t[:], stage[:]), s_st2)
                rec.wait("tensor", s_st2)
                rec.wait("sync", s_cc1, 1)
                rec.wait("gpsimd", s_cc1, 1)
                for r in range(8):
                    rec.do("gpsimd", lambda E, r=r: E.dma_start(
                        out=kf1[0:64, r * TOK:(r + 1) * TOK],
                        in_=hG_out.ap()[1088 * r + 1024:1088 * r + 1088, :]),
                        s_k1, 16)
                rec.do("vector", lambda E: E.memset(kf1[64:65, :], 1.0),
                       s_k1, 1)

                evq = [0, 0]
                evk = [0, 0]
                evr = [0]
                evv = [0, 0]
                rpd = [0, 0]
                hfr_hist = [0]
                for tbi in range(16):
                    r, co = tbi // 2, (tbi % 2) * TB
                    hs = (tbi % 2) * 8
                    par = tbi % 2
                    if tbi >= 2:
                        rec.wait("sync", s_hfr, hfr_hist[tbi - 1])
                    hrsem = rs_hr.next()
                    for l in range(8):
                        rec.do("sync", lambda E, r=r, co=co, l=l, hs=hs:
                               E.dma_start(
                            out=hgr[:, (hs + l) * TB:(hs + l + 1) * TB],
                            in_=hG_out.ap()[1088 * r + 128 * l:
                                            1088 * r + 128 * (l + 1),
                                            co:co + TB]), hrsem, 16)
                    trsem = rs_tr.next()
                    rec.do("sync", lambda E, tbi=tbi, par=par: E.dma_start(
                        out=trr[:, par * 2 * TB:(par * 2 + 1) * TB],
                        in_=d["cosq"].ap()[tbi]), trsem, 16)
                    rec.do("sync", lambda E, tbi=tbi, par=par: E.dma_start(
                        out=trr[:, (par * 2 + 1) * TB:(par * 2 + 2) * TB],
                        in_=d["sinq"].ap()[tbi]), trsem, 16)
                    rec.wait("tensor", hrsem)
                    for h in range(HPC):
                        rec.wait("tensor", s_ev, evq[h])
                        for li in range(4):
                            rec.do("tensor", lambda E, h=h, li=li, hs=hs:
                                   E.matmul(
                                pP[h][:],
                                wuq_bf[:, li * 256 + h * 128:
                                       li * 256 + (h + 1) * 128],
                                hgr[:, (hs + 4 + li) * TB:(hs + 5 + li) * TB],
                                start=(li == 0), stop=(li == 3)),
                                s_mm if li == 3 else None)
                    for h in range(HPC):
                        rec.wait("tensor", s_ev, evk[h])
                        for li in range(4):
                            rec.do("tensor", lambda E, h=h, li=li, hs=hs:
                                   E.matmul(
                                pP[2 + h][:],
                                wuk_bf[:, li * 256 + h * 128:
                                       li * 256 + (h + 1) * 128],
                                hgr[:, (hs + li) * TB:(hs + li + 1) * TB],
                                start=(li == 0), stop=(li == 3)),
                                s_mm if li == 3 else None)
                    mm_q = s_mm.n
                    rec.wait("tensor", s_ev, evr[0])
                    order = [4, 5, 6, 7, 0, 1, 2, 3]
                    for idx, ki in enumerate(order):
                        free = ki >= 4
                        last = idx == 7
                        rec.do("tensor", lambda E, ki=ki, hs=hs, idx=idx:
                               E.matmul(
                            pR2[:], wcc_bf[:, ki * 128:(ki + 1) * 128],
                            hgr[:, (hs + ki) * TB:(hs + ki + 1) * TB],
                            start=(idx == 0), stop=(idx == 7)),
                            s_hfr if free else (s_mm if last else None))
                    mm_qr = s_mm.n
                    rec.wait("scalar", s_mm, mm_q)
                    for h in range(HPC):
                        rec.do("scalar", lambda E, h=h, tbi=tbi: E.activation(
                            out=qf0[:, h * T + tbi * TB:h * T + (tbi + 1) * TB],
                            in_=pP[h][:], func=AF.Copy, scale=SCALE), s_ev)
                        evq[h] = s_ev.n
                        rec.do("scalar", lambda E, h=h, tbi=tbi: E.activation(
                            out=kf0[:, h * T + tbi * TB:h * T + (tbi + 1) * TB],
                            in_=pP[2 + h][:], func=AF.Copy), s_ev)
                        evk[h] = s_ev.n
                    rec.wait("scalar", s_mm, mm_qr)
                    rec.wait("scalar", s_rp, rpd[par])
                    rec.do("scalar", lambda E, par=par: E.activation(
                        out=qrw[:, par * TB:(par + 1) * TB],
                        in_=pR2[:], func=AF.Identity, bias=bC_sb[:]), s_ev)
                    evr[0] = s_ev.n
                    ev_qr = s_ev.n
                    rec.wait("tensor", s_ev, ev_qr)
                    rec.wait("tensor", s_rp, rpd[par])
                    rec.do("tensor", lambda E, par=par: E.matmul(
                        pRot2[:], rt_sb[:], qrw[:, par * TB:(par + 1) * TB],
                        start=True, stop=True), s_mm)
                    mm_rot = s_mm.n
                    for tt in range(4):
                        rec.wait("tensor", s_ev, evv[tt % 2])
                        for li in range(4):
                            sem = None
                            if tt == 3:
                                sem = s_hfr
                            elif li == 3:
                                sem = s_mm
                            rec.do("tensor", lambda E, tt=tt, li=li, hs=hs:
                                   E.matmul(
                                pV[tt % 2][:],
                                hgr[:, (hs + li) * TB + tt * 128:
                                       (hs + li) * TB + (tt + 1) * 128],
                                wuv_bf[:, li * 256:(li + 1) * 256],
                                start=(li == 0), stop=(li == 3)), sem)
                        vt = s_mm.n if tt < 3 else s_hfr.n
                        vs = s_mm if tt < 3 else s_hfr
                        rec.wait("scalar", vs, vt)
                        rec.do("scalar", lambda E, tbi=tbi, tt=tt: E.activation(
                            out=vf[:, (tbi * 4 + tt) * 256:
                                   (tbi * 4 + tt + 1) * 256],
                            in_=pV[tt % 2][:], func=AF.Copy), s_ev)
                        evv[tt % 2] = s_ev.n
                    hfr_hist.append(s_hfr.n)
                    rec.wait("vector", trsem)
                    rec.wait("vector", s_mm, mm_rot)
                    rec.dos("vector", lambda E, par=par: E.tensor_tensor(
                        rr1[:, par * TB:(par + 1) * TB],
                        qrw[:, par * TB:(par + 1) * TB],
                        trr[:, par * 2 * TB:(par * 2 + 1) * TB], ALU.mult),
                        s_rp)
                    rec.dos("vector", lambda E, par=par: E.tensor_tensor(
                        rr2[:, par * TB:(par + 1) * TB], pRot2[:],
                        trr[:, (par * 2 + 1) * TB:(par * 2 + 2) * TB],
                        ALU.mult), s_rp)
                    rec.do("vector", lambda E, par=par, tbi=tbi:
                           E.tensor_tensor(
                        qf1[0][0:64, tbi * TB:(tbi + 1) * TB],
                        rr1[0:64, par * TB:(par + 1) * TB],
                        rr2[0:64, par * TB:(par + 1) * TB], ALU.add))
                    rec.do("vector", lambda E, par=par, tbi=tbi:
                           E.tensor_tensor(
                        qf1[1][0:64, tbi * TB:(tbi + 1) * TB],
                        rr1[64:128, par * TB:(par + 1) * TB],
                        rr2[64:128, par * TB:(par + 1) * TB], ALU.add), s_rp)
                    rpd[par] = s_rp.n

                rec.wait("scalar", s_ev)
                rec.wait("scalar", s_rp)
                rec.wait("scalar", s_k1)
                nm_hist = []
                for h in range(HPC):
                    for tbi in range(16):
                        seq = h * 16 + tbi
                        par = seq % 2
                        if seq >= 2:
                            rec.wait("scalar", s_nm, nm_hist[seq - 2])
                        rec.do("scalar", lambda E, h=h, tbi=tbi, par=par:
                               E.activation(
                            out=nsq[:, par * TB:(par + 1) * TB],
                            in_=kf0[:, h * T + tbi * TB:h * T + (tbi + 1) * TB],
                            func=AF.Square))
                        rec.do("scalar", lambda E, tbi=tbi, par=par:
                               E.activation(
                            out=nsq2[:, par * TB:(par + 1) * TB],
                            in_=kf1[0:64, tbi * TB:(tbi + 1) * TB],
                            func=AF.Square), s_ns)
                        rec.wait("tensor", s_ns)
                        if seq >= 2:
                            rec.wait("tensor", s_nx, seq - 1)
                        rec.do("tensor", lambda E, par=par: E.matmul(
                            pP[par][0:1, :], ones_bf[:, 0:1],
                            nsq[:, par * TB:(par + 1) * TB],
                            start=True, stop=False))
                        rec.do("tensor", lambda E, par=par: E.matmul(
                            pP[par][0:1, :], ones_bf[0:64, 0:1],
                            nsq2[:, par * TB:(par + 1) * TB],
                            start=False, stop=True), s_nm)
                        nm_hist.append(s_nm.n)
                        rec.wait("vector", s_nm)
                        rec.dos("vector", lambda E, par=par: E.reduce_max(
                            row_m[0:1, 0:1], pP[par][0:1, :], axis=AXX), s_nx)
                        if tbi == 0:
                            rec.dos("vector", lambda E, h=h: E.tensor_copy(
                                kmax_sb[0:1, 2 + h:3 + h], row_m[0:1, 0:1]),
                                s_nx)
                        else:
                            rec.dos("vector", lambda E, h=h: E.tensor_tensor(
                                kmax_sb[0:1, 2 + h:3 + h],
                                kmax_sb[0:1, 2 + h:3 + h],
                                row_m[0:1, 0:1], ALU.max), s_nx)
                rec.wait("scalar", s_nx)
                rec.do("scalar", lambda E: E.sqrt(
                    out=kmax_sb[0:1, 0:2], in_=kmax_sb[0:1, 2:4]), s_ns)
                km_t = s_ns.n
                for h in range(HPC):
                    for tbi in range(16):
                        seq = h * 16 + tbi
                        par = seq % 2
                        if seq >= 2:
                            rec.wait("scalar", s_nm)
                        rec.do("scalar", lambda E, h=h, tbi=tbi, par=par:
                               E.activation(
                            out=nsq[:, par * TB:(par + 1) * TB],
                            in_=qf0[:, h * T + tbi * TB:h * T + (tbi + 1) * TB],
                            func=AF.Square))
                        rec.do("scalar", lambda E, h=h, tbi=tbi, par=par:
                               E.activation(
                            out=nsq2[:, par * TB:(par + 1) * TB],
                            in_=qf1[h][0:64, tbi * TB:(tbi + 1) * TB],
                            func=AF.Square), s_ns)
                        rec.wait("tensor", s_ns)
                        if seq >= 1:
                            rec.wait("tensor", s_ns, km_t + 2 * seq)
                        rec.do("tensor", lambda E: E.matmul(
                            pP[2][0:1, :], ones_bf[:, 0:1],
                            nsq[:, (seq % 2) * TB:(seq % 2 + 1) * TB],
                            start=True, stop=False))
                        rec.do("tensor", lambda E: E.matmul(
                            pP[2][0:1, :], ones_bf[0:64, 0:1],
                            nsq2[:, (seq % 2) * TB:(seq % 2 + 1) * TB],
                            start=False, stop=True), s_nm)
                        rec.wait("scalar", s_nm)
                        rec.wait("scalar", s_nx)
                        rec.do("scalar", lambda E: E.sqrt(
                            out=qn_row[:], in_=pP[2][0:1, :]), s_ns)
                        rec.wait("vector", s_ns)
                        rec.do("vector", lambda E, h=h, tbi=tbi:
                               E.tensor_scalar(
                            qf1[h][64:65, tbi * TB:(tbi + 1) * TB],
                            qn_row[:], kmax_sb[0:1, h:h + 1], CMARGIN,
                            ALU.mult, ALU.mult), s_nx)
                rec.play()

            barrier(block)
            release(rs_hr, s_hfr, rs_tr, s_k1, s_mm, s_ev, s_rp, s_nm, s_ns,
                    s_nx, s_st2, s_ld2)

            with ExitStack() as st3:
                e3 = st3.enter_context
                cvout = e3(nc.sbuf_tensor("cvout", [128, 4 * 2048], BF16))
                maskd_sb = e3(nc.sbuf_tensor("masksb", [128, 4 * TB], BF16))
                ebuf = e3(nc.sbuf_tensor("ebuf", [128, 2 * TB], BF16))
                osb = e3(nc.sbuf_tensor("osb", [128, 2 * TB], BF16))
                otile = e3(nc.sbuf_tensor("otile", [128, 2 * TB], BF16))
                dsb = e3(nc.sbuf_tensor("dsb", [1, 2 * TB], F32))
                rsb = e3(nc.sbuf_tensor("rsb", [1, 2 * TB], F32R))
                pS = [e3(nc.psum_tensor(f"ps{i}", [128, TB], F32))
                      for i in range(2)]
                pO = [e3(nc.psum_tensor(f"po{i}", [128, TB], F32))
                      for i in range(2)]
                pDen = [e3(nc.psum_tensor(f"pdn{i}", [1, TB], F32))
                        for i in range(2)]
                pRb2 = [e3(nc.psum_tensor(f"prb2{i}", [128, TB], F32))
                        for i in range(2)]

                s_s = mksem()
                s_e = mksem()
                s_pv = mksem()
                s_dn = mksem()
                s_rc = mksem()
                s_rb = mksem()
                s_oe = mksem()
                s_oo = mksem()
                s_ow = mksem()
                rs_ow = RingSem(mksem, 2)
                ow_t = {}

                rec = Rec(block)
                s_mk = mksem()
                rec.do("sync", lambda E: E.dma_start(
                    out=maskd_sb[:], in_=d["maskd"].ap()), s_mk, 16)
                rec.wait("vector", s_mk)
                cv_jobs = []
                for j in range(88):
                    cv_jobs.append((d["sguT"].ap()[j], sguB.ap()[j], 16))
                for fq in range(4):
                    for i in range(44):
                        cv_jobs.append((d["sdwT"].ap()[fq, i],
                                        sdwB.ap()[fq, i], 4))
                for en in range(NE):
                    for j in range(22):
                        cv_jobs.append((d["w13T"].ap()[en, j],
                                        w13B.ap()[en, j], 16))
                    for fq in range(4):
                        for i in range(11):
                            cv_jobs.append((d["w2T"].ap()[en, fq, i],
                                            w2B.ap()[en, fq, i], 4))
                rs_cvi = RingSem(mksem, 4)
                rs_cvo = RingSem(mksem, 4)
                cv_state = {"i": 0, "out": [None] * 4}

                def emit_cv(njobs):
                    for _ in range(njobs):
                        ci = cv_state["i"]
                        if ci >= len(cv_jobs):
                            return
                        cv_state["i"] += 1
                        srcap, dstap, nt = cv_jobs[ci]
                        par = ci % 4
                        w = nt * 128
                        isem = rs_cvi.next()
                        if cv_state["out"][par] is not None:
                            osem, otgt = cv_state["out"][par]
                            rec.wait("gpsimd", osem, otgt)
                        itgt = rec.do("gpsimd", lambda E, s=srcap, par=par,
                                      w=w, nt=nt: dma_tiles(
                            E, cvout[:, par * 2048:par * 2048 + w], s, nt),
                            isem, 16)
                        rec.wait("sync", isem, itgt)
                        osem = rs_cvo.next()
                        otgt = rec.do("sync", lambda E, dd=dstap, par=par,
                                      w=w, nt=nt: E.dma_start(
                            out=dd.rearrange("a p f -> p a f"),
                            in_=cvout[:, par * 2048:par * 2048 + w]
                            .rearrange("p (a f) -> p a f", a=nt)), osem, 16)
                        cv_state["out"][par] = (osem, otgt)

                pairs = [(h, qb) for h in range(HPC) for qb in range(NQB)]
                e_t = {}
                pv_t = {}
                oe_t = {}
                oo_t = {}
                rc_t = {}
                rb_t = {}
                dn_t = {}
                g = 0
                for p, (h, qb) in enumerate(pairs):
                    emit_cv(25)
                    nk = 4 * qb + 4
                    par = p % 2
                    for kt in range(nk):
                        sl = g % 2
                        if g >= 2:
                            rec.wait("tensor", s_e, e_t[g - 2])
                        rec.do("tensor", lambda E, h=h, kt=kt, qb=qb, sl=sl:
                               E.matmul(
                            pS[sl][:],
                            kf0[:, h * T + kt * 128:h * T + (kt + 1) * 128],
                            qf0[:, h * T + qb * TB:h * T + (qb + 1) * TB],
                            start=True, stop=False))
                        rec.do("tensor", lambda E, h=h, kt=kt, sl=sl, qb=qb:
                               E.matmul(
                            pS[sl][:], kf1[:, kt * 128:(kt + 1) * 128],
                            qf1[h][:, qb * TB:(qb + 1) * TB],
                            start=False, stop=True), s_s)
                        rec.wait("scalar", s_s)
                        if g >= 2:
                            rec.wait("scalar", s_pv, pv_t[g - 2])
                        rec.do("scalar", lambda E, sl=sl: E.activation(
                            out=ebuf[:, sl * TB:(sl + 1) * TB],
                            in_=pS[sl][:], func=AF.Exp), s_e)
                        e_t[g] = s_e.n
                        if kt >= nk - 4:
                            rel = kt - (nk - 4)
                            rec.wait("vector", s_e, e_t[g])
                            rec.do("vector", lambda E, sl=sl, rel=rel:
                                   E.tensor_tensor(
                                ebuf[:, sl * TB:(sl + 1) * TB],
                                ebuf[:, sl * TB:(sl + 1) * TB],
                                maskd_sb[:, rel * TB:(rel + 1) * TB],
                                ALU.mult), s_e)
                            e_t[g] = s_e.n
                        rec.wait("tensor", s_e, e_t[g])
                        if kt == 0 and p >= 2:
                            rec.wait("tensor", s_oe, oe_t[p - 2])
                            rec.wait("tensor", s_dn, dn_t[p - 2])
                        rec.do("tensor", lambda E, kt=kt, h=h, sl=sl, nk=nk,
                               par=par: E.matmul(
                            pO[par][:],
                            vf[:, kt * 256 + h * 128:kt * 256 + (h + 1) * 128],
                            ebuf[:, sl * TB:(sl + 1) * TB],
                            start=(kt == 0), stop=(kt == nk - 1)))
                        rec.do("tensor", lambda E, kt=kt, sl=sl, nk=nk,
                               par=par: E.matmul(
                            pDen[par][0:1, :], ones_bf[:, 0:1],
                            ebuf[:, sl * TB:(sl + 1) * TB],
                            start=(kt == 0), stop=(kt == nk - 1)), s_pv)
                        pv_t[g] = s_pv.n
                        g += 1
                    rec.wait("scalar", s_pv, pv_t[g - 1])
                    if p >= 2:
                        rec.wait("scalar", s_rc, rc_t[p - 2])
                    rec.do("scalar", lambda E, par=par: E.activation(
                        out=dsb[0:1, par * TB:(par + 1) * TB],
                        in_=pDen[par][0:1, :], func=AF.Copy), s_dn)
                    dn_t[p] = s_dn.n
                    if p >= 2:
                        rec.wait("scalar", s_oo, oo_t[p - 2])
                    rec.do("scalar", lambda E, par=par: E.activation(
                        out=osb[:, par * TB:(par + 1) * TB],
                        in_=pO[par][:], func=AF.Copy), s_oe)
                    oe_t[p] = s_oe.n
                    rec.wait("vector", s_dn, dn_t[p])
                    if p >= 2:
                        rec.wait("vector", s_rb, rb_t[p - 2])
                    rec.do("vector", lambda E, par=par: E.reciprocal(
                        rsb[0:1, par * TB:(par + 1) * TB],
                        dsb[0:1, par * TB:(par + 1) * TB]), s_rc)
                    rc_t[p] = s_rc.n
                    rec.wait("tensor", s_rc, rc_t[p])
                    if p >= 2:
                        rec.wait("tensor", s_oo, oo_t[p - 2])
                    rec.do("tensor", lambda E, par=par: E.matmul(
                        pRb2[par][:], ones_r[0:1, :],
                        rsb[0:1, par * TB:(par + 1) * TB],
                        start=True, stop=True), s_rb)
                    rb_t[p] = s_rb.n
                    rec.wait("vector", s_oe, oe_t[p])
                    rec.wait("vector", s_rb, rb_t[p])
                    if p >= 2:
                        sem_prev, tgt_prev = ow_t[p - 2]
                        rec.wait("vector", sem_prev, tgt_prev)
                    rec.do("vector", lambda E, par=par: E.tensor_tensor(
                        otile[:, par * TB:(par + 1) * TB],
                        osb[:, par * TB:(par + 1) * TB],
                        pRb2[par][:], ALU.mult), s_oo)
                    oo_t[p] = s_oo.n
                    rec.wait("gpsimd", s_oo, oo_t[p])
                    owsem = rs_ow.next()
                    tgt = rec.do("gpsimd", lambda E, h=h, qb=qb, par=par:
                           E.dma_start(
                        out=oG_in.ap()[h * 128:(h + 1) * 128,
                                       qb * TB:(qb + 1) * TB],
                        in_=otile[:, par * TB:(par + 1) * TB]), owsem, 16)
                    ow_t[p] = (owsem, tgt)
                for sem in rs_ow.sems:
                    rec.wait("gpsimd", sem)
                rec.do("gpsimd", lambda E: E.collective_compute(
                    "AllGather", ALU.bypass, replica_groups=RG,
                    ins=[oG_in.ap().opt()], outs=[oG_out.ap().opt()]),
                    s_cc2, 1)
                t = dbg_out("og", (NH * DH, T), BF16)
                if t is not None:
                    rec.wait("gpsimd", s_cc2, 1)
                    rec.do("gpsimd", lambda E: E.dma_start(
                        out=t.ap(), in_=oG_out.ap()), mksem(), 16)
                rec.play()
                p2b_rel = [s_s, s_e, s_pv, s_dn, s_rc, s_rb, s_oe, s_oo,
                           s_ow, rs_ow, s_mk, rs_cvi, rs_cvo]

            barrier(block)
            release(*p2b_rel)

        with ExitStack() as stB:
            eB = stB.enter_context
            x2_sb = eB(nc.sbuf_tensor("x2sb", [128, 16 * TOK], BF16))
            s_x2 = mksem()

            with ExitStack() as st4:
                e4 = st4.enter_context
                ogb = e4(nc.sbuf_tensor("ogb", [128, 16 * TB], BF16))
                ogr = e4(nc.sbuf_tensor("ogr", [128, 16 * TB], F32R))
                xsb2 = e4(nc.sbuf_tensor("xsb2", [128, 16 * TB], F32R))
                zsb = e4(nc.sbuf_tensor("zsb", [128, 16 * TB], F32R))
                zsq = e4(nc.sbuf_tensor("zsq", [128, 4 * TB], BF16))
                ztmp = e4(nc.sbuf_tensor("ztmp", [128, 2 * TB], F32R))
                ztmp2 = e4(nc.sbuf_tensor("ztmp2", [128, 2 * TB], F32R))
                wr2 = e4(nc.sbuf_tensor("wr2", [128, 32 * 128], F32R))
                pZ = [e4(nc.psum_tensor(f"pz{i}", [128, TB], F32))
                      for i in range(2)]
                pSX2 = e4(nc.psum_tensor("psx2", [1, TB], F32))
                pSQ2 = e4(nc.psum_tensor("psq2", [1, TB], F32))
                pRb3 = e4(nc.psum_tensor("prb3", [128, TB], F32))
                pMrb3 = e4(nc.psum_tensor("pmrb3", [128, TB], F32))

                sZ = {k: mksem() for k in ("sq", "st", "dv", "sd", "bc", "ch")}
                s_zl = mksem()
                s_zc = mksem()
                rs_zw = RingSem(mksem, 2)
                s_zm = mksem()
                s_ze = mksem()

                rec = Rec(block)
                rec.wait("sync", s_cc2, 1)
                cidval = [None]

                def ld_cid(E):
                    cidval[0] = E.value_load(cid_sb[:])
                rec.do("sync", ld_cid)

                zm_hist = []
                ze_hist = []
                chz = {}

                def load_wz(cidx, war_tgt=None):
                    slot0 = (cidx % 2) * 16
                    f = cidx % 16
                    if war_tgt is not None:
                        rec.wait("sync", s_zm, war_tgt)
                    sem = rs_zw.next()
                    tgt = rec.do("sync", lambda E, f=f, s0=slot0: dma_tiles(
                        E, wr2[:, s0 * 128:(s0 + 16) * 128],
                        d["wzT"].ap()[f], 16), sem, 16)
                    chz[cidx] = (slot0, sem, tgt)

                for cidx in range(2):
                    load_wz(cidx)
                for tb in range(NTB):
                    if tb > 0:
                        rec.wait("sync", s_zm)
                        rec.wait("sync", s_x2)
                        rec.wait("sync", sZ["st"])
                    rec.do("sync", lambda E, tb=tb: dma_cols(
                        E, ogb[:],
                        oG_out.ap()[:, bass.ts(cidval[0] * 2 + tb, TB)]),
                        s_zl, 16)
                    rec.do("sync", lambda E, tb=tb: dma_cols(
                        E, xsb2[:], d["xT"].ap()[:, tb * TB:(tb + 1) * TB]),
                        s_zl, 16)
                    rec.wait("vector", s_zl)
                    if tb > 0:
                        rec.wait("vector", s_zm)
                    for a in range(16):
                        rec.do("vector", lambda E, a=a: E.tensor_copy(
                            ogr[:, a * TB:(a + 1) * TB],
                            ogb[:, a * TB:(a + 1) * TB]), s_zc)
                    rec.wait("tensor", s_zc)
                    if tb > 0:
                        rec.wait("scalar", s_x2)
                    for f in range(16):
                        cidx = tb * 16 + f
                        slot0, zwsem, tgt = chz[cidx]
                        rec.wait("tensor", zwsem, tgt)
                        if cidx >= 2:
                            rec.wait("tensor", s_ze, ze_hist[cidx - 2])
                        for kin in range(16):
                            rec.do("tensor", lambda E, s0=slot0, kin=kin, f=f:
                                   E.matmul(
                                pZ[f % 2][:],
                                wr2[:, (s0 + kin) * 128:(s0 + kin + 1) * 128],
                                ogr[:, kin * TB:(kin + 1) * TB],
                                start=(kin == 0), stop=(kin == 15)),
                                s_zm if kin == 15 else None)
                        zm_hist.append(s_zm.n)
                        if cidx + 2 < 32:
                            load_wz(cidx + 2, war_tgt=s_zm.n)
                        rec.wait("scalar", s_zm)
                        rec.do("scalar", lambda E, f=f: E.activation(
                            out=zsb[:, f * TB:(f + 1) * TB],
                            in_=pZ[f % 2][:], func=AF.Identity,
                            bias=bZ_sb[:, f:f + 1]), s_ze)
                        ze_hist.append(s_ze.n)
                    rec.wait("tensor", s_ze)
                    if tb > 0:
                        rec.wait("tensor", s_x2)

                    def apply2(rec, a, pRb, pMrb, tb=tb):
                        rec.dos("vector", lambda E, a=a: E.tensor_tensor(
                            ztmp[:, (a % 2) * TB:(a % 2 + 1) * TB],
                            zsb[:, a * TB:(a + 1) * TB], pRb, ALU.mult),
                            sZ["ch"])
                        rec.dos("vector", lambda E, a=a: E.tensor_tensor(
                            ztmp2[:, (a % 2) * TB:(a % 2 + 1) * TB],
                            ztmp[:, (a % 2) * TB:(a % 2 + 1) * TB], pMrb,
                            ALU.subtract), sZ["ch"])
                        rec.do("vector", lambda E, a=a, tb=tb: E.tensor_tensor(
                            x2_sb[:, a * TOK + tb * TB:a * TOK + (tb + 1) * TB],
                            xsb2[:, a * TB:(a + 1) * TB],
                            ztmp2[:, (a % 2) * TB:(a % 2 + 1) * TB], ALU.add),
                            s_x2)

                    emit_ln(rec, sZ,
                            dict(ones_x=ones_r[:, 0:1], ones_sq=ones_bf[:, 0:1],
                                 ones_row=ones_r[0:1, :],
                                 pSX=pSX2[0:1, :], pSQ=pSQ2[0:1, :],
                                 pRb=pRb3[:], pMrb=pMrb3[:], rows=ROWS),
                            lambda a: zsb[:, a * TB:(a + 1) * TB],
                            lambda a: zsq[:, (a % 4) * TB:(a % 4 + 1) * TB],
                            apply2)
                t = dbg_out("x2", (128, 16 * TOK), BF16)
                if t is not None:
                    rec.wait("gpsimd", s_x2)
                    rec.do("gpsimd", lambda E: E.dma_start(
                        out=t.ap(), in_=x2_sb[:]), mksem(), 16)
                rec.play()
                p2c_rel = [sZ, s_zl, s_zc, rs_zw, s_zm, s_ze]

            barrier(block)
            release(*p2c_rel)

            with ExitStack() as st5:
                e5 = st5.enter_context
                xn3 = e5(nc.sbuf_tensor("xn3", [128, 16 * TB], BF16))
                x2sq = e5(nc.sbuf_tensor("x2sq", [128, 4 * TB], BF16))
                fsq = e5(nc.sbuf_tensor("fsq", [128, 4 * TB], BF16))
                ftmp = e5(nc.sbuf_tensor("ftmp", [128, 2 * TB], F32R))
                ftmp2 = e5(nc.sbuf_tensor("ftmp2", [128, 2 * TB], F32R))
                facc = e5(nc.sbuf_tensor("facc", [128, 16 * TB], F32R))
                sh_e = e5(nc.sbuf_tensor("sh_e", [128, 11 * TB], BF16))
                asb = e5(nc.sbuf_tensor("asb", [128, 2 * TB], BF16))
                bsb = e5(nc.sbuf_tensor("bsb", [128, 2 * TB], BF16))
                shtile = e5(nc.sbuf_tensor("shtile", [128, 2 * TB], BF16))
                shr = e5(nc.sbuf_tensor("shr", [128, 4 * TB], BF16))
                wr3 = e5(nc.sbuf_tensor("wr3", [128, 32 * 128], BF16))
                sigb = e5(nc.sbuf_tensor("sigb", [128, TB], F32))
                wb_sb = e5(nc.sbuf_tensor("wb_sb", [128, TB], F32))
                lgsb = e5(nc.sbuf_tensor("lgsb", [NE, 4 * 128], BF16))
                ltok = e5(nc.sbuf_tensor("ltok", [128, 4 * NE], BF16))
                wtok = e5(nc.sbuf_tensor("wtok", [128, 4 * NE], BF16))
                wrow = e5(nc.sbuf_tensor("wrow", [NE, 4 * 128], BF16))
                wexp = e5(nc.sbuf_tensor("wexp", [1, TB], BF16))
                rsc = e5(nc.sbuf_tensor("rsc", [128, 40], F32))
                outsb = e5(nc.sbuf_tensor("outsb", [128, 2 * TB], F32))
                sig_row = e5(nc.sbuf_tensor("sigrow", [1, TB], F32R))
                pb = [e5(nc.psum_tensor(f"pb{i}", [128, TB], F32))
                      for i in range(6)]
                pTrB = e5(nc.psum_tensor("ptrb", [128, TB], BF16))

                sL3 = {k: mksem() for k in ("sq", "st", "dv", "sd", "bc", "ch")}
                sL4 = {k: mksem() for k in ("sq", "st", "dv", "sd", "bc", "ch")}
                rs_w3 = RingSem(mksem, 9)
                s_w3f = mksem()
                rs_shr = RingSem(mksem, 4)
                rs_spl = RingSem(mksem, 2)
                s_xn3 = mksem()
                s_rt = mksem()
                s_wex = mksem()
                s_sg = mksem()
                s_sl = mksem()
                s_sh = mksem()
                s_spl = mksem()
                s_shr = mksem()
                s_fa = mksem()
                s_wb = mksem()
                s_out = mksem()

                rec = Rec(block)
                t_xn3 = dbg_out("xn3", (NTB, 128, 16 * TB), F32R)
                t_wts = dbg_out("wts", (NTB, NE, 4 * 128), BF16)
                t_ltok = dbg_out("ltok", (NTB, 128, 4 * NE), BF16)
                t_wtok = dbg_out("wtok", (NTB, 128, 4 * NE), BF16)
                t_rsc = dbg_out("rsc", (NTB, 128, 40), F32)
                t_shs = dbg_out("shs", (44, 128, TB), F32R)
                t_shrd = dbg_out("shrd", (128, 4 * TB), F32R)
                t_fq0 = dbg_out("fq0", (128, 4 * TB), F32R)
                t_sgb = dbg_out("sgb", (128, TB), F32)
                t_sig = dbg_out("sig", (NTB, 128, TB), F32)
                t_fsh = dbg_out("fsh", (NTB, 128, 16 * TB), F32R)
                t_fac = dbg_out("fac", (NTB, 128, 16 * TB), F32R)
                s_dbg = mksem()
                ring_pos = [0]
                ring_owner = [0] * 32
                ring_seq = [0]
                free_tgt = {}

                def r3_load(src_fn, n):
                    if ring_pos[0] + n > 32:
                        ring_pos[0] = 0
                    s0 = ring_pos[0]
                    ring_pos[0] += n
                    prev = max(ring_owner[s0:s0 + n])
                    ring_seq[0] += 1
                    seq = ring_seq[0]
                    for s in range(s0, s0 + n):
                        ring_owner[s] = seq
                    if prev > 0:
                        rec.wait("sync", s_w3f, free_tgt[prev])
                    sem = rs_w3.next()
                    tgt = rec.do("sync", lambda E, s0=s0, n=n, src_fn=src_fn:
                                 dma_tiles(
                        E, wr3[:, s0 * 128:(s0 + n) * 128], src_fn(), n),
                        sem, 16)
                    return seq, s0, (sem, tgt)

                sl_par = [0, 0]
                sh_par = [0, 0]
                spl_hist = [None] * 44
                out_hist = []

                for tb in range(NTB):
                    rec.wait("scalar", s_x2)
                    rec.wait("tensor", s_x2)
                    if tb > 0:
                        rec.wait("tensor", s_xn3)
                        rec.wait("vector", s_w3f)
                        rec.wait("scalar", sL3["st"])

                    def apply3(rec, a, pRb, pMrb, tb=tb):
                        rec.dos("vector", lambda E, a=a, tb=tb: E.tensor_tensor(
                            ftmp[:, (a % 2) * TB:(a % 2 + 1) * TB],
                            x2_sb[:, a * TOK + tb * TB:a * TOK + (tb + 1) * TB],
                            pRb, ALU.mult), sL3["ch"])
                        rec.do("vector", lambda E, a=a: E.tensor_tensor(
                            xn3[:, a * TB:(a + 1) * TB],
                            ftmp[:, (a % 2) * TB:(a % 2 + 1) * TB], pMrb,
                            ALU.subtract), s_xn3)

                    emit_ln(rec, sL3,
                            dict(ones_x=ones_bf[:, 0:1], ones_sq=ones_bf[:, 0:1],
                                 ones_row=ones_r[0:1, :],
                                 pSX=pb[0][0:1, :], pSQ=pb[1][0:1, :],
                                 pRb=pb[2][:], pMrb=pb[3][:], rows=ROWS),
                            lambda a, tb=tb: x2_sb[:, a * TOK + tb * TB:
                                                   a * TOK + (tb + 1) * TB],
                            lambda a: x2sq[:, (a % 4) * TB:(a % 4 + 1) * TB],
                            apply3)
                    if t_xn3 is not None:
                        rec.wait("gpsimd", s_xn3)
                        rec.do("gpsimd", lambda E, tb=tb: E.dma_start(
                            out=t_xn3.ap()[tb], in_=xn3[:]), s_dbg, 16)
                    rec.wait("tensor", s_xn3)
                    for a in range(16):
                        rec.do("tensor", lambda E, a=a: E.matmul(
                            pb[4][0:NE, :], gw_sb[:, a * NE:(a + 1) * NE],
                            xn3[:, a * TB:(a + 1) * TB],
                            start=(a == 0), stop=(a == 15)))
                        rec.do("tensor", lambda E, a=a: E.matmul(
                            pb[5][0:1, :], sgw_sb[:, a:a + 1],
                            xn3[:, a * TB:(a + 1) * TB],
                            start=(a == 0), stop=(a == 15)),
                            s_rt if a == 15 else None)
                    rec.wait("scalar", s_rt)
                    rec.do("scalar", lambda E: E.activation(
                        out=lgsb[0:NE, :], in_=pb[4][0:NE, 0:4 * 128],
                        func=AF.Copy), s_rt)
                    rec.do("scalar", lambda E: E.activation(
                        out=sig_row[:], in_=pb[5][0:1, :], func=AF.Sigmoid),
                        s_sg)
                    rec.wait("tensor", s_rt)
                    for tt in range(4):
                        rec.do("tensor", lambda E, tt=tt: E.transpose(
                            pTrB[:, tt * NE:(tt + 1) * NE],
                            lgsb[0:NE, tt * 128:(tt + 1) * 128],
                            eye_sb[0:NE, 0:NE]),
                            s_rt if tt == 3 else None)
                    rec.wait("scalar", s_rt)
                    rec.dos("scalar", lambda E: E.activation(
                        out=ltok[:], in_=pTrB[:, 0:4 * NE], func=AF.Copy),
                        s_rt)
                    if t_ltok is not None:
                        rec.wait("gpsimd", s_rt)
                        rec.do("gpsimd", lambda E, tb=tb: E.dma_start(
                            out=t_ltok.ap()[tb], in_=ltok[:]), s_dbg, 16)
                    for tt in range(4):
                        sl = slice(tt * NE, (tt + 1) * NE)
                        rec.wait("vector", s_rt)
                        rec.dos("vector", lambda E, sl=sl: E.reduce_max(
                            rsc[:, 0:1], ltok[:, sl], axis=AXX), s_rt)
                        rec.do("vector", lambda E: E.tensor_scalar(
                            rsc[:, 1:2], rsc[:, 0:1], -1.0, None, ALU.mult),
                            s_rt)
                        rec.wait("scalar", s_rt)
                        rec.do("scalar", lambda E, sl=sl: E.activation(
                            out=rsc[:, 8:16], in_=ltok[:, sl], func=AF.Exp,
                            bias=rsc[:, 1:2]), s_rt)
                        rec.wait("vector", s_rt)
                        rec.dos("vector", lambda E: E.reduce_sum(
                            rsc[:, 2:3], rsc[:, 8:16], axis=AXX), s_rt)
                        rec.dos("vector", lambda E: E.reciprocal(
                            rsc[:, 3:4], rsc[:, 2:3]), s_rt)
                        rec.dos("vector", lambda E: E.tensor_scalar(
                            rsc[:, 16:24], rsc[:, 8:16], rsc[:, 3:4], None,
                            ALU.mult), s_rt)
                        rec.dos("vector", lambda E: E.reduce_max(
                            rsc[:, 4:5], rsc[:, 16:24], axis=AXX), s_rt)
                        rec.dos("vector", lambda E: E.tensor_scalar(
                            rsc[:, 24:32], rsc[:, 16:24], rsc[:, 4:5], None,
                            ALU.is_equal), s_rt)
                        rec.dos("vector", lambda E: E.tensor_scalar(
                            rsc[:, 24:32], rsc[:, 24:32], -1.0, 1.0,
                            ALU.mult, ALU.add), s_rt)
                        rec.dos("vector", lambda E: E.tensor_tensor(
                            rsc[:, 32:40], rsc[:, 24:32], rsc[:, 16:24],
                            ALU.mult), s_rt)
                        rec.dos("vector", lambda E: E.reduce_max(
                            rsc[:, 5:6], rsc[:, 32:40], axis=AXX), s_rt)
                        rec.dos("vector", lambda E: E.tensor_scalar(
                            rsc[:, 24:32], rsc[:, 16:24], rsc[:, 5:6], None,
                            ALU.is_ge), s_rt)
                        rec.do("vector", lambda E, tt=tt: E.tensor_tensor(
                            wtok[:, tt * NE:(tt + 1) * NE], rsc[:, 16:24],
                            rsc[:, 24:32], ALU.mult), s_rt)
                    if t_wtok is not None:
                        rec.wait("gpsimd", s_rt)
                        rec.do("gpsimd", lambda E, tb=tb: E.dma_start(
                            out=t_wtok.ap()[tb], in_=wtok[:]), s_dbg, 16)
                    if t_rsc is not None:
                        rec.wait("gpsimd", s_rt)
                        rec.do("gpsimd", lambda E, tb=tb: E.dma_start(
                            out=t_rsc.ap()[tb], in_=rsc[:]), s_dbg, 16)
                    rec.wait("tensor", s_rt)
                    for tt in range(4):
                        rec.do("tensor", lambda E, tt=tt: E.transpose(
                            pTrB[0:NE, tt * 128:(tt + 1) * 128],
                            wtok[:, tt * NE:(tt + 1) * NE], eye_sb[:]),
                            s_rt if tt == 3 else None)
                    rec.wait("scalar", s_rt)
                    rec.do("scalar", lambda E: E.activation(
                        out=wrow[:], in_=pTrB[0:NE, 0:4 * 128], func=AF.Copy),
                        s_rt)
                    rec.wait("gpsimd", s_rt)
                    if t_wts is not None:
                        rec.do("gpsimd", lambda E, tb=tb: E.dma_start(
                            out=t_wts.ap()[tb], in_=wrow[:]), s_dbg, 16)
                    rec.wait("tensor", s_sg)
                    rec.wait("tensor", s_wb)
                    rec.do("tensor", lambda E: E.matmul(
                        pb[4][:], ones_r[0:1, :], sig_row[:],
                        start=True, stop=True), s_sg)
                    rec.wait("scalar", s_sg)
                    rec.do("scalar", lambda E: E.activation(
                        out=sigb[:], in_=pb[4][:], func=AF.Copy), s_sg)
                    sig_done = s_sg.n
                    if t_sig is not None:
                        rec.wait("gpsimd", s_sg, sig_done)
                        rec.do("gpsimd", lambda E, tb=tb: E.dma_start(
                            out=t_sig.ap()[tb], in_=sigb[:]), s_dbg, 16)

                    for j in range(44):
                        _, s0g, tg = r3_load(
                            lambda j=j: sguB.ap()[j], 16)
                        seqg = ring_seq[0]
                        _, s0u, tu = r3_load(
                            lambda j=j: sguB.ap()[44 + j], 16)
                        sequ = ring_seq[0]
                        par = j % 2
                        rec.wait("tensor", tg[0], tg[1])
                        rec.wait("tensor", tu[0], tu[1])
                        if j >= 2:
                            rec.wait("tensor", s_sl, sl_par[par])
                            rec.wait("tensor", s_sh, sh_par[par])
                        if j == 0:
                            rec.wait("tensor", s_sg, sig_done)
                        for a in range(16):
                            rec.do("tensor", lambda E, s0=s0g, a=a, par=par:
                                   E.matmul(
                                pb[0 + par][:],
                                wr3[:, (s0 + a) * 128:(s0 + a + 1) * 128],
                                xn3[:, a * TB:(a + 1) * TB],
                                start=(a == 0), stop=(a == 15)),
                                s_w3f if a == 15 else None)
                        free_tgt[seqg] = s_w3f.n
                        for a in range(16):
                            rec.do("tensor", lambda E, s0=s0u, a=a, par=par:
                                   E.matmul(
                                pb[2 + par][:],
                                wr3[:, (s0 + a) * 128:(s0 + a + 1) * 128],
                                xn3[:, a * TB:(a + 1) * TB],
                                start=(a == 0), stop=(a == 15)),
                                s_w3f if a == 15 else None)
                        free_tgt[sequ] = s_w3f.n
                        rec.wait("scalar", s_w3f)
                        if j >= 2:
                            rec.wait("scalar", s_sh, sh_par[par])
                        rec.do("scalar", lambda E, par=par: E.activation(
                            out=asb[:, par * TB:(par + 1) * TB],
                            in_=pb[0 + par][:], func=AF.Silu), s_sl)
                        sl_par[par] = s_sl.n
                        rec.wait("vector", s_sl, sl_par[par])
                        if j >= 2:
                            psem, ptgt = spl_hist[j - 2]
                            rec.wait("vector", psem, ptgt)
                        rec.do("vector", lambda E, par=par: E.tensor_tensor(
                            shtile[:, par * TB:(par + 1) * TB],
                            asb[:, par * TB:(par + 1) * TB],
                            pb[2 + par][:], ALU.mult), s_sh)
                        sh_par[par] = s_sh.n
                        rec.wait("gpsimd", s_sh, sh_par[par])
                        spsem = rs_spl.next()
                        sptgt = rec.do("gpsimd", lambda E, j=j, par=par:
                                       E.dma_start(
                            out=shs.ap()[j], in_=shtile[:, par * TB:
                                                        (par + 1) * TB]),
                            spsem, 16)
                        spl_hist[j] = (spsem, sptgt)
                    if t_shs is not None and tb == 0:
                        rec.wait("gpsimd", s_spl)
                        rec.do("gpsimd", lambda E: E.dma_start(
                            out=t_shs.ap(), in_=shs.ap()), s_dbg, 16)
                    if t_sgb is not None and tb == 0:
                        rec.wait("gpsimd", s_sg, sig_done)
                        rec.do("gpsimd", lambda E: E.dma_start(
                            out=t_sgb.ap(), in_=sigb[:]), s_dbg, 16)
                    shr_t = [0, 0, 0, 0]
                    dn_mm = [0]
                    for fq in range(4):
                        for i in range(44):
                            seq, s0, tw = r3_load(
                                lambda fq=fq, i=i: sdwB.ap()[fq, i], 4)
                            if fq == 0 and i == 0:
                                for ssem, stgt in spl_hist[-2:]:
                                    rec.wait("gpsimd", ssem, stgt)
                            if fq * 44 + i >= 4:
                                rec.wait("gpsimd", s_w3f, shr_t[i % 4])
                            shsem = rs_shr.next()
                            shtgt = rec.do("gpsimd", lambda E, i=i:
                                           E.dma_start(
                                out=shr[:, (i % 4) * TB:(i % 4 + 1) * TB],
                                in_=shs.ap()[i]), shsem, 16)
                            rec.wait("tensor", shsem, shtgt)
                            rec.wait("tensor", tw[0], tw[1])
                            if i == 0:
                                rec.wait("tensor", s_fa)
                                rec.wait("tensor", s_sl)
                                rec.wait("tensor", s_sh)
                            for f4 in range(4):
                                rec.do("tensor", lambda E, s0=s0, f4=f4, i=i:
                                       E.matmul(
                                    pb[f4][:],
                                    wr3[:, (s0 + f4) * 128:(s0 + f4 + 1) * 128],
                                    shr[:, (i % 4) * TB:(i % 4 + 1) * TB],
                                    start=(i == 0), stop=(i == 43)),
                                    s_w3f if f4 == 3 else None)
                            free_tgt[seq] = s_w3f.n
                            shr_t[i % 4] = s_w3f.n
                        if t_shrd is not None and tb == 0 and fq == 0:
                            rec.wait("gpsimd", s_shr)
                            rec.do("gpsimd", lambda E: E.dma_start(
                                out=t_shrd.ap(), in_=shr[:]), s_dbg, 16)
                            rec.wait("sync", s_dbg)
                            rec.wait("tensor", s_dbg)
                        rec.wait("vector", s_w3f)
                        rec.wait("vector", s_sg, sig_done)
                        if tb > 0 and fq == 0:
                            rec.wait("vector", sL4["sq"])
                        for f4 in range(4):
                            rec.do("vector", lambda E, fq=fq, f4=f4, tb=tb:
                                   E.tensor_tensor(
                                facc[:, (fq * 4 + f4) * TB:
                                     (fq * 4 + f4 + 1) * TB],
                                pb[f4][:], sigb[:], ALU.mult), s_fa)
                        if t_fq0 is not None and tb == 0 and fq == 0:
                            rec.wait("gpsimd", s_fa)
                            rec.do("gpsimd", lambda E: E.dma_start(
                                out=t_fq0.ap(), in_=facc[:, 0:4 * TB]),
                                s_dbg, 16)
                            rec.wait("tensor", s_dbg)
                    if t_fsh is not None:
                        rec.wait("gpsimd", s_fa)
                        rec.do("gpsimd", lambda E, tb=tb: E.dma_start(
                            out=t_fsh.ap()[tb], in_=facc[:]), s_dbg, 16)
                        rec.wait("vector", s_dbg)
                    for en in range(NE):
                        if s_wb.n > 0:
                            rec.wait("gpsimd", s_wb)
                        rec.do("gpsimd", lambda E, en=en: E.dma_start(
                            out=wexp[0:1, :], in_=wrow[en:en + 1, :]),
                            s_wex, 16)
                        rec.wait("tensor", s_wex)
                        rec.wait("tensor", s_wb)
                        rec.do("tensor", lambda E: E.matmul(
                            pb[4][:], ones_bf[0:1, :], wexp[0:1, :],
                            start=True, stop=True), s_wb)
                        rec.wait("scalar", s_wb)
                        rec.do("scalar", lambda E: E.activation(
                            out=wb_sb[:], in_=pb[4][:], func=AF.Copy), s_wb)
                        wb_done = s_wb.n
                        for j in range(11):
                            _, s0g, tg = r3_load(
                                lambda en=en, j=j: w13B.ap()[en, j], 16)
                            seqg = ring_seq[0]
                            _, s0u, tu = r3_load(
                                lambda en=en, j=j: w13B.ap()[en, 11 + j], 16)
                            sequ = ring_seq[0]
                            par = j % 2
                            rec.wait("tensor", tg[0], tg[1])
                            rec.wait("tensor", tu[0], tu[1])
                            if j >= 2:
                                rec.wait("tensor", s_sl, sl_par[par])
                                rec.wait("tensor", s_sh, sh_par[par])
                            if j == 0:
                                rec.wait("tensor", s_fa)
                            for a in range(16):
                                rec.do("tensor", lambda E, s0=s0g, a=a,
                                       par=par: E.matmul(
                                    pb[0 + par][:],
                                    wr3[:, (s0 + a) * 128:(s0 + a + 1) * 128],
                                    xn3[:, a * TB:(a + 1) * TB],
                                    start=(a == 0), stop=(a == 15)),
                                    s_w3f if a == 15 else None)
                            free_tgt[seqg] = s_w3f.n
                            for a in range(16):
                                rec.do("tensor", lambda E, s0=s0u, a=a,
                                       par=par: E.matmul(
                                    pb[2 + par][:],
                                    wr3[:, (s0 + a) * 128:(s0 + a + 1) * 128],
                                    xn3[:, a * TB:(a + 1) * TB],
                                    start=(a == 0), stop=(a == 15)),
                                    s_w3f if a == 15 else None)
                            free_tgt[sequ] = s_w3f.n
                            rec.wait("scalar", s_w3f)
                            rec.wait("scalar", s_wb, wb_done)
                            if True:
                                rec.wait("scalar", s_sh, sh_par[par])
                            rec.do("scalar", lambda E, par=par: E.activation(
                                out=asb[:, par * TB:(par + 1) * TB],
                                in_=pb[0 + par][:], func=AF.Silu), s_sl)
                            sl_par[par] = s_sl.n
                            rec.wait("vector", s_sl, sl_par[par])
                            rec.dos("vector", lambda E, par=par:
                                   E.tensor_tensor(
                                bsb[:, par * TB:(par + 1) * TB],
                                pb[2 + par][:], wb_sb[:], ALU.mult), s_sh)
                            rec.do("vector", lambda E, par=par, j=j:
                                   E.tensor_tensor(
                                sh_e[:, j * TB:(j + 1) * TB],
                                asb[:, par * TB:(par + 1) * TB],
                                bsb[:, par * TB:(par + 1) * TB], ALU.mult),
                                s_sh)
                            sh_par[par] = s_sh.n
                        for fq in range(4):
                            for i in range(11):
                                seq, s0, tw = r3_load(
                                    lambda en=en, fq=fq, i=i:
                                    w2B.ap()[en, fq, i], 4)
                                rec.wait("tensor", tw[0], tw[1])
                                if i == 0:
                                    rec.wait("tensor", s_fa)
                                    rec.wait("tensor", s_sl)
                                    rec.wait("tensor", s_sh)
                                for f4 in range(4):
                                    rec.do("tensor", lambda E, s0=s0, f4=f4,
                                           i=i: E.matmul(
                                        pb[f4][:],
                                        wr3[:, (s0 + f4) * 128:
                                            (s0 + f4 + 1) * 128],
                                        sh_e[:, i * TB:(i + 1) * TB],
                                        start=(i == 0), stop=(i == 10)),
                                        s_w3f if f4 == 3 else None)
                                free_tgt[seq] = s_w3f.n
                            rec.wait("vector", s_w3f)
                            for f4 in range(4):
                                rec.do("vector", lambda E, fq=fq, f4=f4:
                                       E.tensor_tensor(
                                    facc[:, (fq * 4 + f4) * TB:
                                         (fq * 4 + f4 + 1) * TB],
                                    facc[:, (fq * 4 + f4) * TB:
                                         (fq * 4 + f4 + 1) * TB],
                                    pb[f4][:], ALU.add), s_fa)
                    if t_fac is not None:
                        rec.wait("gpsimd", s_fa)
                        rec.do("gpsimd", lambda E, tb=tb: E.dma_start(
                            out=t_fac.ap()[tb], in_=facc[:]), s_dbg, 16)
                        rec.wait("scalar", s_dbg)
                    rec.wait("scalar", s_fa)
                    rec.wait("tensor", s_fa)

                    def apply4(rec, a, pRb, pMrb, tb=tb):
                        rec.dos("vector", lambda E, a=a: E.tensor_tensor(
                            ftmp[:, (a % 2) * TB:(a % 2 + 1) * TB],
                            facc[:, a * TB:(a + 1) * TB], pRb, ALU.mult),
                            sL4["ch"])
                        rec.dos("vector", lambda E, a=a: E.tensor_tensor(
                            ftmp2[:, (a % 2) * TB:(a % 2 + 1) * TB],
                            ftmp[:, (a % 2) * TB:(a % 2 + 1) * TB], pMrb,
                            ALU.subtract), sL4["ch"])
                        if len(out_hist) >= 2:
                            rec.wait("vector", s_out, out_hist[-2])
                        rec.do("vector", lambda E, a=a, tb=tb: E.tensor_tensor(
                            outsb[:, (a % 2) * TB:(a % 2 + 1) * TB],
                            x2_sb[:, a * TOK + tb * TB:a * TOK + (tb + 1) * TB],
                            ftmp2[:, (a % 2) * TB:(a % 2 + 1) * TB], ALU.add),
                            s_xn3)
                        rec.wait("gpsimd", s_xn3)
                        rec.do("gpsimd", lambda E, a=a, tb=tb: E.dma_start(
                            out=yT.ap()[a * 128:(a + 1) * 128,
                                        tb * TB:(tb + 1) * TB],
                            in_=outsb[:, (a % 2) * TB:(a % 2 + 1) * TB]),
                            s_out, 16)
                        out_hist.append(s_out.n)

                    emit_ln(rec, sL4,
                            dict(ones_x=ones_r[:, 0:1], ones_sq=ones_bf[:, 0:1],
                                 ones_row=ones_r[0:1, :],
                                 pSX=pb[0][0:1, :], pSQ=pb[1][0:1, :],
                                 pRb=pb[2][:], pMrb=pb[3][:], rows=ROWS),
                            lambda a: facc[:, a * TB:(a + 1) * TB],
                            lambda a: fsq[:, (a % 4) * TB:(a % 4 + 1) * TB],
                            apply4)
                rec.wait("gpsimd", s_out)
                rec.play()

    return nc, dbg_t



def _rope_tables(t_len, dd):
    pos = np.repeat(np.arange(dd // 2, dtype=np.float64), 2)
    theta = np.exp(-pos / dd * np.log(10000.0))
    ang = np.arange(1, t_len + 1, dtype=np.float64)[:, None] * theta[None, :]
    return np.cos(ang), np.sin(ang)


def prep_in_maps(inputs):
    f32 = np.float32
    x = np.asarray(inputs["L0_attn_x"], f32)[0]
    WA = np.asarray(inputs["L0_attn_WA"], f32)
    bA = np.asarray(inputs["L0_attn_bA"], f32)
    WB = np.asarray(inputs["L0_attn_WB"], f32)
    bB = np.asarray(inputs["L0_attn_bB"], f32)
    WC = np.asarray(inputs["L0_attn_WC"], f32)
    bC = np.asarray(inputs["L0_attn_bC"], f32)
    WUQ = np.asarray(inputs["L0_attn_WUQ"], f32)
    WUK = np.asarray(inputs["L0_attn_WUK"], f32)
    WUV = np.asarray(inputs["L0_attn_WUV"], f32)
    WZ = np.asarray(inputs["L0_attn_WZ"], f32)
    bZ = np.asarray(inputs["L0_attn_bZ"], f32)
    gw = np.asarray(inputs["L0_ffn_gate_w"], f32)
    sgw = np.asarray(inputs["L0_ffn_shared_gate_w"], f32)
    sguw = np.asarray(inputs["L0_ffn_shared_gate_up_w"], f32)
    sdw = np.asarray(inputs["L0_ffn_shared_down_w"], f32)
    w13 = np.asarray(inputs["L0_ffn_w13"], f32)
    w2 = np.asarray(inputs["L0_ffn_w2"], f32)

    xT = np.ascontiguousarray(x.T)
    waT = np.ascontiguousarray(
        WA.reshape(16, 128, 8, 128).transpose(2, 0, 1, 3))
    bA_r = np.ascontiguousarray(bA.reshape(8, 128).T)
    wbT = np.ascontiguousarray(WB.reshape(8, 128, DHR))
    bB_c = bB.reshape(DHR, 1)
    wzT = np.ascontiguousarray(
        WZ.reshape(16, 128, 16, 128).transpose(2, 0, 1, 3))
    bZ_r = np.ascontiguousarray(bZ.reshape(16, 128).T)
    gwT = np.ascontiguousarray(
        gw.T.reshape(16, 128, NE).transpose(1, 0, 2).reshape(128, 16 * NE))
    sgwT = np.ascontiguousarray(sgw.T.reshape(16, 128).T)
    sguT = np.ascontiguousarray(
        sguw.T.reshape(16, 128, 88, 128).transpose(2, 0, 1, 3))
    sdwT = np.ascontiguousarray(
        sdw.T.reshape(44, 128, 4, 4, 128).transpose(2, 0, 3, 1, 4))
    w13T = np.ascontiguousarray(
        w13.transpose(0, 2, 1).reshape(NE, 16, 128, 22, 128)
        .transpose(0, 3, 1, 2, 4))
    w2T = np.ascontiguousarray(
        w2.transpose(0, 2, 1).reshape(NE, 11, 128, 4, 4, 128)
        .transpose(0, 3, 1, 4, 2, 5))

    cq_full, sq_full = _rope_tables(T, NH * DHR)
    ck_full, sk_full = _rope_tables(T, DHR)
    ones_bf = np.ones((128, 128), BF)
    ones_r = np.ones((128, 128), f32)
    eye = np.eye(128, dtype=f32).astype(BF)
    RTm = np.zeros((128, 128), f32)
    for i in range(64):
        RTm[2 * i + 1, 2 * i] = -1.0
        RTm[2 * i, 2 * i + 1] = 1.0
    RTm = RTm.astype(BF)
    maskd = np.zeros((128, 4 * TB), f32)
    for rel in range(4):
        kk = np.arange(128)[:, None] + 128 * rel
        qq = np.arange(TB)[None, :]
        maskd[:, rel * TB:(rel + 1) * TB] = (kk <= qq).astype(f32)
    maskd = maskd.astype(BF)

    in_maps = []
    for c in range(NCORE):
        tok = slice(c * TOK, (c + 1) * TOK)
        hd = slice(c * 2 * DH, (c + 1) * 2 * DH)
        rd = slice(c * 128, (c + 1) * 128)
        cosq = np.ascontiguousarray(
            (cq_full[:, rd].T * SCALE)).astype(BF)
        sinq = np.ascontiguousarray(
            (sq_full[:, rd].T * SCALE)).astype(BF)
        m = {
            "xT": np.ascontiguousarray(xT[:, tok]),
            "cid": np.array([[c]], np.int32),
            "waT": waT, "bA_r": bA_r, "wbT": wbT, "bB_c": bB_c,
            "cosk": np.ascontiguousarray(ck_full[tok, :].T).astype(BF),
            "sink": np.ascontiguousarray(sk_full[tok, :].T).astype(BF),
            "wuq": np.ascontiguousarray(WUQ[:, hd].reshape(4, 128, 256)),
            "wuk": np.ascontiguousarray(WUK[:, hd].reshape(4, 128, 256)),
            "wuv": np.ascontiguousarray(WUV[:, hd].reshape(4, 128, 256)),
            "wcc": np.ascontiguousarray(WC[:, rd].reshape(8, 128, 128)),
            "bC_c": np.ascontiguousarray(bC[rd].reshape(128, 1)),
            "cosq": np.ascontiguousarray(
                cosq.reshape(128, 16, TB).transpose(1, 0, 2)),
            "sinq": np.ascontiguousarray(
                sinq.reshape(128, 16, TB).transpose(1, 0, 2)),
            "RT": RTm, "eye": eye, "maskd": maskd,
            "ones_bf": ones_bf, "ones_r": ones_r,
            "wzT": wzT, "bZ_r": bZ_r, "gwT": gwT, "sgwT": sgwT,
            "sguT": sguT, "sdwT": sdwT, "w13T": w13T, "w2T": w2T,
        }
        in_maps.append(m)
    return in_maps


_CACHE = {}


def kernel(**inputs):
    if "nc" not in _CACHE:
        _CACHE["nc"], _ = build()
    nc = _CACHE["nc"]
    in_maps = prep_in_maps(inputs)
    res = run_bass_kernel_spmd(nc, in_maps, core_ids=list(range(NCORE)))
    y = np.concatenate([res.results[c]["yT"] for c in range(NCORE)], axis=1)
    return np.ascontiguousarray(y.T).reshape(1, T, C).astype(np.float32)


# version 06
# speedup vs baseline: 1.3332x; 1.1430x over previous
import numpy as np
import ml_dtypes
from contextlib import ExitStack

import concourse.bass as bass
import concourse.mybir as mybir
from concourse.bass_utils import run_bass_kernel_spmd

F32 = mybir.dt.float32
F32R = mybir.dt.float32r
BF16 = mybir.dt.bfloat16
I32 = mybir.dt.int32
AF = mybir.ActivationFunctionType
ALU = mybir.AluOpType
AXX = mybir.AxisListType.X

NCORE = 8
T, C = 8192, 2048
NH, DH, DHR, LL = 16, 128, 64, 512
TOK = T // NCORE
TB = 512
NTB = TOK // TB
SI, MI, NE = 5632, 1408, 8
EPS = 1e-5
SCALE = float((DH + DHR) ** -0.5)
CMARGIN = -1.02
NQB = T // TB
HPC = 2
BF = ml_dtypes.bfloat16


def dma_tiles(E, dst2d, src3d, a):
    return E.dma_start(out=dst2d.rearrange("p (a f) -> p a f", a=a),
                       in_=src3d.rearrange("a p f -> p a f"))


def dma_cols(E, dst2d, src2d, a=16):
    return E.dma_start(out=dst2d.rearrange("p (a t) -> p a t", a=a),
                       in_=src2d.rearrange("(a p) t -> p a t", p=128))


class Sem:
    def __init__(self, handle):
        self.h = handle
        self.n = 0


class Rec:
    ENG = ("sync", "tensor", "scalar", "vector", "gpsimd")

    def __init__(self, block):
        self.block = block
        self.q = {k: [] for k in self.ENG}

    def wait(self, eng, sem, n=None):
        tgt = sem.n if n is None else n
        if tgt > 0:
            self.q[eng].append(lambda E, s=sem.h, t=tgt: E.wait_ge(s, t))

    def do(self, eng, build, sem=None, k=1):
        if sem is None:
            self.q[eng].append(build)
            return None
        sem.n += k
        tgt = sem.n

        def f(E, b=build, s=sem.h, kk=k):
            b(E).then_inc(s, kk)
        self.q[eng].append(f)
        return tgt

    def dos(self, eng, build, sem, k=1):
        t = self.do(eng, build, sem, k)
        self.wait(eng, sem, t)
        return t

    def play(self):
        for name in self.ENG:
            ops = self.q[name]
            if ops:
                getattr(self.block, name)(
                    lambda E, ops=ops: [op(E) for op in ops])
        self.q = {k: [] for k in self.ENG}


class RingSem:

    def __init__(self, mksem, k):
        self.sems = [mksem() for _ in range(k)]
        self.k = k
        self.idx = 0

    def next(self):
        s = self.sems[self.idx % self.k]
        self.idx += 1
        return s


def emit_ln(rec, S, B, src_tile, sq_tile, apply_emit, n_tiles=16, denom=C):
    row_m, row_v, row_sd, row_r, row_mr = B["rows"]
    sq_hist = []
    st_t = []
    rec.wait("tensor", S["dv"])
    for a in range(n_tiles):
        if a >= 4:
            rec.wait("scalar", S["st"], st_t[a - 4])
        rec.do("scalar", lambda E, d=sq_tile(a), s=src_tile(a): E.activation(
            out=d, in_=s, func=AF.Square), S["sq"])
        sq_hist.append(S["sq"].n)
        rec.wait("tensor", S["sq"], sq_hist[a])
        rec.do("tensor", lambda E, s=src_tile(a), a=a, n=n_tiles: E.matmul(
            B["pSX"], B["ones_x"], s, start=(a == 0), stop=(a == n - 1)))
        rec.do("tensor", lambda E, d=sq_tile(a), a=a, n=n_tiles: E.matmul(
            B["pSQ"], B["ones_sq"], d, start=(a == 0), stop=(a == n - 1)),
            S["st"])
        st_t.append(S["st"].n)
    rec.wait("vector", S["st"])
    rec.dos("vector", lambda E: E.tensor_scalar(
        row_m[:], B["pSX"], 1.0 / denom, None, ALU.mult), S["ch"])
    rec.dos("vector", lambda E: E.tensor_scalar(
        row_v[:], B["pSQ"], 1.0 / denom, None, ALU.mult), S["ch"])
    rec.dos("vector", lambda E: E.tensor_scalar(
        row_v[:], row_v[:], EPS, None, ALU.add), S["ch"])
    rec.dos("vector", lambda E: E.tensor_tensor(
        row_mr[:], row_m[:], row_m[:], ALU.mult), S["ch"])
    rec.do("vector", lambda E: E.tensor_tensor(
        row_v[:], row_v[:], row_mr[:], ALU.subtract), S["dv"])
    rec.wait("scalar", S["dv"])
    rec.do("scalar", lambda E: E.sqrt(out=row_sd[:], in_=row_v[:]), S["sd"])
    rec.wait("vector", S["sd"])
    rec.dos("vector", lambda E: E.reciprocal(row_r[:], row_sd[:]), S["ch"])
    rec.do("vector", lambda E: E.tensor_tensor(
        row_mr[:], row_m[:], row_r[:], ALU.mult), S["dv"])
    rec.wait("tensor", S["dv"])
    rec.do("tensor", lambda E: E.matmul(
        B["pRb"], B["ones_row"], row_r[:], start=True, stop=True))
    rec.do("tensor", lambda E: E.matmul(
        B["pMrb"], B["ones_row"], row_mr[:], start=True, stop=True), S["bc"])
    rec.wait("vector", S["bc"])
    for a in range(n_tiles):
        apply_emit(rec, a, B["pRb"], B["pMrb"])


def build(dbg=()):
    dbg = set(dbg)
    nc = bass.Bass("TRN2", target_bir_lowering=False)
    d = {}

    def din(name, shape, dt):
        d[name] = nc.dram_tensor(name, list(shape), dt, kind="ExternalInput")

    din("xT", (C, TOK), F32R)
    din("cid", (1, 1), I32)
    din("waT", (8, 16, 128, 128), F32R)
    din("bA_r", (128, 8), F32)
    din("wbT", (8, 128, DHR), F32)
    din("bB_c", (DHR, 1), F32)
    din("cosk", (DHR, TOK), BF16)
    din("sink", (DHR, TOK), BF16)
    din("wuq", (4, 128, 256), F32)
    din("wuk", (4, 128, 256), F32)
    din("wuv", (4, 128, 256), F32)
    din("wcc", (8, 128, 128), F32)
    din("bC_c", (128, 1), F32)
    din("cosq", (16, 128, TB), BF16)
    din("sinq", (16, 128, TB), BF16)
    din("RT", (128, 128), BF16)
    din("eye", (128, 128), BF16)
    din("maskd", (128, 4 * TB), BF16)
    din("ones_bf", (128, 128), BF16)
    din("ones_r", (128, 128), F32R)
    din("wzT", (16, 16, 128, 128), F32R)
    din("bZ_r", (128, 16), F32)
    din("gwT", (128, 16 * NE), F32R)
    din("sgwT", (128, 16), F32R)
    din("sguT", (88, 16, 128, 128), F32R)
    din("sdwT", (4, 44, 4, 128, 128), F32R)
    din("w13T", (NE, 22, 16, 128, 128), F32R)
    din("w2T", (NE, 4, 11, 4, 128, 128), F32R)

    yT = nc.dram_tensor("yT", [C, TOK], F32, kind="ExternalOutput")

    hG_in = nc.dram_tensor("hG_in", [1088, TOK], BF16)
    hG_out = nc.dram_tensor("hG_out", [NCORE * 1088, TOK], BF16, addr_space="Shared")
    oG_in = nc.dram_tensor("oG_in", [HPC * DH, T], BF16)
    oG_out = nc.dram_tensor("oG_out", [NH * DH, T], BF16, addr_space="Shared")
    shs = nc.dram_tensor("shs", [44, 128, TB], BF16)
    wzB = nc.dram_tensor("wzB", [16, 16, 128, 128], BF16)
    sguB = nc.dram_tensor("sguB", [88, 16, 128, 128], BF16)
    sdwB = nc.dram_tensor("sdwB", [4, 44, 4, 128, 128], BF16)
    w13B = nc.dram_tensor("w13B", [NE, 22, 16, 128, 128], BF16)
    w2B = nc.dram_tensor("w2B", [NE, 4, 11, 4, 128, 128], BF16)
    RG = [list(range(NCORE))]

    dbg_t = {}

    def dbg_out(name, shape, dt):
        if name in dbg:
            dbg_t[name] = nc.dram_tensor("dbg_" + name, list(shape), dt,
                                         kind="ExternalOutput")
            return dbg_t[name]
        return None

    with ExitStack() as TOP:
        e = TOP.enter_context
        e(nc.allow_low_precision(
            reason="deliberate f32r/bf16 intermediates; fp32 PSUM accumulation"))
        _semctr = [0]
        _sem_free = []

        def mksem(nm=None):
            if nm is None and _sem_free:
                return _sem_free.pop()
            _semctr[0] += 1
            return Sem(e(nc.semaphore(nm or f"s{_semctr[0]}")))

        def release(*items):
            for it in items:
                if isinstance(it, RingSem):
                    _sem_free.extend(it.sems)
                elif isinstance(it, dict):
                    _sem_free.extend(it.values())
                elif isinstance(it, Sem):
                    _sem_free.append(it)

        bar = mksem("bar")
        bar_k = [0]

        def barrier(block):
            bar_k[0] += 1
            tgt = 5 * bar_k[0]
            for reg in (block.sync, block.tensor, block.scalar, block.vector,
                        block.gpsimd):
                def f(eng, tgt=tgt):
                    eng.sem_inc(bar.h, 1)
                    eng.wait_ge(bar.h, tgt)
                reg(f)
            bar.n = tgt

        ones_bf = e(nc.sbuf_tensor("onesbf", [128, 128], BF16))
        ones_r = e(nc.sbuf_tensor("onesr", [128, 128], F32R))
        rt_sb = e(nc.sbuf_tensor("rtsb", [128, 128], BF16))
        eye_sb = e(nc.sbuf_tensor("eyesb", [128, 128], BF16))
        bA_sb = e(nc.sbuf_tensor("bAsb", [128, 8], F32))
        bB_sb = e(nc.sbuf_tensor("bBsb", [DHR, 1], F32))
        bC_sb = e(nc.sbuf_tensor("bCsb", [128, 1], F32))
        bZ_sb = e(nc.sbuf_tensor("bZsb", [128, 16], F32))
        gw_sb = e(nc.sbuf_tensor("gwsb", [128, 16 * NE], BF16))
        sgw_sb = e(nc.sbuf_tensor("sgwsb", [128, 16], BF16))
        cid_sb = e(nc.sbuf_tensor("cidsb", [1, 1], I32))

        row_m = e(nc.sbuf_tensor("rowm", [1, TB], F32))
        row_v = e(nc.sbuf_tensor("rowv", [1, TB], F32))
        row_sd = e(nc.sbuf_tensor("rowsd", [1, TB], F32))
        row_r = e(nc.sbuf_tensor("rowr", [1, TB], F32R))
        row_mr = e(nc.sbuf_tensor("rowmr", [1, TB], F32R))
        ROWS = (row_m, row_v, row_sd, row_r, row_mr)

        block = e(nc.Block())

        s_ld = mksem("ld")
        s_cv = mksem("cv")
        with ExitStack() as st0:
            e0 = st0.enter_context
            gw_f = e0(nc.sbuf_tensor("gwf", [128, 16 * NE], F32R))
            sgw_f = e0(nc.sbuf_tensor("sgwf", [128, 16], F32R))
            rec = Rec(block)
            for nm, t in [("ones_bf", ones_bf), ("ones_r", ones_r),
                          ("RT", rt_sb), ("eye", eye_sb),
                          ("bA_r", bA_sb), ("bB_c", bB_sb), ("bC_c", bC_sb),
                          ("bZ_r", bZ_sb), ("cid", cid_sb)]:
                rec.do("sync", lambda E, t=t, nm=nm: E.dma_start(
                    out=t[:], in_=d[nm].ap()), s_ld, 16)
            for nm, t in [("gwT", gw_f), ("sgwT", sgw_f)]:
                rec.do("sync", lambda E, t=t, nm=nm: E.dma_start(
                    out=t[:], in_=d[nm].ap()), s_ld, 16)
            rec.wait("vector", s_ld)
            rec.do("vector", lambda E: E.tensor_copy(gw_sb[:], gw_f[:]))
            rec.do("vector", lambda E: E.tensor_copy(sgw_sb[:], sgw_f[:]), s_cv)
            rec.play()
            barrier(block)
        LDC = s_ld.n

        s_cc1 = mksem("cc1")
        with ExitStack() as st1:
            e1 = st1.enter_context
            xsb = e1(nc.sbuf_tensor("p1x", [128, 16 * TB], F32R))
            sqb = e1(nc.sbuf_tensor("p1sqb", [128, 4 * TB], BF16))
            tmpb = e1(nc.sbuf_tensor("p1tmp", [128, 2 * TB], F32R))
            xn1 = e1(nc.sbuf_tensor("p1xn1", [128, 16 * TB], F32R))
            hsb = e1(nc.sbuf_tensor("p1h", [128, 8 * TOK], BF16))
            krraw = e1(nc.sbuf_tensor("p1kraw", [DHR, TB], BF16))
            krt1 = e1(nc.sbuf_tensor("p1krt1", [DHR, TB], BF16))
            krt2 = e1(nc.sbuf_tensor("p1krt2", [DHR, TB], BF16))
            hkr = e1(nc.sbuf_tensor("p1hkr", [DHR, TOK], BF16))
            wr1 = e1(nc.sbuf_tensor("p1wr", [128, 48 * 128], F32R))
            cosk_sb = e1(nc.sbuf_tensor("cosksb", [DHR, TOK], BF16))
            sink_sb = e1(nc.sbuf_tensor("sinksb", [DHR, TOK], BF16))
            wb_f = e1(nc.sbuf_tensor("wbf", [128, 8 * DHR], F32))
            wb_bf = e1(nc.sbuf_tensor("wbbf", [128, 8 * DHR], BF16))
            pH = [e1(nc.psum_tensor(f"p1ph{i}", [128, TB], F32)) for i in range(2)]
            pSX = e1(nc.psum_tensor("p1psx", [1, TB], F32))
            pSQ = e1(nc.psum_tensor("p1psq", [1, TB], F32))
            pRbA = e1(nc.psum_tensor("p1prb", [128, TB], F32))
            pMrbA = e1(nc.psum_tensor("p1pmrb", [128, TB], F32))
            pKR = e1(nc.psum_tensor("p1pkr", [DHR, TB], F32))
            pRot = e1(nc.psum_tensor("p1prot", [DHR, TB], F32))

            sS = {k: mksem() for k in ("sq", "st", "dv", "sd", "bc", "ch")}
            s_x = mksem()
            rs_wld = RingSem(mksem, 3)
            s_xn = mksem()
            s_hmm = mksem()
            s_hev = mksem()
            s_kr = mksem()
            s_hgw = mksem()

            rec = Rec(block)
            s_ld1 = mksem()
            for eng in ("tensor", "scalar", "vector"):
                rec.wait(eng, s_ld, LDC)
            for nm, t in [("cosk", cosk_sb), ("sink", sink_sb)]:
                rec.do("sync", lambda E, t=t, nm=nm: E.dma_start(
                    out=t[:], in_=d[nm].ap()), s_ld1, 16)
            rec.do("sync", lambda E: dma_tiles(E, wb_f[:], d["wbT"].ap(), 8),
                   s_ld1, 16)
            rec.wait("vector", s_ld1)
            rec.do("vector", lambda E: E.tensor_copy(wb_bf[:], wb_f[:]), s_cv)
            rec.wait("tensor", s_cv)
            ev_hist = []
            hmm_hist = []
            ch_all = {}

            def load_wa(cidx, war_tgt=None):
                slot0 = (cidx % 3) * 16
                j = cidx % 8
                if war_tgt is not None:
                    rec.wait("sync", s_hmm, war_tgt)
                sem = rs_wld.next()
                tgt = rec.do("sync", lambda E, j=j, s0=slot0: dma_tiles(
                    E, wr1[:, s0 * 128:(s0 + 16) * 128],
                    d["waT"].ap()[j], 16), sem, 16)
                ch_all[cidx] = (slot0, sem, tgt)

            for cidx in range(3):
                load_wa(cidx)
            for tb in range(NTB):
                if tb > 0:
                    rec.wait("sync", s_xn)
                    rec.wait("sync", sS["st"])
                rec.do("sync", lambda E, tb=tb: dma_cols(
                    E, xsb[:], d["xT"].ap()[:, tb * TB:(tb + 1) * TB]),
                    s_x, 16)
                rec.wait("scalar", s_x)
                rec.wait("tensor", s_x)
                if tb > 0:
                    rec.wait("tensor", s_xn)
                    rec.wait("vector", s_hmm)

                def apply1(rec, a, pRb, pMrb):
                    rec.dos("vector", lambda E, a=a: E.tensor_tensor(
                        tmpb[:, (a % 2) * TB:(a % 2 + 1) * TB],
                        xsb[:, a * TB:(a + 1) * TB], pRb, ALU.mult), sS["ch"])
                    rec.do("vector", lambda E, a=a: E.tensor_tensor(
                        xn1[:, a * TB:(a + 1) * TB],
                        tmpb[:, (a % 2) * TB:(a % 2 + 1) * TB], pMrb,
                        ALU.subtract), s_xn)

                emit_ln(rec, sS,
                        dict(ones_x=ones_r[:, 0:1], ones_sq=ones_bf[:, 0:1],
                             ones_row=ones_r[0:1, :],
                             pSX=pSX[0:1, :], pSQ=pSQ[0:1, :],
                             pRb=pRbA[:], pMrb=pMrbA[:], rows=ROWS),
                        lambda a: xsb[:, a * TB:(a + 1) * TB],
                        lambda a: sqb[:, (a % 4) * TB:(a % 4 + 1) * TB],
                        apply1)
                rec.wait("tensor", s_xn)
                for j in range(8):
                    cidx = tb * 8 + j
                    slot0, wsem, tgt = ch_all[cidx]
                    rec.wait("tensor", wsem, tgt)
                    if cidx >= 2:
                        rec.wait("tensor", s_hev, ev_hist[cidx - 2])
                    for ki in range(16):
                        rec.do("tensor", lambda E, s0=slot0, ki=ki, j=j:
                               E.matmul(
                            pH[j % 2][:],
                            wr1[:, (s0 + ki) * 128:(s0 + ki + 1) * 128],
                            xn1[:, ki * TB:(ki + 1) * TB],
                            start=(ki == 0), stop=(ki == 15)),
                            s_hmm if ki == 15 else None)
                    hmm_hist.append(s_hmm.n)
                    if cidx + 3 < 16:
                        load_wa(cidx + 3, war_tgt=s_hmm.n)
                    rec.wait("scalar", s_hmm)
                    rec.do("scalar", lambda E, j=j, tb=tb: E.activation(
                        out=hsb[:, j * TOK + tb * TB:j * TOK + (tb + 1) * TB],
                        in_=pH[j % 2][:], func=AF.Identity,
                        bias=bA_sb[:, j:j + 1]), s_hev)
                    ev_hist.append(s_hev.n)
                rec.wait("tensor", s_hev)
                if tb > 0:
                    rec.wait("tensor", s_kr)
                for ki in range(8):
                    rec.do("tensor", lambda E, ki=ki, tb=tb: E.matmul(
                        pKR[:], wb_bf[:, ki * DHR:(ki + 1) * DHR],
                        hsb[:, ki * TOK + tb * TB:ki * TOK + (tb + 1) * TB],
                        start=(ki == 0), stop=(ki == 7)),
                        s_kr if ki == 7 else None)
                rec.wait("scalar", s_kr)
                rec.do("scalar", lambda E: E.activation(
                    out=krraw[:], in_=pKR[:], func=AF.Identity, bias=bB_sb[:]),
                    s_kr)
                rec.wait("tensor", s_kr)
                rec.do("tensor", lambda E: E.matmul(
                    pRot[:], rt_sb[0:DHR, 0:DHR], krraw[:],
                    start=True, stop=True), s_kr)
                rec.wait("vector", s_kr)
                rec.dos("vector", lambda E, tb=tb: E.tensor_tensor(
                    krt1[:], krraw[:], cosk_sb[:, tb * TB:(tb + 1) * TB],
                    ALU.mult), sS["ch"])
                rec.dos("vector", lambda E, tb=tb: E.tensor_tensor(
                    krt2[:], pRot[:], sink_sb[:, tb * TB:(tb + 1) * TB],
                    ALU.mult), sS["ch"])
                rec.do("vector", lambda E, tb=tb: E.tensor_tensor(
                    hkr[:, tb * TB:(tb + 1) * TB], krt1[:], krt2[:], ALU.add),
                    s_kr)
            for j in range(8):
                rec.wait("gpsimd", s_hev, ev_hist[8 + j])
                rec.do("gpsimd", lambda E, j=j: E.dma_start(
                    out=hG_in.ap()[j * 128:(j + 1) * 128, :],
                    in_=hsb[:, j * TOK:(j + 1) * TOK]), s_hgw, 16)
            rec.wait("gpsimd", s_kr)
            rec.do("gpsimd", lambda E: E.dma_start(
                out=hG_in.ap()[1024:1088, :], in_=hkr[:]), s_hgw, 16)
            rec.wait("gpsimd", s_hgw)
            rec.do("gpsimd", lambda E: E.collective_compute(
                "AllGather", ALU.bypass, replica_groups=RG,
                ins=[hG_in.ap().opt()], outs=[hG_out.ap().opt()]), s_cc1, 1)
            t = dbg_out("hg", (NCORE * 1088, TOK), BF16)
            if t is not None:
                rec.wait("gpsimd", s_cc1, 1)
                rec.do("gpsimd", lambda E: E.dma_start(
                    out=t.ap(), in_=hG_out.ap()), mksem(), 16)
            rec.play()
            barrier(block)
            release(sS, s_x, rs_wld, s_xn, s_hmm, s_hev, s_kr, s_hgw, s_ld1)

        s_cc2 = mksem("cc2")
        with ExitStack() as stA:
            eA = stA.enter_context
            kf0 = eA(nc.sbuf_tensor("kf0", [128, HPC * T], BF16))
            kf1 = eA(nc.sbuf_tensor("kf1", [65, T], BF16))
            qf0 = eA(nc.sbuf_tensor("qf0", [128, HPC * T], BF16))
            qf1 = [eA(nc.sbuf_tensor(f"qf1_{h}", [65, T], BF16))
                   for h in range(HPC)]
            vf = eA(nc.sbuf_tensor("vf", [128, (T // 128) * 256], BF16))
            kmax_sb = eA(nc.sbuf_tensor("kmax", [1, 4], F32))
            qn_row = eA(nc.sbuf_tensor("qnrow", [1, TB], F32))

            wuq_bf = eA(nc.sbuf_tensor("wuqbf", [128, 4 * 256], BF16))
            wuk_bf = eA(nc.sbuf_tensor("wukbf", [128, 4 * 256], BF16))
            wuv_bf = eA(nc.sbuf_tensor("wuvbf", [128, 4 * 256], BF16))
            wcc_bf = eA(nc.sbuf_tensor("wccbf", [128, 8 * 128], BF16))
            with ExitStack() as st2:
                e2 = st2.enter_context
                stage = e2(nc.sbuf_tensor("p2stage", [128, 1024], F32))
                hgr = e2(nc.sbuf_tensor("hgr", [128, 16 * TB], BF16))
                trr = e2(nc.sbuf_tensor("trr", [128, 4 * TB], BF16))
                qrw = e2(nc.sbuf_tensor("qrw", [128, 2 * TB], BF16))
                rr1 = e2(nc.sbuf_tensor("rr1", [128, 2 * TB], BF16))
                rr2 = e2(nc.sbuf_tensor("rr2", [128, 2 * TB], BF16))
                nsq = e2(nc.sbuf_tensor("nsq", [128, 2 * TB], BF16))
                nsq2 = e2(nc.sbuf_tensor("nsq2", [64, 2 * TB], BF16))
                pP = [e2(nc.psum_tensor(f"p2p{i}", [128, TB], F32))
                      for i in range(4)]
                pR2 = e2(nc.psum_tensor("p2r", [128, TB], F32))
                pRot2 = e2(nc.psum_tensor("p2rot", [128, TB], F32))
                pV = [e2(nc.psum_tensor(f"p2v{i}", [128, 256], F32))
                      for i in range(2)]

                rs_hr = RingSem(mksem, 2)
                s_hfr = mksem()
                rs_tr = RingSem(mksem, 2)
                s_k1 = mksem()
                s_mm = mksem()
                s_ev = mksem()
                s_rp = mksem()
                s_nm = mksem()
                s_ns = mksem()
                s_nx = mksem()

                rec = Rec(block)
                s_st2 = mksem()
                s_ld2 = mksem()
                for nm, t, aa in [("wuq", wuq_bf, 4), ("wuk", wuk_bf, 4),
                                  ("wuv", wuv_bf, 4), ("wcc", wcc_bf, 8)]:
                    rec.wait("sync", s_st2)
                    rec.do("sync", lambda E, nm=nm, aa=aa: dma_tiles(
                        E, stage[:], d[nm].ap(), aa), s_ld2, 16)
                    rec.wait("vector", s_ld2)
                    rec.do("vector", lambda E, t=t: E.tensor_copy(
                        t[:], stage[:]), s_st2)
                rec.wait("tensor", s_st2)
                rec.wait("sync", s_cc1, 1)
                rec.wait("gpsimd", s_cc1, 1)
                for r in range(8):
                    rec.do("gpsimd", lambda E, r=r: E.dma_start(
                        out=kf1[0:64, r * TOK:(r + 1) * TOK],
                        in_=hG_out.ap()[1088 * r + 1024:1088 * r + 1088, :]),
                        s_k1, 16)
                rec.do("vector", lambda E: E.memset(kf1[64:65, :], 1.0),
                       s_k1, 1)

                evq = [0, 0]
                evk = [0, 0]
                evr = [0]
                evv = [0, 0]
                rpd = [0, 0]
                hfr_hist = [0]
                for tbi in range(16):
                    r, co = tbi // 2, (tbi % 2) * TB
                    hs = (tbi % 2) * 8
                    par = tbi % 2
                    if tbi >= 2:
                        rec.wait("sync", s_hfr, hfr_hist[tbi - 1])
                    hrsem = rs_hr.next()
                    for l in range(8):
                        rec.do("sync", lambda E, r=r, co=co, l=l, hs=hs:
                               E.dma_start(
                            out=hgr[:, (hs + l) * TB:(hs + l + 1) * TB],
                            in_=hG_out.ap()[1088 * r + 128 * l:
                                            1088 * r + 128 * (l + 1),
                                            co:co + TB]), hrsem, 16)
                    trsem = rs_tr.next()
                    rec.do("sync", lambda E, tbi=tbi, par=par: E.dma_start(
                        out=trr[:, par * 2 * TB:(par * 2 + 1) * TB],
                        in_=d["cosq"].ap()[tbi]), trsem, 16)
                    rec.do("sync", lambda E, tbi=tbi, par=par: E.dma_start(
                        out=trr[:, (par * 2 + 1) * TB:(par * 2 + 2) * TB],
                        in_=d["sinq"].ap()[tbi]), trsem, 16)
                    rec.wait("tensor", hrsem)
                    for h in range(HPC):
                        rec.wait("tensor", s_ev, evq[h])
                        for li in range(4):
                            rec.do("tensor", lambda E, h=h, li=li, hs=hs:
                                   E.matmul(
                                pP[h][:],
                                wuq_bf[:, li * 256 + h * 128:
                                       li * 256 + (h + 1) * 128],
                                hgr[:, (hs + 4 + li) * TB:(hs + 5 + li) * TB],
                                start=(li == 0), stop=(li == 3)),
                                s_mm if li == 3 else None)
                    for h in range(HPC):
                        rec.wait("tensor", s_ev, evk[h])
                        for li in range(4):
                            rec.do("tensor", lambda E, h=h, li=li, hs=hs:
                                   E.matmul(
                                pP[2 + h][:],
                                wuk_bf[:, li * 256 + h * 128:
                                       li * 256 + (h + 1) * 128],
                                hgr[:, (hs + li) * TB:(hs + li + 1) * TB],
                                start=(li == 0), stop=(li == 3)),
                                s_mm if li == 3 else None)
                    mm_q = s_mm.n
                    rec.wait("tensor", s_ev, evr[0])
                    order = [4, 5, 6, 7, 0, 1, 2, 3]
                    for idx, ki in enumerate(order):
                        free = ki >= 4
                        last = idx == 7
                        rec.do("tensor", lambda E, ki=ki, hs=hs, idx=idx:
                               E.matmul(
                            pR2[:], wcc_bf[:, ki * 128:(ki + 1) * 128],
                            hgr[:, (hs + ki) * TB:(hs + ki + 1) * TB],
                            start=(idx == 0), stop=(idx == 7)),
                            s_hfr if free else (s_mm if last else None))
                    mm_qr = s_mm.n
                    rec.wait("scalar", s_mm, mm_q)
                    for h in range(HPC):
                        rec.do("scalar", lambda E, h=h, tbi=tbi: E.activation(
                            out=qf0[:, h * T + tbi * TB:h * T + (tbi + 1) * TB],
                            in_=pP[h][:], func=AF.Copy, scale=SCALE), s_ev)
                        evq[h] = s_ev.n
                        rec.do("scalar", lambda E, h=h, tbi=tbi: E.activation(
                            out=kf0[:, h * T + tbi * TB:h * T + (tbi + 1) * TB],
                            in_=pP[2 + h][:], func=AF.Copy), s_ev)
                        evk[h] = s_ev.n
                    rec.wait("scalar", s_mm, mm_qr)
                    rec.wait("scalar", s_rp, rpd[par])
                    rec.do("scalar", lambda E, par=par: E.activation(
                        out=qrw[:, par * TB:(par + 1) * TB],
                        in_=pR2[:], func=AF.Identity, bias=bC_sb[:]), s_ev)
                    evr[0] = s_ev.n
                    ev_qr = s_ev.n
                    rec.wait("tensor", s_ev, ev_qr)
                    rec.wait("tensor", s_rp, rpd[par])
                    rec.do("tensor", lambda E, par=par: E.matmul(
                        pRot2[:], rt_sb[:], qrw[:, par * TB:(par + 1) * TB],
                        start=True, stop=True), s_mm)
                    mm_rot = s_mm.n
                    for tt in range(4):
                        rec.wait("tensor", s_ev, evv[tt % 2])
                        for li in range(4):
                            sem = None
                            if tt == 3:
                                sem = s_hfr
                            elif li == 3:
                                sem = s_mm
                            rec.do("tensor", lambda E, tt=tt, li=li, hs=hs:
                                   E.matmul(
                                pV[tt % 2][:],
                                hgr[:, (hs + li) * TB + tt * 128:
                                       (hs + li) * TB + (tt + 1) * 128],
                                wuv_bf[:, li * 256:(li + 1) * 256],
                                start=(li == 0), stop=(li == 3)), sem)
                        vt = s_mm.n if tt < 3 else s_hfr.n
                        vs = s_mm if tt < 3 else s_hfr
                        rec.wait("scalar", vs, vt)
                        rec.do("scalar", lambda E, tbi=tbi, tt=tt: E.activation(
                            out=vf[:, (tbi * 4 + tt) * 256:
                                   (tbi * 4 + tt + 1) * 256],
                            in_=pV[tt % 2][:], func=AF.Copy), s_ev)
                        evv[tt % 2] = s_ev.n
                    hfr_hist.append(s_hfr.n)
                    rec.wait("vector", trsem)
                    rec.wait("vector", s_mm, mm_rot)
                    rec.dos("vector", lambda E, par=par: E.tensor_tensor(
                        rr1[:, par * TB:(par + 1) * TB],
                        qrw[:, par * TB:(par + 1) * TB],
                        trr[:, par * 2 * TB:(par * 2 + 1) * TB], ALU.mult),
                        s_rp)
                    rec.dos("vector", lambda E, par=par: E.tensor_tensor(
                        rr2[:, par * TB:(par + 1) * TB], pRot2[:],
                        trr[:, (par * 2 + 1) * TB:(par * 2 + 2) * TB],
                        ALU.mult), s_rp)
                    rec.do("vector", lambda E, par=par, tbi=tbi:
                           E.tensor_tensor(
                        qf1[0][0:64, tbi * TB:(tbi + 1) * TB],
                        rr1[0:64, par * TB:(par + 1) * TB],
                        rr2[0:64, par * TB:(par + 1) * TB], ALU.add))
                    rec.do("vector", lambda E, par=par, tbi=tbi:
                           E.tensor_tensor(
                        qf1[1][0:64, tbi * TB:(tbi + 1) * TB],
                        rr1[64:128, par * TB:(par + 1) * TB],
                        rr2[64:128, par * TB:(par + 1) * TB], ALU.add), s_rp)
                    rpd[par] = s_rp.n

                rec.wait("scalar", s_ev)
                rec.wait("scalar", s_rp)
                rec.wait("scalar", s_k1)
                nm_hist = []
                for h in range(HPC):
                    for tbi in range(16):
                        seq = h * 16 + tbi
                        par = seq % 2
                        if seq >= 2:
                            rec.wait("scalar", s_nm, nm_hist[seq - 2])
                        rec.do("scalar", lambda E, h=h, tbi=tbi, par=par:
                               E.activation(
                            out=nsq[:, par * TB:(par + 1) * TB],
                            in_=kf0[:, h * T + tbi * TB:h * T + (tbi + 1) * TB],
                            func=AF.Square))
                        rec.do("scalar", lambda E, tbi=tbi, par=par:
                               E.activation(
                            out=nsq2[:, par * TB:(par + 1) * TB],
                            in_=kf1[0:64, tbi * TB:(tbi + 1) * TB],
                            func=AF.Square), s_ns)
                        rec.wait("tensor", s_ns)
                        if seq >= 2:
                            rec.wait("tensor", s_nx, seq - 1)
                        rec.do("tensor", lambda E, par=par: E.matmul(
                            pP[par][0:1, :], ones_bf[:, 0:1],
                            nsq[:, par * TB:(par + 1) * TB],
                            start=True, stop=False))
                        rec.do("tensor", lambda E, par=par: E.matmul(
                            pP[par][0:1, :], ones_bf[0:64, 0:1],
                            nsq2[:, par * TB:(par + 1) * TB],
                            start=False, stop=True), s_nm)
                        nm_hist.append(s_nm.n)
                        rec.wait("vector", s_nm)
                        rec.dos("vector", lambda E, par=par: E.reduce_max(
                            row_m[0:1, 0:1], pP[par][0:1, :], axis=AXX), s_nx)
                        if tbi == 0:
                            rec.dos("vector", lambda E, h=h: E.tensor_copy(
                                kmax_sb[0:1, 2 + h:3 + h], row_m[0:1, 0:1]),
                                s_nx)
                        else:
                            rec.dos("vector", lambda E, h=h: E.tensor_tensor(
                                kmax_sb[0:1, 2 + h:3 + h],
                                kmax_sb[0:1, 2 + h:3 + h],
                                row_m[0:1, 0:1], ALU.max), s_nx)
                rec.wait("scalar", s_nx)
                rec.do("scalar", lambda E: E.sqrt(
                    out=kmax_sb[0:1, 0:2], in_=kmax_sb[0:1, 2:4]), s_ns)
                km_t = s_ns.n
                for h in range(HPC):
                    for tbi in range(16):
                        seq = h * 16 + tbi
                        par = seq % 2
                        if seq >= 2:
                            rec.wait("scalar", s_nm)
                        rec.do("scalar", lambda E, h=h, tbi=tbi, par=par:
                               E.activation(
                            out=nsq[:, par * TB:(par + 1) * TB],
                            in_=qf0[:, h * T + tbi * TB:h * T + (tbi + 1) * TB],
                            func=AF.Square))
                        rec.do("scalar", lambda E, h=h, tbi=tbi, par=par:
                               E.activation(
                            out=nsq2[:, par * TB:(par + 1) * TB],
                            in_=qf1[h][0:64, tbi * TB:(tbi + 1) * TB],
                            func=AF.Square), s_ns)
                        rec.wait("tensor", s_ns)
                        if seq >= 1:
                            rec.wait("tensor", s_ns, km_t + 2 * seq)
                        rec.do("tensor", lambda E: E.matmul(
                            pP[2][0:1, :], ones_bf[:, 0:1],
                            nsq[:, (seq % 2) * TB:(seq % 2 + 1) * TB],
                            start=True, stop=False))
                        rec.do("tensor", lambda E: E.matmul(
                            pP[2][0:1, :], ones_bf[0:64, 0:1],
                            nsq2[:, (seq % 2) * TB:(seq % 2 + 1) * TB],
                            start=False, stop=True), s_nm)
                        rec.wait("scalar", s_nm)
                        rec.wait("scalar", s_nx)
                        rec.do("scalar", lambda E: E.sqrt(
                            out=qn_row[:], in_=pP[2][0:1, :]), s_ns)
                        rec.wait("vector", s_ns)
                        rec.do("vector", lambda E, h=h, tbi=tbi:
                               E.tensor_scalar(
                            qf1[h][64:65, tbi * TB:(tbi + 1) * TB],
                            qn_row[:], kmax_sb[0:1, h:h + 1], CMARGIN,
                            ALU.mult, ALU.mult), s_nx)
                rec.play()

            barrier(block)
            release(rs_hr, s_hfr, rs_tr, s_k1, s_mm, s_ev, s_rp, s_nm, s_ns,
                    s_nx, s_st2, s_ld2)

            with ExitStack() as st3:
                e3 = st3.enter_context
                cvout = e3(nc.sbuf_tensor("cvout", [128, 4 * 2048], BF16))
                maskd_sb = e3(nc.sbuf_tensor("masksb", [128, 4 * TB], BF16))
                ebuf = e3(nc.sbuf_tensor("ebuf", [128, 2 * TB], BF16))
                osb = e3(nc.sbuf_tensor("osb", [128, 2 * TB], BF16))
                otile = e3(nc.sbuf_tensor("otile", [128, 2 * TB], BF16))
                dsb = e3(nc.sbuf_tensor("dsb", [1, 2 * TB], F32))
                rsb = e3(nc.sbuf_tensor("rsb", [1, 2 * TB], F32R))
                pS = [e3(nc.psum_tensor(f"ps{i}", [128, TB], F32))
                      for i in range(2)]
                pO = [e3(nc.psum_tensor(f"po{i}", [128, TB], F32))
                      for i in range(2)]
                pDen = [e3(nc.psum_tensor(f"pdn{i}", [1, TB], F32))
                        for i in range(2)]
                pRb2 = [e3(nc.psum_tensor(f"prb2{i}", [128, TB], F32))
                        for i in range(2)]

                s_s = mksem()
                s_e = mksem()
                s_pv = mksem()
                s_dn = mksem()
                s_rc = mksem()
                s_rb = mksem()
                s_oe = mksem()
                s_oo = mksem()
                s_ow = mksem()
                rs_ow = RingSem(mksem, 2)
                ow_t = {}

                rec = Rec(block)
                s_mk = mksem()
                rec.do("sync", lambda E: E.dma_start(
                    out=maskd_sb[:], in_=d["maskd"].ap()), s_mk, 16)
                rec.wait("vector", s_mk)
                cv_jobs = []
                for f in range(16):
                    cv_jobs.append((d["wzT"].ap()[f], wzB.ap()[f], 16))
                for j in range(88):
                    cv_jobs.append((d["sguT"].ap()[j], sguB.ap()[j], 16))
                for fq in range(4):
                    for i in range(44):
                        cv_jobs.append((d["sdwT"].ap()[fq, i],
                                        sdwB.ap()[fq, i], 4))
                for en in range(NE):
                    for j in range(22):
                        cv_jobs.append((d["w13T"].ap()[en, j],
                                        w13B.ap()[en, j], 16))
                    for fq in range(4):
                        for i in range(11):
                            cv_jobs.append((d["w2T"].ap()[en, fq, i],
                                            w2B.ap()[en, fq, i], 4))
                rs_cvi = RingSem(mksem, 4)
                rs_cvo = RingSem(mksem, 4)
                cv_state = {"i": 0, "out": [None] * 4}

                def emit_cv(njobs):
                    for _ in range(njobs):
                        ci = cv_state["i"]
                        if ci >= len(cv_jobs):
                            return
                        cv_state["i"] += 1
                        srcap, dstap, nt = cv_jobs[ci]
                        par = ci % 4
                        w = nt * 128
                        isem = rs_cvi.next()
                        if cv_state["out"][par] is not None:
                            osem, otgt = cv_state["out"][par]
                            rec.wait("gpsimd", osem, otgt)
                        itgt = rec.do("gpsimd", lambda E, s=srcap, par=par,
                                      w=w, nt=nt: dma_tiles(
                            E, cvout[:, par * 2048:par * 2048 + w], s, nt),
                            isem, 16)
                        rec.wait("sync", isem, itgt)
                        osem = rs_cvo.next()
                        otgt = rec.do("sync", lambda E, dd=dstap, par=par,
                                      w=w, nt=nt: E.dma_start(
                            out=dd.rearrange("a p f -> p a f"),
                            in_=cvout[:, par * 2048:par * 2048 + w]
                            .rearrange("p (a f) -> p a f", a=nt)), osem, 16)
                        cv_state["out"][par] = (osem, otgt)

                pairs = [(h, qb) for h in range(HPC) for qb in range(NQB)]
                e_t = {}
                pv_t = {}
                oe_t = {}
                oo_t = {}
                rc_t = {}
                rb_t = {}
                dn_t = {}
                g = 0
                for p, (h, qb) in enumerate(pairs):
                    emit_cv(13)
                    nk = 4 * qb + 4
                    par = p % 2
                    for kt in range(nk):
                        sl = g % 2
                        if g >= 2:
                            rec.wait("tensor", s_e, e_t[g - 2])
                        rec.do("tensor", lambda E, h=h, kt=kt, qb=qb, sl=sl:
                               E.matmul(
                            pS[sl][:],
                            kf0[:, h * T + kt * 128:h * T + (kt + 1) * 128],
                            qf0[:, h * T + qb * TB:h * T + (qb + 1) * TB],
                            start=True, stop=False))
                        rec.do("tensor", lambda E, h=h, kt=kt, sl=sl, qb=qb:
                               E.matmul(
                            pS[sl][:], kf1[:, kt * 128:(kt + 1) * 128],
                            qf1[h][:, qb * TB:(qb + 1) * TB],
                            start=False, stop=True), s_s)
                        rec.wait("scalar", s_s)
                        if g >= 2:
                            rec.wait("scalar", s_pv, pv_t[g - 2])
                        rec.do("scalar", lambda E, sl=sl: E.activation(
                            out=ebuf[:, sl * TB:(sl + 1) * TB],
                            in_=pS[sl][:], func=AF.Exp), s_e)
                        e_t[g] = s_e.n
                        if kt >= nk - 4:
                            rel = kt - (nk - 4)
                            rec.wait("vector", s_e, e_t[g])
                            rec.do("vector", lambda E, sl=sl, rel=rel:
                                   E.tensor_tensor(
                                ebuf[:, sl * TB:(sl + 1) * TB],
                                ebuf[:, sl * TB:(sl + 1) * TB],
                                maskd_sb[:, rel * TB:(rel + 1) * TB],
                                ALU.mult), s_e)
                            e_t[g] = s_e.n
                        rec.wait("tensor", s_e, e_t[g])
                        if kt == 0 and p >= 2:
                            rec.wait("tensor", s_oe, oe_t[p - 2])
                            rec.wait("tensor", s_dn, dn_t[p - 2])
                        rec.do("tensor", lambda E, kt=kt, h=h, sl=sl, nk=nk,
                               par=par: E.matmul(
                            pO[par][:],
                            vf[:, kt * 256 + h * 128:kt * 256 + (h + 1) * 128],
                            ebuf[:, sl * TB:(sl + 1) * TB],
                            start=(kt == 0), stop=(kt == nk - 1)))
                        rec.do("tensor", lambda E, kt=kt, sl=sl, nk=nk,
                               par=par: E.matmul(
                            pDen[par][0:1, :], ones_bf[:, 0:1],
                            ebuf[:, sl * TB:(sl + 1) * TB],
                            start=(kt == 0), stop=(kt == nk - 1)), s_pv)
                        pv_t[g] = s_pv.n
                        g += 1
                    emit_cv(13)
                    rec.wait("scalar", s_pv, pv_t[g - 1])
                    if p >= 2:
                        rec.wait("scalar", s_rc, rc_t[p - 2])
                    rec.do("scalar", lambda E, par=par: E.activation(
                        out=dsb[0:1, par * TB:(par + 1) * TB],
                        in_=pDen[par][0:1, :], func=AF.Copy), s_dn)
                    dn_t[p] = s_dn.n
                    if p >= 2:
                        rec.wait("scalar", s_oo, oo_t[p - 2])
                    rec.do("scalar", lambda E, par=par: E.activation(
                        out=osb[:, par * TB:(par + 1) * TB],
                        in_=pO[par][:], func=AF.Copy), s_oe)
                    oe_t[p] = s_oe.n
                    rec.wait("vector", s_dn, dn_t[p])
                    if p >= 2:
                        rec.wait("vector", s_rb, rb_t[p - 2])
                    rec.do("vector", lambda E, par=par: E.reciprocal(
                        rsb[0:1, par * TB:(par + 1) * TB],
                        dsb[0:1, par * TB:(par + 1) * TB]), s_rc)
                    rc_t[p] = s_rc.n
                    rec.wait("tensor", s_rc, rc_t[p])
                    if p >= 2:
                        rec.wait("tensor", s_oo, oo_t[p - 2])
                    rec.do("tensor", lambda E, par=par: E.matmul(
                        pRb2[par][:], ones_r[0:1, :],
                        rsb[0:1, par * TB:(par + 1) * TB],
                        start=True, stop=True), s_rb)
                    rb_t[p] = s_rb.n
                    rec.wait("vector", s_oe, oe_t[p])
                    rec.wait("vector", s_rb, rb_t[p])
                    if p >= 2:
                        sem_prev, tgt_prev = ow_t[p - 2]
                        rec.wait("vector", sem_prev, tgt_prev)
                    rec.do("vector", lambda E, par=par: E.tensor_tensor(
                        otile[:, par * TB:(par + 1) * TB],
                        osb[:, par * TB:(par + 1) * TB],
                        pRb2[par][:], ALU.mult), s_oo)
                    oo_t[p] = s_oo.n
                    rec.wait("gpsimd", s_oo, oo_t[p])
                    owsem = rs_ow.next()
                    tgt = rec.do("gpsimd", lambda E, h=h, qb=qb, par=par:
                           E.dma_start(
                        out=oG_in.ap()[h * 128:(h + 1) * 128,
                                       qb * TB:(qb + 1) * TB],
                        in_=otile[:, par * TB:(par + 1) * TB]), owsem, 16)
                    ow_t[p] = (owsem, tgt)
                for sem in rs_ow.sems:
                    rec.wait("gpsimd", sem)
                rec.do("gpsimd", lambda E: E.collective_compute(
                    "AllGather", ALU.bypass, replica_groups=RG,
                    ins=[oG_in.ap().opt()], outs=[oG_out.ap().opt()]),
                    s_cc2, 1)
                t = dbg_out("og", (NH * DH, T), BF16)
                if t is not None:
                    rec.wait("gpsimd", s_cc2, 1)
                    rec.do("gpsimd", lambda E: E.dma_start(
                        out=t.ap(), in_=oG_out.ap()), mksem(), 16)
                rec.play()
                p2b_rel = [s_s, s_e, s_pv, s_dn, s_rc, s_rb, s_oe, s_oo,
                           s_ow, rs_ow, s_mk, rs_cvi, rs_cvo]

            barrier(block)
            release(*p2b_rel)

        with ExitStack() as stB:
            eB = stB.enter_context
            x2_sb = eB(nc.sbuf_tensor("x2sb", [128, 16 * TOK], BF16))
            s_x2 = mksem()

            with ExitStack() as st4:
                e4 = st4.enter_context
                ogb = e4(nc.sbuf_tensor("ogb", [128, 16 * TB], BF16))
                xsb2 = e4(nc.sbuf_tensor("xsb2", [128, 16 * TB], F32R))
                zsb = e4(nc.sbuf_tensor("zsb", [128, 16 * TB], F32R))
                zsq = e4(nc.sbuf_tensor("zsq", [128, 4 * TB], BF16))
                ztmp = e4(nc.sbuf_tensor("ztmp", [128, 2 * TB], F32R))
                ztmp2 = e4(nc.sbuf_tensor("ztmp2", [128, 2 * TB], F32R))
                wr2 = e4(nc.sbuf_tensor("wr2", [128, 32 * 128], BF16))
                pZ = [e4(nc.psum_tensor(f"pz{i}", [128, TB], F32))
                      for i in range(2)]
                pSX2 = e4(nc.psum_tensor("psx2", [1, TB], F32))
                pSQ2 = e4(nc.psum_tensor("psq2", [1, TB], F32))
                pRb3 = e4(nc.psum_tensor("prb3", [128, TB], F32))
                pMrb3 = e4(nc.psum_tensor("pmrb3", [128, TB], F32))

                sZ = {k: mksem() for k in ("sq", "st", "dv", "sd", "bc", "ch")}
                s_zl = mksem()
                s_zc = mksem()
                rs_zw = RingSem(mksem, 2)
                s_zm = mksem()
                s_ze = mksem()

                rec = Rec(block)
                rec.wait("sync", s_cc2, 1)
                cidval = [None]

                def ld_cid(E):
                    cidval[0] = E.value_load(cid_sb[:])
                rec.do("sync", ld_cid)

                zm_hist = []
                ze_hist = []
                chz = {}

                def load_wz(cidx, war_tgt=None):
                    slot0 = (cidx % 2) * 16
                    f = cidx % 16
                    if war_tgt is not None:
                        rec.wait("sync", s_zm, war_tgt)
                    sem = rs_zw.next()
                    tgt = rec.do("sync", lambda E, f=f, s0=slot0: dma_tiles(
                        E, wr2[:, s0 * 128:(s0 + 16) * 128],
                        wzB.ap()[f], 16), sem, 16)
                    chz[cidx] = (slot0, sem, tgt)

                for cidx in range(2):
                    load_wz(cidx)
                for tb in range(NTB):
                    if tb > 0:
                        rec.wait("sync", s_zm)
                        rec.wait("sync", s_x2)
                        rec.wait("sync", sZ["st"])
                    rec.do("sync", lambda E, tb=tb: dma_cols(
                        E, ogb[:],
                        oG_out.ap()[:, bass.ts(cidval[0] * 2 + tb, TB)]),
                        s_zl, 16)
                    rec.do("sync", lambda E, tb=tb: dma_cols(
                        E, xsb2[:], d["xT"].ap()[:, tb * TB:(tb + 1) * TB]),
                        s_zl, 16)
                    rec.wait("tensor", s_zl)
                    if tb > 0:
                        rec.wait("scalar", s_x2)
                    for f in range(16):
                        cidx = tb * 16 + f
                        slot0, zwsem, tgt = chz[cidx]
                        rec.wait("tensor", zwsem, tgt)
                        if cidx >= 2:
                            rec.wait("tensor", s_ze, ze_hist[cidx - 2])
                        for kin in range(16):
                            rec.do("tensor", lambda E, s0=slot0, kin=kin, f=f:
                                   E.matmul(
                                pZ[f % 2][:],
                                wr2[:, (s0 + kin) * 128:(s0 + kin + 1) * 128],
                                ogb[:, kin * TB:(kin + 1) * TB],
                                start=(kin == 0), stop=(kin == 15)),
                                s_zm if kin == 15 else None)
                        zm_hist.append(s_zm.n)
                        if cidx + 2 < 32:
                            load_wz(cidx + 2, war_tgt=s_zm.n)
                        rec.wait("scalar", s_zm)
                        rec.do("scalar", lambda E, f=f: E.activation(
                            out=zsb[:, f * TB:(f + 1) * TB],
                            in_=pZ[f % 2][:], func=AF.Identity,
                            bias=bZ_sb[:, f:f + 1]), s_ze)
                        ze_hist.append(s_ze.n)
                    rec.wait("tensor", s_ze)
                    if tb > 0:
                        rec.wait("tensor", s_x2)

                    def apply2(rec, a, pRb, pMrb, tb=tb):
                        rec.dos("vector", lambda E, a=a: E.tensor_tensor(
                            ztmp[:, (a % 2) * TB:(a % 2 + 1) * TB],
                            zsb[:, a * TB:(a + 1) * TB], pRb, ALU.mult),
                            sZ["ch"])
                        rec.dos("vector", lambda E, a=a: E.tensor_tensor(
                            ztmp2[:, (a % 2) * TB:(a % 2 + 1) * TB],
                            ztmp[:, (a % 2) * TB:(a % 2 + 1) * TB], pMrb,
                            ALU.subtract), sZ["ch"])
                        rec.do("vector", lambda E, a=a, tb=tb: E.tensor_tensor(
                            x2_sb[:, a * TOK + tb * TB:a * TOK + (tb + 1) * TB],
                            xsb2[:, a * TB:(a + 1) * TB],
                            ztmp2[:, (a % 2) * TB:(a % 2 + 1) * TB], ALU.add),
                            s_x2)

                    emit_ln(rec, sZ,
                            dict(ones_x=ones_r[:, 0:1], ones_sq=ones_bf[:, 0:1],
                                 ones_row=ones_r[0:1, :],
                                 pSX=pSX2[0:1, :], pSQ=pSQ2[0:1, :],
                                 pRb=pRb3[:], pMrb=pMrb3[:], rows=ROWS),
                            lambda a: zsb[:, a * TB:(a + 1) * TB],
                            lambda a: zsq[:, (a % 4) * TB:(a % 4 + 1) * TB],
                            apply2)
                t = dbg_out("x2", (128, 16 * TOK), BF16)
                if t is not None:
                    rec.wait("gpsimd", s_x2)
                    rec.do("gpsimd", lambda E: E.dma_start(
                        out=t.ap(), in_=x2_sb[:]), mksem(), 16)
                rec.play()
                p2c_rel = [sZ, s_zl, s_zc, rs_zw, s_zm, s_ze]

            barrier(block)
            release(*p2c_rel)

            with ExitStack() as st5:
                e5 = st5.enter_context
                xn3 = e5(nc.sbuf_tensor("xn3", [128, 16 * TB], BF16))
                x2sq = e5(nc.sbuf_tensor("x2sq", [128, 4 * TB], BF16))
                fsq = e5(nc.sbuf_tensor("fsq", [128, 4 * TB], BF16))
                ftmp = e5(nc.sbuf_tensor("ftmp", [128, 2 * TB], F32R))
                ftmp2 = e5(nc.sbuf_tensor("ftmp2", [128, 2 * TB], F32R))
                facc = e5(nc.sbuf_tensor("facc", [128, 16 * TB], F32R))
                sh_e = e5(nc.sbuf_tensor("sh_e", [128, 11 * TB], BF16))
                asb = e5(nc.sbuf_tensor("asb", [128, 2 * TB], BF16))
                bsb = e5(nc.sbuf_tensor("bsb", [128, 2 * TB], BF16))
                shtile = e5(nc.sbuf_tensor("shtile", [128, 2 * TB], BF16))
                shr = e5(nc.sbuf_tensor("shr", [128, 4 * TB], BF16))
                wr3 = e5(nc.sbuf_tensor("wr3", [128, 96 * 128], BF16))
                sigb = e5(nc.sbuf_tensor("sigb", [128, TB], F32))
                wb_all = e5(nc.sbuf_tensor("wb_all", [128, NE * TB], F32))
                lgsb = e5(nc.sbuf_tensor("lgsb", [NE, 4 * 128], BF16))
                ltok = e5(nc.sbuf_tensor("ltok", [128, 4 * NE], BF16))
                wtok = e5(nc.sbuf_tensor("wtok", [128, 4 * NE], BF16))
                wrow = e5(nc.sbuf_tensor("wrow", [NE, 4 * 128], BF16))
                wexp = e5(nc.sbuf_tensor("wexp", [1, TB], BF16))
                rsc = e5(nc.sbuf_tensor("rsc", [128, 40], F32))
                outsb = e5(nc.sbuf_tensor("outsb", [128, 2 * TB], F32))
                sig_row = e5(nc.sbuf_tensor("sigrow", [1, TB], F32R))
                pb = [e5(nc.psum_tensor(f"pb{i}", [128, TB], F32))
                      for i in range(6)]
                pTrB = e5(nc.psum_tensor("ptrb", [128, TB], BF16))

                sL3 = {k: mksem() for k in ("sq", "st", "dv", "sd", "bc", "ch")}
                sL4 = {k: mksem() for k in ("sq", "st", "dv", "sd", "bc", "ch")}
                rs_w3 = RingSem(mksem, 9)
                s_w3f = mksem()
                rs_shr = RingSem(mksem, 4)
                rs_spl = RingSem(mksem, 2)
                s_xn3 = mksem()
                s_rt = mksem()
                s_wex = mksem()
                s_sg = mksem()
                s_sl = mksem()
                s_sh = mksem()
                s_spl = mksem()
                s_shr = mksem()
                s_fa = mksem()
                s_wb = mksem()
                s_out = mksem()

                rec = Rec(block)
                t_xn3 = dbg_out("xn3", (NTB, 128, 16 * TB), F32R)
                t_wts = dbg_out("wts", (NTB, NE, 4 * 128), BF16)
                t_ltok = dbg_out("ltok", (NTB, 128, 4 * NE), BF16)
                t_wtok = dbg_out("wtok", (NTB, 128, 4 * NE), BF16)
                t_rsc = dbg_out("rsc", (NTB, 128, 40), F32)
                t_shs = dbg_out("shs", (44, 128, TB), F32R)
                t_shrd = dbg_out("shrd", (128, 4 * TB), F32R)
                t_fq0 = dbg_out("fq0", (128, 4 * TB), F32R)
                t_sgb = dbg_out("sgb", (128, TB), F32)
                t_sig = dbg_out("sig", (NTB, 128, TB), F32)
                t_fsh = dbg_out("fsh", (NTB, 128, 16 * TB), F32R)
                t_fac = dbg_out("fac", (NTB, 128, 16 * TB), F32R)
                s_dbg = mksem()
                ring_pos = [0]
                ring_owner = [0] * 96
                ring_seq = [0]
                free_tgt = {}

                def r3_load(src_fn, n):
                    if ring_pos[0] + n > 96:
                        ring_pos[0] = 0
                    s0 = ring_pos[0]
                    ring_pos[0] += n
                    prev = max(ring_owner[s0:s0 + n])
                    ring_seq[0] += 1
                    seq = ring_seq[0]
                    for s in range(s0, s0 + n):
                        ring_owner[s] = seq
                    if prev > 0:
                        rec.wait("sync", s_w3f, free_tgt[prev])
                    if seq > 8:
                        rec.wait("sync", s_w3f, seq - 8)
                    sem = rs_w3.next()
                    tgt = rec.do("sync", lambda E, s0=s0, n=n, src_fn=src_fn:
                                 dma_tiles(
                        E, wr3[:, s0 * 128:(s0 + n) * 128], src_fn(), n),
                        sem, 16)
                    return seq, s0, (sem, tgt)

                sl_par = [0, 0]
                sh_par = [0, 0]
                spl_hist = [None] * 44
                out_hist = []

                for tb in range(NTB):
                    rec.wait("scalar", s_x2)
                    rec.wait("tensor", s_x2)
                    if tb > 0:
                        rec.wait("tensor", s_xn3)
                        rec.wait("vector", s_w3f)
                        rec.wait("scalar", sL3["st"])

                    def apply3(rec, a, pRb, pMrb, tb=tb):
                        rec.dos("vector", lambda E, a=a, tb=tb: E.tensor_tensor(
                            ftmp[:, (a % 2) * TB:(a % 2 + 1) * TB],
                            x2_sb[:, a * TOK + tb * TB:a * TOK + (tb + 1) * TB],
                            pRb, ALU.mult), sL3["ch"])
                        rec.do("vector", lambda E, a=a: E.tensor_tensor(
                            xn3[:, a * TB:(a + 1) * TB],
                            ftmp[:, (a % 2) * TB:(a % 2 + 1) * TB], pMrb,
                            ALU.subtract), s_xn3)

                    emit_ln(rec, sL3,
                            dict(ones_x=ones_bf[:, 0:1], ones_sq=ones_bf[:, 0:1],
                                 ones_row=ones_r[0:1, :],
                                 pSX=pb[0][0:1, :], pSQ=pb[1][0:1, :],
                                 pRb=pb[2][:], pMrb=pb[3][:], rows=ROWS),
                            lambda a, tb=tb: x2_sb[:, a * TOK + tb * TB:
                                                   a * TOK + (tb + 1) * TB],
                            lambda a: x2sq[:, (a % 4) * TB:(a % 4 + 1) * TB],
                            apply3)
                    if t_xn3 is not None:
                        rec.wait("gpsimd", s_xn3)
                        rec.do("gpsimd", lambda E, tb=tb: E.dma_start(
                            out=t_xn3.ap()[tb], in_=xn3[:]), s_dbg, 16)
                    rec.wait("tensor", s_xn3)
                    for a in range(16):
                        rec.do("tensor", lambda E, a=a: E.matmul(
                            pb[4][0:NE, :], gw_sb[:, a * NE:(a + 1) * NE],
                            xn3[:, a * TB:(a + 1) * TB],
                            start=(a == 0), stop=(a == 15)))
                        rec.do("tensor", lambda E, a=a: E.matmul(
                            pb[5][0:1, :], sgw_sb[:, a:a + 1],
                            xn3[:, a * TB:(a + 1) * TB],
                            start=(a == 0), stop=(a == 15)),
                            s_rt if a == 15 else None)
                    rec.wait("scalar", s_rt)
                    rec.do("scalar", lambda E: E.activation(
                        out=lgsb[0:NE, :], in_=pb[4][0:NE, 0:4 * 128],
                        func=AF.Copy), s_rt)
                    rec.do("scalar", lambda E: E.activation(
                        out=sig_row[:], in_=pb[5][0:1, :], func=AF.Sigmoid),
                        s_sg)
                    rec.wait("tensor", s_rt)
                    for tt in range(4):
                        rec.do("tensor", lambda E, tt=tt: E.transpose(
                            pTrB[:, tt * NE:(tt + 1) * NE],
                            lgsb[0:NE, tt * 128:(tt + 1) * 128],
                            eye_sb[0:NE, 0:NE]),
                            s_rt if tt == 3 else None)
                    rec.wait("scalar", s_rt)
                    rec.dos("scalar", lambda E: E.activation(
                        out=ltok[:], in_=pTrB[:, 0:4 * NE], func=AF.Copy),
                        s_rt)
                    if t_ltok is not None:
                        rec.wait("gpsimd", s_rt)
                        rec.do("gpsimd", lambda E, tb=tb: E.dma_start(
                            out=t_ltok.ap()[tb], in_=ltok[:]), s_dbg, 16)
                    for tt in range(4):
                        sl = slice(tt * NE, (tt + 1) * NE)
                        rec.wait("vector", s_rt)
                        rec.dos("vector", lambda E, sl=sl: E.reduce_max(
                            rsc[:, 0:1], ltok[:, sl], axis=AXX), s_rt)
                        rec.do("vector", lambda E: E.tensor_scalar(
                            rsc[:, 1:2], rsc[:, 0:1], -1.0, None, ALU.mult),
                            s_rt)
                        rec.wait("scalar", s_rt)
                        rec.do("scalar", lambda E, sl=sl: E.activation(
                            out=rsc[:, 8:16], in_=ltok[:, sl], func=AF.Exp,
                            bias=rsc[:, 1:2]), s_rt)
                        rec.wait("vector", s_rt)
                        rec.dos("vector", lambda E: E.reduce_sum(
                            rsc[:, 2:3], rsc[:, 8:16], axis=AXX), s_rt)
                        rec.dos("vector", lambda E: E.reciprocal(
                            rsc[:, 3:4], rsc[:, 2:3]), s_rt)
                        rec.dos("vector", lambda E: E.tensor_scalar(
                            rsc[:, 16:24], rsc[:, 8:16], rsc[:, 3:4], None,
                            ALU.mult), s_rt)
                        rec.dos("vector", lambda E: E.reduce_max(
                            rsc[:, 4:5], rsc[:, 16:24], axis=AXX), s_rt)
                        rec.dos("vector", lambda E: E.tensor_scalar(
                            rsc[:, 24:32], rsc[:, 16:24], rsc[:, 4:5], None,
                            ALU.is_equal), s_rt)
                        rec.dos("vector", lambda E: E.tensor_scalar(
                            rsc[:, 24:32], rsc[:, 24:32], -1.0, 1.0,
                            ALU.mult, ALU.add), s_rt)
                        rec.dos("vector", lambda E: E.tensor_tensor(
                            rsc[:, 32:40], rsc[:, 24:32], rsc[:, 16:24],
                            ALU.mult), s_rt)
                        rec.dos("vector", lambda E: E.reduce_max(
                            rsc[:, 5:6], rsc[:, 32:40], axis=AXX), s_rt)
                        rec.dos("vector", lambda E: E.tensor_scalar(
                            rsc[:, 24:32], rsc[:, 16:24], rsc[:, 5:6], None,
                            ALU.is_ge), s_rt)
                        rec.do("vector", lambda E, tt=tt: E.tensor_tensor(
                            wtok[:, tt * NE:(tt + 1) * NE], rsc[:, 16:24],
                            rsc[:, 24:32], ALU.mult), s_rt)
                    if t_wtok is not None:
                        rec.wait("gpsimd", s_rt)
                        rec.do("gpsimd", lambda E, tb=tb: E.dma_start(
                            out=t_wtok.ap()[tb], in_=wtok[:]), s_dbg, 16)
                    if t_rsc is not None:
                        rec.wait("gpsimd", s_rt)
                        rec.do("gpsimd", lambda E, tb=tb: E.dma_start(
                            out=t_rsc.ap()[tb], in_=rsc[:]), s_dbg, 16)
                    rec.wait("tensor", s_rt)
                    for tt in range(4):
                        rec.do("tensor", lambda E, tt=tt: E.transpose(
                            pTrB[0:NE, tt * 128:(tt + 1) * 128],
                            wtok[:, tt * NE:(tt + 1) * NE], eye_sb[:]),
                            s_rt if tt == 3 else None)
                    rec.wait("scalar", s_rt)
                    rec.do("scalar", lambda E: E.activation(
                        out=wrow[:], in_=pTrB[0:NE, 0:4 * 128], func=AF.Copy),
                        s_rt)
                    rec.wait("gpsimd", s_rt)
                    if t_wts is not None:
                        rec.do("gpsimd", lambda E, tb=tb: E.dma_start(
                            out=t_wts.ap()[tb], in_=wrow[:]), s_dbg, 16)
                    rec.wait("tensor", s_sg)
                    rec.wait("tensor", s_wb)
                    rec.do("tensor", lambda E: E.matmul(
                        pb[4][:], ones_r[0:1, :], sig_row[:],
                        start=True, stop=True), s_sg)
                    rec.wait("scalar", s_sg)
                    rec.do("scalar", lambda E: E.activation(
                        out=sigb[:], in_=pb[4][:], func=AF.Copy), s_sg)
                    sig_done = s_sg.n
                    if t_sig is not None:
                        rec.wait("gpsimd", s_sg, sig_done)
                        rec.do("gpsimd", lambda E, tb=tb: E.dma_start(
                            out=t_sig.ap()[tb], in_=sigb[:]), s_dbg, 16)
                    for en in range(NE):
                        if s_wb.n > 0:
                            rec.wait("gpsimd", s_wb)
                        rec.do("gpsimd", lambda E, en=en: E.dma_start(
                            out=wexp[0:1, :], in_=wrow[en:en + 1, :]),
                            s_wex, 16)
                        rec.wait("tensor", s_wex)
                        rec.wait("tensor", s_wb)
                        if en == 0:
                            rec.wait("tensor", s_sg, sig_done)
                        rec.do("tensor", lambda E: E.matmul(
                            pb[4][:], ones_bf[0:1, :], wexp[0:1, :],
                            start=True, stop=True), s_wb)
                        rec.wait("scalar", s_wb)
                        rec.do("scalar", lambda E, en=en: E.activation(
                            out=wb_all[:, en * TB:(en + 1) * TB],
                            in_=pb[4][:], func=AF.Copy), s_wb)
                    wb_done_all = s_wb.n

                    for j in range(44):
                        _, s0g, tg = r3_load(
                            lambda j=j: sguB.ap()[j], 16)
                        seqg = ring_seq[0]
                        _, s0u, tu = r3_load(
                            lambda j=j: sguB.ap()[44 + j], 16)
                        sequ = ring_seq[0]
                        par = j % 2
                        rec.wait("tensor", tg[0], tg[1])
                        rec.wait("tensor", tu[0], tu[1])
                        if j >= 2:
                            rec.wait("tensor", s_sl, sl_par[par])
                            rec.wait("tensor", s_sh, sh_par[par])
                        if j == 0:
                            rec.wait("tensor", s_sg, sig_done)
                        for a in range(16):
                            rec.do("tensor", lambda E, s0=s0g, a=a, par=par:
                                   E.matmul(
                                pb[0 + par][:],
                                wr3[:, (s0 + a) * 128:(s0 + a + 1) * 128],
                                xn3[:, a * TB:(a + 1) * TB],
                                start=(a == 0), stop=(a == 15)),
                                s_w3f if a == 15 else None)
                        free_tgt[seqg] = s_w3f.n
                        for a in range(16):
                            rec.do("tensor", lambda E, s0=s0u, a=a, par=par:
                                   E.matmul(
                                pb[2 + par][:],
                                wr3[:, (s0 + a) * 128:(s0 + a + 1) * 128],
                                xn3[:, a * TB:(a + 1) * TB],
                                start=(a == 0), stop=(a == 15)),
                                s_w3f if a == 15 else None)
                        free_tgt[sequ] = s_w3f.n
                        rec.wait("scalar", s_w3f)
                        if j >= 2:
                            rec.wait("scalar", s_sh, sh_par[par])
                        rec.do("scalar", lambda E, par=par: E.activation(
                            out=asb[:, par * TB:(par + 1) * TB],
                            in_=pb[0 + par][:], func=AF.Silu), s_sl)
                        sl_par[par] = s_sl.n
                        rec.wait("vector", s_sl, sl_par[par])
                        if j >= 2:
                            psem, ptgt = spl_hist[j - 2]
                            rec.wait("vector", psem, ptgt)
                        rec.do("vector", lambda E, par=par: E.tensor_tensor(
                            shtile[:, par * TB:(par + 1) * TB],
                            asb[:, par * TB:(par + 1) * TB],
                            pb[2 + par][:], ALU.mult), s_sh)
                        sh_par[par] = s_sh.n
                        rec.wait("gpsimd", s_sh, sh_par[par])
                        spsem = rs_spl.next()
                        sptgt = rec.do("gpsimd", lambda E, j=j, par=par:
                                       E.dma_start(
                            out=shs.ap()[j], in_=shtile[:, par * TB:
                                                        (par + 1) * TB]),
                            spsem, 16)
                        spl_hist[j] = (spsem, sptgt)
                    if t_shs is not None and tb == 0:
                        rec.wait("gpsimd", s_spl)
                        rec.do("gpsimd", lambda E: E.dma_start(
                            out=t_shs.ap(), in_=shs.ap()), s_dbg, 16)
                    if t_sgb is not None and tb == 0:
                        rec.wait("gpsimd", s_sg, sig_done)
                        rec.do("gpsimd", lambda E: E.dma_start(
                            out=t_sgb.ap(), in_=sigb[:]), s_dbg, 16)
                    shr_t = [0, 0, 0, 0]
                    dn_mm = [0]
                    for fq in range(4):
                        for i in range(44):
                            seq, s0, tw = r3_load(
                                lambda fq=fq, i=i: sdwB.ap()[fq, i], 4)
                            if fq == 0 and i == 0:
                                for ssem, stgt in spl_hist[-2:]:
                                    rec.wait("gpsimd", ssem, stgt)
                            if fq * 44 + i >= 4:
                                rec.wait("gpsimd", s_w3f, shr_t[i % 4])
                            shsem = rs_shr.next()
                            shtgt = rec.do("gpsimd", lambda E, i=i:
                                           E.dma_start(
                                out=shr[:, (i % 4) * TB:(i % 4 + 1) * TB],
                                in_=shs.ap()[i]), shsem, 16)
                            rec.wait("tensor", shsem, shtgt)
                            rec.wait("tensor", tw[0], tw[1])
                            if i == 0:
                                rec.wait("tensor", s_fa)
                                rec.wait("tensor", s_sl)
                                rec.wait("tensor", s_sh)
                            for f4 in range(4):
                                rec.do("tensor", lambda E, s0=s0, f4=f4, i=i:
                                       E.matmul(
                                    pb[f4][:],
                                    wr3[:, (s0 + f4) * 128:(s0 + f4 + 1) * 128],
                                    shr[:, (i % 4) * TB:(i % 4 + 1) * TB],
                                    start=(i == 0), stop=(i == 43)),
                                    s_w3f if f4 == 3 else None)
                            free_tgt[seq] = s_w3f.n
                            shr_t[i % 4] = s_w3f.n
                        if t_shrd is not None and tb == 0 and fq == 0:
                            rec.wait("gpsimd", s_shr)
                            rec.do("gpsimd", lambda E: E.dma_start(
                                out=t_shrd.ap(), in_=shr[:]), s_dbg, 16)
                            rec.wait("sync", s_dbg)
                            rec.wait("tensor", s_dbg)
                        rec.wait("vector", s_w3f)
                        rec.wait("vector", s_sg, sig_done)
                        if tb > 0 and fq == 0:
                            rec.wait("vector", sL4["sq"])
                        for f4 in range(4):
                            rec.do("vector", lambda E, fq=fq, f4=f4, tb=tb:
                                   E.tensor_tensor(
                                facc[:, (fq * 4 + f4) * TB:
                                     (fq * 4 + f4 + 1) * TB],
                                pb[f4][:], sigb[:], ALU.mult), s_fa)
                        if t_fq0 is not None and tb == 0 and fq == 0:
                            rec.wait("gpsimd", s_fa)
                            rec.do("gpsimd", lambda E: E.dma_start(
                                out=t_fq0.ap(), in_=facc[:, 0:4 * TB]),
                                s_dbg, 16)
                            rec.wait("tensor", s_dbg)
                    if t_fsh is not None:
                        rec.wait("gpsimd", s_fa)
                        rec.do("gpsimd", lambda E, tb=tb: E.dma_start(
                            out=t_fsh.ap()[tb], in_=facc[:]), s_dbg, 16)
                        rec.wait("vector", s_dbg)
                    for en in range(NE):
                        wb_done = wb_done_all
                        for j in range(11):
                            _, s0g, tg = r3_load(
                                lambda en=en, j=j: w13B.ap()[en, j], 16)
                            seqg = ring_seq[0]
                            _, s0u, tu = r3_load(
                                lambda en=en, j=j: w13B.ap()[en, 11 + j], 16)
                            sequ = ring_seq[0]
                            par = j % 2
                            rec.wait("tensor", tg[0], tg[1])
                            rec.wait("tensor", tu[0], tu[1])
                            if j >= 2:
                                rec.wait("tensor", s_sl, sl_par[par])
                                rec.wait("tensor", s_sh, sh_par[par])
                            if j == 0:
                                rec.wait("tensor", s_fa)
                            for a in range(16):
                                rec.do("tensor", lambda E, s0=s0g, a=a,
                                       par=par: E.matmul(
                                    pb[0 + par][:],
                                    wr3[:, (s0 + a) * 128:(s0 + a + 1) * 128],
                                    xn3[:, a * TB:(a + 1) * TB],
                                    start=(a == 0), stop=(a == 15)),
                                    s_w3f if a == 15 else None)
                            free_tgt[seqg] = s_w3f.n
                            for a in range(16):
                                rec.do("tensor", lambda E, s0=s0u, a=a,
                                       par=par: E.matmul(
                                    pb[2 + par][:],
                                    wr3[:, (s0 + a) * 128:(s0 + a + 1) * 128],
                                    xn3[:, a * TB:(a + 1) * TB],
                                    start=(a == 0), stop=(a == 15)),
                                    s_w3f if a == 15 else None)
                            free_tgt[sequ] = s_w3f.n
                            rec.wait("scalar", s_w3f)
                            rec.wait("scalar", s_wb, wb_done)
                            if True:
                                rec.wait("scalar", s_sh, sh_par[par])
                            rec.do("scalar", lambda E, par=par: E.activation(
                                out=asb[:, par * TB:(par + 1) * TB],
                                in_=pb[0 + par][:], func=AF.Silu), s_sl)
                            sl_par[par] = s_sl.n
                            rec.wait("vector", s_sl, sl_par[par])
                            rec.dos("vector", lambda E, par=par, en=en:
                                   E.tensor_tensor(
                                bsb[:, par * TB:(par + 1) * TB],
                                pb[2 + par][:],
                                wb_all[:, en * TB:(en + 1) * TB],
                                ALU.mult), s_sh)
                            rec.do("vector", lambda E, par=par, j=j:
                                   E.tensor_tensor(
                                sh_e[:, j * TB:(j + 1) * TB],
                                asb[:, par * TB:(par + 1) * TB],
                                bsb[:, par * TB:(par + 1) * TB], ALU.mult),
                                s_sh)
                            sh_par[par] = s_sh.n
                        for fq in range(4):
                            for i in range(11):
                                seq, s0, tw = r3_load(
                                    lambda en=en, fq=fq, i=i:
                                    w2B.ap()[en, fq, i], 4)
                                rec.wait("tensor", tw[0], tw[1])
                                if i == 0:
                                    rec.wait("tensor", s_fa)
                                    rec.wait("tensor", s_sl)
                                    rec.wait("tensor", s_sh)
                                for f4 in range(4):
                                    rec.do("tensor", lambda E, s0=s0, f4=f4,
                                           i=i: E.matmul(
                                        pb[f4][:],
                                        wr3[:, (s0 + f4) * 128:
                                            (s0 + f4 + 1) * 128],
                                        sh_e[:, i * TB:(i + 1) * TB],
                                        start=(i == 0), stop=(i == 10)),
                                        s_w3f if f4 == 3 else None)
                                free_tgt[seq] = s_w3f.n
                            rec.wait("vector", s_w3f)
                            for f4 in range(4):
                                rec.do("vector", lambda E, fq=fq, f4=f4:
                                       E.tensor_tensor(
                                    facc[:, (fq * 4 + f4) * TB:
                                         (fq * 4 + f4 + 1) * TB],
                                    facc[:, (fq * 4 + f4) * TB:
                                         (fq * 4 + f4 + 1) * TB],
                                    pb[f4][:], ALU.add), s_fa)
                    if t_fac is not None:
                        rec.wait("gpsimd", s_fa)
                        rec.do("gpsimd", lambda E, tb=tb: E.dma_start(
                            out=t_fac.ap()[tb], in_=facc[:]), s_dbg, 16)
                        rec.wait("scalar", s_dbg)
                    rec.wait("scalar", s_fa)
                    rec.wait("tensor", s_fa)

                    def apply4(rec, a, pRb, pMrb, tb=tb):
                        rec.dos("vector", lambda E, a=a: E.tensor_tensor(
                            ftmp[:, (a % 2) * TB:(a % 2 + 1) * TB],
                            facc[:, a * TB:(a + 1) * TB], pRb, ALU.mult),
                            sL4["ch"])
                        rec.dos("vector", lambda E, a=a: E.tensor_tensor(
                            ftmp2[:, (a % 2) * TB:(a % 2 + 1) * TB],
                            ftmp[:, (a % 2) * TB:(a % 2 + 1) * TB], pMrb,
                            ALU.subtract), sL4["ch"])
                        if len(out_hist) >= 2:
                            rec.wait("vector", s_out, out_hist[-2])
                        rec.do("vector", lambda E, a=a, tb=tb: E.tensor_tensor(
                            outsb[:, (a % 2) * TB:(a % 2 + 1) * TB],
                            x2_sb[:, a * TOK + tb * TB:a * TOK + (tb + 1) * TB],
                            ftmp2[:, (a % 2) * TB:(a % 2 + 1) * TB], ALU.add),
                            s_xn3)
                        rec.wait("gpsimd", s_xn3)
                        rec.do("gpsimd", lambda E, a=a, tb=tb: E.dma_start(
                            out=yT.ap()[a * 128:(a + 1) * 128,
                                        tb * TB:(tb + 1) * TB],
                            in_=outsb[:, (a % 2) * TB:(a % 2 + 1) * TB]),
                            s_out, 16)
                        out_hist.append(s_out.n)

                    emit_ln(rec, sL4,
                            dict(ones_x=ones_r[:, 0:1], ones_sq=ones_bf[:, 0:1],
                                 ones_row=ones_r[0:1, :],
                                 pSX=pb[0][0:1, :], pSQ=pb[1][0:1, :],
                                 pRb=pb[2][:], pMrb=pb[3][:], rows=ROWS),
                            lambda a: facc[:, a * TB:(a + 1) * TB],
                            lambda a: fsq[:, (a % 4) * TB:(a % 4 + 1) * TB],
                            apply4)
                rec.wait("gpsimd", s_out)
                rec.play()

    return nc, dbg_t



def _rope_tables(t_len, dd):
    pos = np.repeat(np.arange(dd // 2, dtype=np.float64), 2)
    theta = np.exp(-pos / dd * np.log(10000.0))
    ang = np.arange(1, t_len + 1, dtype=np.float64)[:, None] * theta[None, :]
    return np.cos(ang), np.sin(ang)


def prep_in_maps(inputs):
    f32 = np.float32
    x = np.asarray(inputs["L0_attn_x"], f32)[0]
    WA = np.asarray(inputs["L0_attn_WA"], f32)
    bA = np.asarray(inputs["L0_attn_bA"], f32)
    WB = np.asarray(inputs["L0_attn_WB"], f32)
    bB = np.asarray(inputs["L0_attn_bB"], f32)
    WC = np.asarray(inputs["L0_attn_WC"], f32)
    bC = np.asarray(inputs["L0_attn_bC"], f32)
    WUQ = np.asarray(inputs["L0_attn_WUQ"], f32)
    WUK = np.asarray(inputs["L0_attn_WUK"], f32)
    WUV = np.asarray(inputs["L0_attn_WUV"], f32)
    WZ = np.asarray(inputs["L0_attn_WZ"], f32)
    bZ = np.asarray(inputs["L0_attn_bZ"], f32)
    gw = np.asarray(inputs["L0_ffn_gate_w"], f32)
    sgw = np.asarray(inputs["L0_ffn_shared_gate_w"], f32)
    sguw = np.asarray(inputs["L0_ffn_shared_gate_up_w"], f32)
    sdw = np.asarray(inputs["L0_ffn_shared_down_w"], f32)
    w13 = np.asarray(inputs["L0_ffn_w13"], f32)
    w2 = np.asarray(inputs["L0_ffn_w2"], f32)

    xT = np.ascontiguousarray(x.T)
    waT = np.ascontiguousarray(
        WA.reshape(16, 128, 8, 128).transpose(2, 0, 1, 3))
    bA_r = np.ascontiguousarray(bA.reshape(8, 128).T)
    wbT = np.ascontiguousarray(WB.reshape(8, 128, DHR))
    bB_c = bB.reshape(DHR, 1)
    wzT = np.ascontiguousarray(
        WZ.reshape(16, 128, 16, 128).transpose(2, 0, 1, 3))
    bZ_r = np.ascontiguousarray(bZ.reshape(16, 128).T)
    gwT = np.ascontiguousarray(
        gw.T.reshape(16, 128, NE).transpose(1, 0, 2).reshape(128, 16 * NE))
    sgwT = np.ascontiguousarray(sgw.T.reshape(16, 128).T)
    sguT = np.ascontiguousarray(
        sguw.T.reshape(16, 128, 88, 128).transpose(2, 0, 1, 3))
    sdwT = np.ascontiguousarray(
        sdw.T.reshape(44, 128, 4, 4, 128).transpose(2, 0, 3, 1, 4))
    w13T = np.ascontiguousarray(
        w13.transpose(0, 2, 1).reshape(NE, 16, 128, 22, 128)
        .transpose(0, 3, 1, 2, 4))
    w2T = np.ascontiguousarray(
        w2.transpose(0, 2, 1).reshape(NE, 11, 128, 4, 4, 128)
        .transpose(0, 3, 1, 4, 2, 5))

    cq_full, sq_full = _rope_tables(T, NH * DHR)
    ck_full, sk_full = _rope_tables(T, DHR)
    ones_bf = np.ones((128, 128), BF)
    ones_r = np.ones((128, 128), f32)
    eye = np.eye(128, dtype=f32).astype(BF)
    RTm = np.zeros((128, 128), f32)
    for i in range(64):
        RTm[2 * i + 1, 2 * i] = -1.0
        RTm[2 * i, 2 * i + 1] = 1.0
    RTm = RTm.astype(BF)
    maskd = np.zeros((128, 4 * TB), f32)
    for rel in range(4):
        kk = np.arange(128)[:, None] + 128 * rel
        qq = np.arange(TB)[None, :]
        maskd[:, rel * TB:(rel + 1) * TB] = (kk <= qq).astype(f32)
    maskd = maskd.astype(BF)

    in_maps = []
    for c in range(NCORE):
        tok = slice(c * TOK, (c + 1) * TOK)
        hd = slice(c * 2 * DH, (c + 1) * 2 * DH)
        rd = slice(c * 128, (c + 1) * 128)
        cosq = np.ascontiguousarray(
            (cq_full[:, rd].T * SCALE)).astype(BF)
        sinq = np.ascontiguousarray(
            (sq_full[:, rd].T * SCALE)).astype(BF)
        m = {
            "xT": np.ascontiguousarray(xT[:, tok]),
            "cid": np.array([[c]], np.int32),
            "waT": waT, "bA_r": bA_r, "wbT": wbT, "bB_c": bB_c,
            "cosk": np.ascontiguousarray(ck_full[tok, :].T).astype(BF),
            "sink": np.ascontiguousarray(sk_full[tok, :].T).astype(BF),
            "wuq": np.ascontiguousarray(WUQ[:, hd].reshape(4, 128, 256)),
            "wuk": np.ascontiguousarray(WUK[:, hd].reshape(4, 128, 256)),
            "wuv": np.ascontiguousarray(WUV[:, hd].reshape(4, 128, 256)),
            "wcc": np.ascontiguousarray(WC[:, rd].reshape(8, 128, 128)),
            "bC_c": np.ascontiguousarray(bC[rd].reshape(128, 1)),
            "cosq": np.ascontiguousarray(
                cosq.reshape(128, 16, TB).transpose(1, 0, 2)),
            "sinq": np.ascontiguousarray(
                sinq.reshape(128, 16, TB).transpose(1, 0, 2)),
            "RT": RTm, "eye": eye, "maskd": maskd,
            "ones_bf": ones_bf, "ones_r": ones_r,
            "wzT": wzT, "bZ_r": bZ_r, "gwT": gwT, "sgwT": sgwT,
            "sguT": sguT, "sdwT": sdwT, "w13T": w13T, "w2T": w2T,
        }
        in_maps.append(m)
    return in_maps


_CACHE = {}


def kernel(**inputs):
    if "nc" not in _CACHE:
        _CACHE["nc"], _ = build()
    nc = _CACHE["nc"]
    in_maps = prep_in_maps(inputs)
    res = run_bass_kernel_spmd(nc, in_maps, core_ids=list(range(NCORE)))
    y = np.concatenate([res.results[c]["yT"] for c in range(NCORE)], axis=1)
    return np.ascontiguousarray(y.T).reshape(1, T, C).astype(np.float32)
